# version 03
# baseline (speedup 1.0000x reference)
import numpy as np
import ml_dtypes
from contextlib import ExitStack

import concourse.bass as bass
import concourse.mybir as mybir
from concourse.bass_utils import run_bass_kernel_spmd

F32 = mybir.dt.float32
BF16 = mybir.dt.bfloat16
I32 = mybir.dt.int32
AF = mybir.ActivationFunctionType
ALU = mybir.AluOpType
AX = mybir.AxisListType

NCORES = 8
T = 2048
D = 1024
NT = T // 128
ND = D // 128
NH = 16
HD = 64
NEG = -1.0e30
LN_EPS = 1.0e-5

SBUF_BASE = 16512
SBUF_TOP = 229344


class Buf:
    __slots__ = ("w", "rs", "nm")

    def __init__(self, nm=""):
        self.w = None
        self.rs = []
        self.nm = nm


class SB:
    __slots__ = ("h", "b", "off", "sz")

    def __init__(self, h, b, off=None, sz=None):
        self.h = h
        self.b = b
        self.off = off
        self.sz = sz

    def __getitem__(self, idx):
        return self.h[idx]


def RD(*bufs):
    out = []
    for b in bufs:
        if b is not None and b.w is not None:
            out.append(b.w)
    return out


def WR(*bufs):
    out = []
    for b in bufs:
        if b is None:
            continue
        if b.w is not None:
            out.append(b.w)
        out.extend(t for t in b.rs if t is not None)
    return out


class _Alloc:
    def __init__(self, base, top):
        self.free = [(base, top - base)]
        self.pending = []

    def alloc(self, size, align=128):
        size = -(-size // align) * align
        for i, (o, s) in enumerate(self.free):
            oo = -(-o // align) * align
            pad = oo - o
            if s >= size + pad:
                del self.free[i]
                if pad:
                    self.free.insert(i, (o, pad))
                rest = s - pad - size
                if rest:
                    self.free.append((oo + size, rest))
                self.free.sort()
                toks = []
                keep = []
                for (po, ps, pt) in self.pending:
                    if po < oo + size and oo < po + ps:
                        toks.extend(pt)
                        if po < oo:
                            keep.append((po, oo - po, pt))
                        if po + ps > oo + size:
                            keep.append((oo + size, po + ps - (oo + size), pt))
                    else:
                        keep.append((po, ps, pt))
                self.pending = keep
                return oo, size, toks
        raise MemoryError(f"SBUF alloc of {size} failed; free={self.free}")

    def free_(self, off, size, tokens):
        self.free.append((off, size))
        self.free.sort()
        merged = []
        for o, s in self.free:
            if merged and merged[-1][0] + merged[-1][1] == o:
                merged[-1] = (merged[-1][0], merged[-1][1] + s)
            else:
                merged.append((o, s))
        self.free = [tuple(x) for x in merged]
        if tokens:
            self.pending.append((off, size, list(tokens)))


_DTS = {F32: 4, BF16: 2, I32: 4}


class Prog:
    ENGS = ("t", "v", "s", "p", "q")

    def __init__(self, name="prog"):
        self.nc = bass.Bass("TRN2", target_bir_lowering=False)
        self.q = {k: [] for k in self.ENGS}
        self.cnt = {}
        self.sems = {}
        self.sat = {k: {} for k in self.ENGS}
        self._stack = ExitStack()
        for k in self.ENGS:
            self._mksem("E" + k)
        for nm in ("Dw", "Dx", "Ds"):
            self._mksem(nm)
        self._mksem("bar")
        self.al = _Alloc(SBUF_BASE, SBUF_TOP)
        self._uid = 0

    def _mksem(self, name):
        self.sems[name] = self._stack.enter_context(self.nc.semaphore(name))
        self.cnt[name] = 0

    def uid(self):
        self._uid += 1
        return self._uid

    def op(self, eng, fname, *args, deps=(), sem=None, amt=1, inc=True, **kw):
        sat = self.sat[eng]
        need = {}
        for d in deps:
            if d is None:
                continue
            s, v = d
            if need.get(s, 0) < v:
                need[s] = v
        waits = []
        for s, v in need.items():
            if sat.get(s, 0) < v:
                waits.append((s, v))
                sat[s] = v
        if eng == "s" and fname == "tensor_copy":
            fname = "copy"
        if sem is None and inc:
            sem = "E" + eng
            amt = 1
        tok = None
        semh = None
        if sem is not None and inc:
            self.cnt[sem] += amt
            tok = (sem, self.cnt[sem])
            semh = self.sems[sem]
        sems = self.sems

        def emit(E, waits=tuple(waits), fname=fname, args=args, kw=kw, semh=semh, amt=amt):
            for s, v in waits:
                E.wait_ge(sems[s], v)
            ins = getattr(E, fname)(*args, **kw)
            if semh is not None:
                ins.then_inc(semh, amt)

        self.q[eng].append(emit)
        return tok

    def do(self, eng, fname, *args, r=(), w=(), extra=(), sem=None, amt=1, **kw):
        deps = RD(*r) + WR(*w) + list(extra)
        tok = self.op(eng, fname, *args, deps=deps, sem=sem, amt=amt, **kw)
        for b in r:
            if b is not None:
                b.rs.append(tok)
        for b in w:
            if b is not None:
                b.w = tok
                b.rs = []
        return tok

    def wait_tok(self, eng, tok):
        if tok is None:
            return
        s, v = tok
        if self.sat[eng].get(s, 0) < v:
            self.sat[eng][s] = v
            sems = self.sems

            def emit(E, s=s, v=v):
                E.wait_ge(sems[s], v)

            self.q[eng].append(emit)

    def mm_set(self, pbuf, groups, extra=()):
        first_deps = WR(pbuf) + list(extra)
        rbufs_all = []
        total = sum(len(g) for g in groups)
        n = 0
        tok = None
        for g in groups:
            L = len(g)
            for i, (pap, lh, rh, rb) in enumerate(g):
                n += 1
                deps = RD(*rb)
                if n == 1:
                    deps = deps + first_deps
                tok = self.op("t", "matmul", pap, lh, rh,
                              start=(i == 0), stop=(i == L - 1),
                              deps=deps, inc=(n == total))
                for b in rb:
                    if b is not None and b not in rbufs_all:
                        rbufs_all.append(b)
        pbuf.w = tok
        pbuf.rs = []
        for b in rbufs_all:
            b.rs.append(tok)
        return tok

    def sbuf(self, name, shape, dtype):
        bpp = int(np.prod(shape[1:])) * _DTS[dtype]
        off, sz, toks = self.al.alloc(bpp)
        h = self.nc.alloc_sbuf_tensor_at(f"{name}_{self.uid()}", list(shape), dtype, offset=off)
        b = Buf(name)
        b.rs = list(toks)
        return SB(h, b, off, sz)

    def sfree(self, sb):
        toks = ([sb.b.w] if sb.b.w else []) + sb.b.rs
        self.al.free_(sb.off, sb.sz, [t for t in toks if t])

    def psum(self, stack, name, shape, dtype=F32):
        h = stack.enter_context(self.nc.psum_tensor(f"{name}_{self.uid()}", list(shape), dtype))
        return SB(h, Buf(name))

    def barrier(self):
        self.cnt["bar"] += 5
        target = self.cnt["bar"]
        sems = self.sems
        for k in self.ENGS:
            def em(E, t=target):
                E.sem_inc(sems["bar"], 1)
                E.wait_ge(sems["bar"], t)
            self.q[k].append(em)
        for k in self.ENGS:
            for s in ("Et", "Ev", "Es", "Ep"):
                if self.sat[k].get(s, 0) < self.cnt[s]:
                    self.sat[k][s] = self.cnt[s]

    def build(self):
        with self.nc.Block() as blk:
            blk.tensor(lambda E: [f(E) for f in self.q["t"]])
            blk.vector(lambda E: [f(E) for f in self.q["v"]])
            blk.scalar(lambda E: [f(E) for f in self.q["s"]])
            blk.gpsimd(lambda E: [f(E) for f in self.q["p"]])
            blk.sync(lambda E: [f(E) for f in self.q["q"]])
        self._stack.close()
        return self.nc


def load_consts(P, A):
    c = {}
    c["ident_bf"] = P.sbuf("ident_bf", [128, 128], BF16)
    c["ident_f"] = P.sbuf("ident_f", [128, 128], F32)
    c["ones_bf"] = P.sbuf("ones_bf", [1, 512], BF16)
    c["ones_f"] = P.sbuf("ones_f", [1, 128], F32)
    c["ones_col"] = P.sbuf("ones_col", [128, 1], F32)
    ins = A["in"]
    P.do("p", "dma_start", c["ident_bf"][:], ins["c_ident"][:], w=[c["ident_bf"].b], sem="Dw", amt=16)
    P.do("q", "dma_start", c["ident_f"][:], ins["c_ident"][:], w=[c["ident_f"].b], sem="Dx", amt=16)
    P.do("p", "dma_start", c["ones_bf"][:], ins["c_ones"][0:1, 0:512], w=[c["ones_bf"].b], sem="Dw", amt=16)
    P.do("q", "dma_start", c["ones_f"][:], ins["c_ones"][0:1, 0:128], w=[c["ones_f"].b], sem="Dx", amt=16)
    P.do("q", "dma_start", c["ones_col"][:], ins["c_ones"][0:1, 0:128].rearrange("o p -> p o"),
         w=[c["ones_col"].b], sem="Dx", amt=16)
    c["am_add"] = P.sbuf("am_add", [128, 128], F32)
    c["am_mul"] = P.sbuf("am_mul", [128, 128], F32)
    P.do("p", "memset", c["am_add"][:], 0.0, w=[c["am_add"].b])
    P.do("p", "affine_select", c["am_add"][:], c["am_add"][:], [[-1, 128]], ALU.is_ge, NEG,
         base=0, channel_multiplier=1, r=[c["am_add"].b], w=[c["am_add"].b])
    P.do("p", "memset", c["am_mul"][:], 1.0, w=[c["am_mul"].b])
    P.do("p", "affine_select", c["am_mul"][:], c["am_mul"][:], [[-1, 128]], ALU.is_ge, 0.0,
         base=0, channel_multiplier=1, r=[c["am_mul"].b], w=[c["am_mul"].b])
    if "c_eye8" in ins:
        c["eye8"] = P.sbuf("eye8", [8, 1024], F32)
        P.do("q", "dma_start", c["eye8"][:], ins["c_eye8"][:], w=[c["eye8"].b], sem="Dx", amt=16)
    if "c_eye16" in ins:
        c["eye16"] = P.sbuf("eye16", [16, 2048], F32)
        P.do("q", "dma_start", c["eye16"][:], ins["c_eye16"][:], w=[c["eye16"].b], sem="Dx", amt=16)
    A["c"] = c


def ln_feature_major(P, A, ps, y, with_store=None):
    c = A["c"]
    PA, PB = ps["PA"], ps["PB"]
    sq = P.sbuf("ln_sq", [128, ND, 1024], F32)
    s1 = P.sbuf("ln_s1", [1, 2048], F32)
    s2 = P.sbuf("ln_s2", [1, 2048], F32)
    arow = P.sbuf("ln_ar", [1, 2048], F32)
    brow = P.sbuf("ln_br", [1, 2048], F32)
    arep = P.sbuf("ln_arep", [128, 2048], F32)
    brep = P.sbuf("ln_brep", [128, 2048], F32)

    for half in range(2):
        groups = []
        for st in range(2):
            lo = half * 1024 + st * 512
            g = [(PA[0:1, st * 512:(st + 1) * 512], c["ones_col"][:], y[:, cch, lo:lo + 512],
                  (c["ones_col"].b, y.b)) for cch in range(ND)]
            groups.append(g)
        P.mm_set(PA.b, groups)
        P.do("v", "tensor_copy", s1[0:1, half * 1024:(half + 1) * 1024], PA[0:1, :],
             r=[PA.b], w=[s1.b])
        for cch in range(ND):
            P.do("s", "activation", sq[:, cch, :], y[:, cch, half * 1024:(half + 1) * 1024],
                 AF.Square, r=[y.b], w=[sq.b])
        groups = []
        for st in range(2):
            g = [(PB[0:1, st * 512:(st + 1) * 512], c["ones_col"][:],
                  sq[:, cch, st * 512:(st + 1) * 512],
                  (c["ones_col"].b, sq.b)) for cch in range(ND)]
            groups.append(g)
        P.mm_set(PB.b, groups)
        P.do("v", "tensor_copy", s2[0:1, half * 1024:(half + 1) * 1024], PB[0:1, :],
             r=[PB.b], w=[s2.b])
    mu = P.sbuf("ln_mu", [1, 2048], F32)
    P.do("v", "tensor_scalar_mul", mu[:], s1[:], 1.0 / D, r=[s1.b], w=[mu.b])
    P.do("v", "tensor_scalar_mul", s2[:], s2[:], 1.0 / D, r=[s2.b], w=[s2.b])
    P.do("v", "tensor_mul", arow[:], mu[:], mu[:], r=[mu.b], w=[arow.b])
    P.do("v", "tensor_sub", s2[:], s2[:], arow[:], r=[s2.b, arow.b], w=[s2.b])
    P.do("v", "tensor_scalar_add", s2[:], s2[:], LN_EPS, r=[s2.b], w=[s2.b])
    P.do("s", "activation", s2[:], s2[:], AF.Ln, r=[s2.b], w=[s2.b])
    P.do("s", "activation", arow[:], s2[:], AF.Exp, scale=-0.5, r=[s2.b], w=[arow.b])
    P.do("v", "tensor_mul", brow[:], mu[:], arow[:], r=[mu.b, arow.b], w=[brow.b])
    P.do("v", "tensor_scalar_mul", brow[:], brow[:], -1.0, r=[brow.b], w=[brow.b])
    for (row, rep) in ((arow, arep), (brow, brep)):
        for half in range(2):
            pb = PA if half == 0 else PB
            groups = []
            for st in range(2):
                lo = half * 1024 + st * 512
                groups.append([(pb[:, st * 512:(st + 1) * 512], c["ones_f"][0:1, 0:128],
                                row[0:1, lo:lo + 512], (c["ones_f"].b, row.b))])
            P.mm_set(pb.b, groups)
            P.do("v" if half == 0 else "s", "tensor_copy", rep[:, half * 1024:(half + 1) * 1024],
                 pb[:], r=[pb.b], w=[rep.b])
    for cch in range(ND):
        P.do("v", "tensor_mul", y[:, cch, :], y[:, cch, :], arep[:], r=[arep.b, y.b], w=[y.b])
        P.do("p", "tensor_add", y[:, cch, :], y[:, cch, :], brep[:], r=[brep.b, y.b], w=[y.b])
        if with_store is not None:
            P.do("q", "dma_start", with_store.rearrange("(c p) t -> p c t", p=128)[:, cch, :],
                 y[:, cch, :], r=[y.b], sem="Ds", amt=16)
    for t_ in (sq, s1, s2, arow, brow, arep, brep, mu):
        P.sfree(t_)


def cast_stream_to_bf(P, A, x_fm, x_bf):
    for cch in range(ND):
        eng = ("v", "s", "p")[cch % 3]
        P.do(eng, "tensor_copy", x_bf[:, cch, :], x_fm[:, cch, :], r=[x_fm.b], w=[x_bf.b])


def emit_l0attn(P, A):
    ins = A["in"]
    c = A["c"]
    nc = P.nc

    x_bf = P.sbuf("x_bf", [128, ND, 2048], BF16)
    if A.get("stream") is None:
        P.do("p", "dma_start", x_bf[:],
             ins["xin"][:].rearrange("(c p) t -> p c t", p=128),
             w=[x_bf.b], sem="Dw", amt=16)
    else:
        cast_stream_to_bf(P, A, A["stream"], x_bf)
        park_stream(P, A)

    g_rep = P.sbuf("g_rep", [128, 1024], F32)
    b_rep = P.sbuf("b_rep", [128, 1024], F32)
    grow = P.sbuf("grow", [1, 1024], F32)
    brow = P.sbuf("brow", [1, 1024], F32)
    P.do("q", "dma_start", grow[:], ins["l0_gamma"][:], w=[grow.b], sem="Dx", amt=16)
    P.do("q", "dma_start", brow[:], ins["l0_beta"][:], w=[brow.b], sem="Dx", amt=16)

    with ExitStack() as pst:
        PRA = P.psum(pst, "pra", [128, 1024], F32)
        PRB = P.psum(pst, "prb", [128, 1024], F32)
        for (row, rep, pb) in ((grow, g_rep, PRA), (brow, b_rep, PRB)):
            groups = [[(pb[:, st * 512:(st + 1) * 512], c["ones_f"][0:1, 0:128],
                        row[0:1, st * 512:(st + 1) * 512], (c["ones_f"].b, row.b))]
                      for st in range(2)]
            P.mm_set(pb.b, groups)
            P.do("v", "tensor_copy", rep[:], pb[:], r=[pb.b], w=[rep.b])

        qT = P.sbuf("qT", [128, ND, 2048], BF16)
        kT = P.sbuf("kT", [128, ND, 2048], BF16)
        vtm = P.sbuf("vtm", [128, NT, 1024], BF16)
        wbuf = P.sbuf("wbuf", [128, ND, 1024], BF16)

        def load_w(name):
            return P.do("p", "dma_start", wbuf[:],
                        ins[name][:].rearrange("(c p) o -> p c o", p=128),
                        w=[wbuf.b], sem="Dw", amt=16)

        for wname, dst, scale in (("l0_wqT", qT, None), ("l0_wkT", kT, 0.125)):
            load_w(wname)
            for oc in range(ND):
                for half in range(2):
                    pb = PRA if (oc * 2 + half) % 2 == 0 else PRB
                    groups = []
                    for st in range(2):
                        lo = half * 1024 + st * 512
                        groups.append([(pb[:, st * 512:(st + 1) * 512],
                                        wbuf[:, ic, oc * 128:(oc + 1) * 128],
                                        x_bf[:, ic, lo:lo + 512],
                                        (wbuf.b, x_bf.b)) for ic in range(ND)])
                    P.mm_set(pb.b, groups)
                    eng = "s" if (oc + half) % 2 == 0 else "v"
                    if scale is None:
                        P.do(eng, "tensor_copy", dst[:, oc, half * 1024:(half + 1) * 1024],
                             pb[:], r=[pb.b], w=[dst.b])
                    else:
                        P.do("s", "activation", dst[:, oc, half * 1024:(half + 1) * 1024],
                             pb[:], AF.Identity, scale=scale, r=[pb.b], w=[dst.b])
        load_w("l0_wvpT")
        for tc in range(NT):
            pb = PRA if tc % 2 == 0 else PRB
            groups = []
            for st in range(2):
                groups.append([(pb[:, st * 512:(st + 1) * 512],
                                x_bf[:, ic, tc * 128:(tc + 1) * 128],
                                wbuf[:, ic, st * 512:(st + 1) * 512],
                                (wbuf.b, x_bf.b)) for ic in range(ND)])
            P.mm_set(pb.b, groups)
            eng = "s" if tc % 2 == 0 else "v"
            P.do(eng, "tensor_copy", vtm[:, tc, :], pb[:], r=[pb.b], w=[vtm.b])
    P.sfree(wbuf)
    P.sfree(grow)
    P.sfree(brow)
    P.sfree(x_bf)

    qkv = P.sbuf("qkv", [128, NT, 1024], BF16)
    iter_bufs = []
    for par in range(2):
        iter_bufs.append(dict(
            Ef=P.sbuf(f"Ef{par}", [128, 2048], F32),
            Eb=P.sbuf(f"Eb{par}", [128, 2048], BF16),
            ET=P.sbuf(f"ET{par}", [128, NT, 128], BF16),
            negm=P.sbuf(f"negm{par}", [128, 1], F32),
            den=P.sbuf(f"den{par}", [128, 1], F32),
            rd=P.sbuf(f"rd{par}", [128, 1], F32),
        ))

    with ExitStack() as pst:
        PS = [P.psum(pst, "ps0", [128, 512], F32), P.psum(pst, "ps1", [128, 512], F32)]
        PT = [P.psum(pst, "pt0", [128, 128], BF16), P.psum(pst, "pt1", [128, 128], BF16)]
        PAV = [P.psum(pst, "pav0", [128, 64], F32), P.psum(pst, "pav1", [128, 64], F32)]

        it = 0
        for h in range(NH):
            hc = h // 2
            ho = (h % 2) * 64
            for qt in range(NT):
                B = iter_bufs[it % 2]
                Ef, Eb, ET = B["Ef"], B["Eb"], B["ET"]
                negm, den, rd = B["negm"], B["den"], B["rd"]
                Pkeys = (qt + 1) * 128
                nstrips = -(-Pkeys // 512)
                for st in range(nstrips):
                    w_ = min(512, Pkeys - st * 512)
                    pb = PS[st % 2]
                    P.mm_set(pb.b, [[(pb[:, 0:w_],
                                      qT[ho:ho + 64, hc, qt * 128:(qt + 1) * 128],
                                      kT[ho:ho + 64, hc, st * 512:st * 512 + w_],
                                      (qT.b, kT.b))]])
                    eng = "s" if st % 2 == 0 else "v"
                    P.do(eng, "tensor_copy", Ef[:, st * 512:st * 512 + w_], pb[:, 0:w_],
                         r=[pb.b], w=[Ef.b])
                P.do("p", "tensor_add", Ef[:, qt * 128:(qt + 1) * 128],
                     Ef[:, qt * 128:(qt + 1) * 128], c["am_add"][:],
                     r=[Ef.b, c["am_add"].b], w=[Ef.b])
                P.do("v", "reduce_max", negm[:], Ef[:, 0:Pkeys], axis=AX.X, negate=True,
                     r=[Ef.b], w=[negm.b])
                P.do("s", "activation", Eb[:, 0:Pkeys], Ef[:, 0:Pkeys], AF.Exp,
                     bias=negm[:], accum_out=den[:], r=[Ef.b, negm.b], w=[Eb.b, den.b])
                P.do("v", "reciprocal", rd[:], den[:], r=[den.b], w=[rd.b])
                for sc in range(qt + 1):
                    pt = PT[sc % 2]
                    P.do("t", "transpose", pt[:], Eb[:, sc * 128:(sc + 1) * 128],
                         c["ident_bf"][:], r=[Eb.b, c["ident_bf"].b], w=[pt.b])
                    eng = "v" if sc % 2 == 0 else "s"
                    P.do(eng, "tensor_copy", ET[:, sc, :], pt[:], r=[pt.b], w=[ET.b])
                pav = PAV[it % 2]
                P.mm_set(pav.b, [[(pav[:], ET[:, sc, :], vtm[:, sc, h * 64:h * 64 + 64],
                                   (ET.b, vtm.b)) for sc in range(qt + 1)]])
                P.do("s", "activation", qkv[:, qt, h * 64:h * 64 + 64], pav[:],
                     AF.Identity, scale=rd[:], r=[pav.b, rd.b], w=[qkv.b])
                it += 1

    P.sfree(qT)
    P.sfree(kT)
    P.sfree(vtm)
    for B in iter_bufs:
        for sb in B.values():
            P.sfree(sb)

    dump = P.sbuf("dump", [128, 1024], BF16)
    tmp = P.sbuf("mlntmp", [128, 1024], F32)
    rows = [dict(s1=P.sbuf(f"m_s1{i}", [128, 1], F32), s2=P.sbuf(f"m_s2{i}", [128, 1], F32),
                 rstd=P.sbuf(f"m_r{i}", [128, 1], F32), nmr=P.sbuf(f"m_n{i}", [128, 1], F32))
            for i in range(2)]
    for tc in range(NT):
        R = rows[tc % 2]
        s1, s2, rstd, nmr = R["s1"], R["s2"], R["rstd"], R["nmr"]
        P.do("v", "reduce_sum", s1[:], qkv[:, tc, :], axis=AX.X, r=[qkv.b], w=[s1.b])
        P.do("s", "activation", dump[:], qkv[:, tc, :], AF.Square, accum_out=s2[:],
             r=[qkv.b], w=[dump.b, s2.b])
        P.do("v", "tensor_scalar_mul", s1[:], s1[:], 1.0 / D, r=[s1.b], w=[s1.b])
        P.do("v", "tensor_scalar_mul", s2[:], s2[:], 1.0 / D, r=[s2.b], w=[s2.b])
        P.do("v", "tensor_mul", rstd[:], s1[:], s1[:], r=[s1.b], w=[rstd.b])
        P.do("v", "tensor_sub", s2[:], s2[:], rstd[:], r=[s2.b, rstd.b], w=[s2.b])
        P.do("v", "tensor_scalar_add", s2[:], s2[:], LN_EPS, r=[s2.b], w=[s2.b])
        P.do("s", "activation", s2[:], s2[:], AF.Ln, r=[s2.b], w=[s2.b])
        P.do("s", "activation", rstd[:], s2[:], AF.Exp, scale=-0.5, r=[s2.b], w=[rstd.b])
        P.do("v", "tensor_mul", nmr[:], s1[:], rstd[:], r=[s1.b, rstd.b], w=[nmr.b])
        P.do("v", "tensor_scalar_mul", nmr[:], nmr[:], -1.0, r=[nmr.b], w=[nmr.b])
        P.do("s", "activation", tmp[:], qkv[:, tc, :], AF.Identity, scale=rstd[:],
             bias=nmr[:], r=[qkv.b, rstd.b, nmr.b], w=[tmp.b])
        P.do("v", "tensor_mul", tmp[:], tmp[:], g_rep[:], r=[g_rep.b, tmp.b], w=[tmp.b])
        P.do("p", "tensor_add", qkv[:, tc, :], tmp[:], b_rep[:], r=[tmp.b, b_rep.b], w=[qkv.b])

    P.sfree(dump)
    P.sfree(tmp)
    for R in rows:
        for sb in R.values():
            P.sfree(sb)
    P.sfree(g_rep)
    P.sfree(b_rep)

    lnT = P.sbuf("lnT", [128, ND, 2048], BF16)
    with ExitStack() as pst:
        PT = [P.psum(pst, "pt0", [128, 128], BF16), P.psum(pst, "pt1", [128, 128], BF16)]
        for tc in range(NT):
            for oc in range(ND):
                pt = PT[(tc * ND + oc) % 2]
                P.do("t", "transpose", pt[:], qkv[:, tc, oc * 128:(oc + 1) * 128],
                     c["ident_bf"][:], r=[qkv.b, c["ident_bf"].b], w=[pt.b])
                eng = "v" if oc % 2 == 0 else "s"
                P.do(eng, "tensor_copy", lnT[:, oc, tc * 128:(tc + 1) * 128], pt[:],
                     r=[pt.b], w=[lnT.b])
    P.sfree(qkv)

    y = P.sbuf("y_fm", [128, ND, 2048], F32)
    xres = P.sbuf("xres", [128, 2, 2048], F32)
    with ExitStack() as pst:
        PA = P.psum(pst, "pa", [128, 1024], F32)
        PB = P.psum(pst, "pb", [128, 1024], F32)
        wbuf = A["_l0_wo_sb"]
        for oc in range(ND):
            if A.get("parked"):
                src = A["park"][:].rearrange("(c p) t -> p c t", p=128)[:, oc, :]
                P.do("q", "dma_start", xres[:, oc % 2, :], src, r=[A["park_buf"]],
                     w=[xres.b], sem="Dx", amt=16)
            else:
                src = A["in"]["xin"][:].rearrange("(c p) t -> p c t", p=128)[:, oc, :]
                P.do("q", "dma_start", xres[:, oc % 2, :], src, w=[xres.b], sem="Dx", amt=16)
            for half in range(2):
                pb = PA if (oc * 2 + half) % 2 == 0 else PB
                groups = []
                for st in range(2):
                    lo = half * 1024 + st * 512
                    groups.append([(pb[:, st * 512:(st + 1) * 512],
                                    wbuf[:, ic, oc * 128:(oc + 1) * 128],
                                    lnT[:, ic, lo:lo + 512],
                                    (wbuf.b, lnT.b)) for ic in range(ND)])
                P.mm_set(pb.b, groups)
                P.do("v", "tensor_add", y[:, oc, half * 1024:(half + 1) * 1024],
                     pb[:], xres[:, oc % 2, half * 1024:(half + 1) * 1024],
                     r=[pb.b, xres.b], w=[y.b])
        P.sfree(lnT)
        P.sfree(xres)
        ln_feature_major(P, A, {"PA": PA, "PB": PB}, y,
                         with_store=A["out_ap"] if A.get("store_out") else None)
    return y


def park_stream(P, A):
    x = A["stream"]
    pb = A.setdefault("park_buf", Buf("park"))
    P.do("q", "dma_start", A["park"][:].rearrange("(c p) t -> p c t", p=128), x[:],
         r=[x.b], w=[pb], sem="Ds", amt=16)
    P.sfree(x)
    A["stream"] = None
    A["parked"] = True


def emit_l0ffn(P, A):
    ins, c = A["in"], A["c"]

    if A.get("stream") is None:
        x_fm = P.sbuf("x_fm", [128, ND, 2048], F32)
        P.do("q", "dma_start", x_fm[:], ins["xin"][:].rearrange("(c p) t -> p c t", p=128),
             w=[x_fm.b], sem="Dx", amt=16)
    else:
        x_fm = A["stream"]
    x_bf = P.sbuf("x_bf", [128, ND, 2048], BF16)
    cast_stream_to_bf(P, A, x_fm, x_bf)

    wg = P.sbuf("wg", [128, ND, 8], BF16)
    bg = P.sbuf("bg", [1, 8], BF16)
    P.do("p", "dma_start", wg[:], ins["l0_wg"][:].rearrange("(c p) e -> p c e", p=128),
         w=[wg.b], sem="Dw", amt=16)
    P.do("p", "dma_start", bg[:], ins["l0_bg"][:], w=[bg.b], sem="Dw", amt=16)
    gnT = P.sbuf("gnT", [8, 2048], F32)
    gb = P.sbuf("gb", [128, NT, 8], F32)
    rws = [dict(m=P.sbuf(f"g_m{i}", [128, 1], F32), d=P.sbuf(f"g_d{i}", [128, 1], F32),
                r=P.sbuf(f"g_r{i}", [128, 1], F32), t8=P.sbuf(f"g_t8{i}", [128, 8], F32),
                u8=P.sbuf(f"g_u8{i}", [128, 8], F32)) for i in range(2)]
    with ExitStack() as pst:
        PD = P.psum(pst, "pd", [128, 512], F32)
        PTf = P.psum(pst, "ptf", [128, 128], F32)
        for tc in range(NT):
            R = rws[tc % 2]
            m, dsum, rd, t8, u8 = R["m"], R["d"], R["r"], R["t8"], R["u8"]
            g = [(PD[:, 0:8], x_bf[:, ic, tc * 128:(tc + 1) * 128], wg[:, ic, :],
                  (x_bf.b, wg.b)) for ic in range(ND)]
            g.append((PD[:, 0:8], c["ones_bf"][0:1, 0:128], bg[:], (c["ones_bf"].b, bg.b)))
            P.mm_set(PD.b, [g])
            gsl = gb[:, tc, :]
            P.do("s", "tensor_copy", gsl, PD[:, 0:8], r=[PD.b], w=[gb.b])
            P.do("v", "reduce_max", m[:], gsl, axis=AX.X, negate=True, r=[gb.b], w=[m.b])
            P.do("s", "activation", gsl, gsl, AF.Exp, bias=m[:], accum_out=dsum[:],
                 r=[gb.b, m.b], w=[gb.b, dsum.b])
            P.do("v", "reciprocal", rd[:], dsum[:], r=[dsum.b], w=[rd.b])
            P.do("v", "tensor_scalar_mul", gsl, gsl, rd[:], r=[gb.b, rd.b], w=[gb.b])
            P.do("v", "reduce_max", m[:], gsl, axis=AX.X, r=[gb.b], w=[m.b])
            P.do("v", "tensor_scalar", t8[:], gsl, m[:], None, ALU.is_ge,
                 r=[gb.b, m.b], w=[t8.b])
            P.do("v", "scalar_tensor_tensor", u8[:], t8[:], NEG, gsl, ALU.mult, ALU.add,
                 r=[t8.b, gb.b], w=[u8.b])
            P.do("v", "reduce_max", m[:], u8[:], axis=AX.X, r=[u8.b], w=[m.b])
            P.do("v", "tensor_scalar", t8[:], gsl, m[:], None, ALU.is_ge,
                 r=[gb.b, m.b], w=[t8.b])
            P.do("v", "tensor_mul", t8[:], t8[:], gsl, r=[gb.b, t8.b], w=[t8.b])
            P.do("v", "reduce_sum", dsum[:], t8[:], axis=AX.X, r=[t8.b], w=[dsum.b])
            P.do("v", "tensor_scalar_max", dsum[:], dsum[:], 1.0e-12, r=[dsum.b], w=[dsum.b])
            P.do("v", "reciprocal", rd[:], dsum[:], r=[dsum.b], w=[rd.b])
            P.do("v", "tensor_scalar_mul", t8[:], t8[:], rd[:], r=[t8.b, rd.b], w=[t8.b])
            P.do("t", "transpose", PTf[0:8, :], t8[:], c["ident_f"][:],
                 r=[t8.b, c["ident_f"].b], w=[PTf.b])
            P.do("v", "tensor_copy", gnT[0:8, tc * 128:(tc + 1) * 128], PTf[0:8, :],
                 r=[PTf.b], w=[gnT.b])
    for R in rws:
        for sb in R.values():
            P.sfree(sb)
    P.sfree(gb)
    P.sfree(wg)
    P.sfree(bg)

    h_sc = P.sbuf("h_sc", [128, 8, 2048], BF16)
    w1e = P.sbuf("w1e", [128, ND, 512], BF16)
    w2q = P.sbuf("w2q", [128, 8, 1024], BF16)
    b1row = P.sbuf("b1row", [1, 4096], BF16)
    gnrep = P.sbuf("gnrep", [128, 2048], F32)
    tmp = [P.sbuf(f"dtmp{i}", [128, 1024], F32) for i in range(2)]

    with ExitStack() as pst:
        PU = [P.psum(pst, "pu0", [128, 1024], F32), P.psum(pst, "pu1", [128, 1024], F32)]
        PW = [P.psum(pst, "pw0", [128, 1024], F32), P.psum(pst, "pw1", [128, 1024], F32)]
        rot_u = 0
        rot_w = 0
        for e in range(8):
            P.do("p", "dma_start", b1row[:], ins["l0_b1"][e:e + 1, :], w=[b1row.b],
                 sem="Dw", amt=16)
            for half in range(2):
                pu = PU[half]
                groups = [[(pu[:, st * 512:(st + 1) * 512],
                            c["eye8"][0:8, e * 128:(e + 1) * 128],
                            gnT[0:8, half * 1024 + st * 512:half * 1024 + (st + 1) * 512],
                            (c["eye8"].b, gnT.b))] for st in range(2)]
                P.mm_set(pu.b, groups)
                P.do("v", "tensor_copy", gnrep[:, half * 1024:(half + 1) * 1024], pu[:],
                     r=[pu.b], w=[gnrep.b])
            for hq in range(4):
                P.do("p", "dma_start", w2q[:],
                     ins["l0_w2"][e, hq * 1024:(hq + 1) * 1024, :].rearrange("(c p) d -> p c d", p=128),
                     w=[w2q.b], sem="Dw", amt=16)
                for j in range(2):
                    P.do("p", "dma_start", w1e[:],
                         ins["l0_w1"][e, :, hq * 1024 + j * 512:hq * 1024 + (j + 1) * 512]
                         .rearrange("(c p) h -> p c h", p=128),
                         w=[w1e.b], sem="Dw", amt=16)
                    for hc in range(4):
                        hglob = hq * 1024 + j * 512 + hc * 128
                        for th in range(2):
                            pu = PU[rot_u % 2]
                            rot_u += 1
                            groups = []
                            for st in range(2):
                                lo = th * 1024 + st * 512
                                gr = [(pu[:, st * 512:(st + 1) * 512],
                                       w1e[:, ic, hc * 128:(hc + 1) * 128],
                                       x_bf[:, ic, lo:lo + 512],
                                       (w1e.b, x_bf.b)) for ic in range(ND)]
                                gr.append((pu[:, st * 512:(st + 1) * 512],
                                           b1row[0:1, hglob:hglob + 128],
                                           c["ones_bf"][0:1, 0:512],
                                           (b1row.b, c["ones_bf"].b)))
                                groups.append(gr)
                            P.mm_set(pu.b, groups)
                            P.do("s", "activation", h_sc[:, j * 4 + hc, th * 1024:(th + 1) * 1024],
                                 pu[:], AF.Gelu, r=[pu.b], w=[h_sc.b])
                for dc in range(ND):
                    for th in range(2):
                        pw = PW[rot_w % 2]
                        tm = tmp[rot_w % 2]
                        rot_w += 1
                        groups = []
                        for st in range(2):
                            lo = th * 1024 + st * 512
                            gr = [(pw[:, st * 512:(st + 1) * 512],
                                   w2q[:, hcq, dc * 128:(dc + 1) * 128],
                                   h_sc[:, hcq, lo:lo + 512],
                                   (w2q.b, h_sc.b)) for hcq in range(8)]
                            groups.append(gr)
                        P.mm_set(pw.b, groups)
                        P.do("v", "tensor_mul", tm[:], pw[:],
                             gnrep[:, th * 1024:(th + 1) * 1024],
                             r=[pw.b, gnrep.b], w=[tm.b])
                        P.do("p", "tensor_add", x_fm[:, dc, th * 1024:(th + 1) * 1024],
                             x_fm[:, dc, th * 1024:(th + 1) * 1024], tm[:],
                             r=[tm.b, x_fm.b], w=[x_fm.b])
        b2f = P.sbuf("b2f", [8, 1024], F32)
        P.do("q", "dma_start", b2f[:], ins["l0_b2"][:], w=[b2f.b], sem="Dx", amt=16)
        for dc in range(ND):
            for th in range(2):
                pw = PW[(dc * 2 + th) % 2]
                groups = []
                for st in range(2):
                    lo = th * 1024 + st * 512
                    groups.append([(pw[:, st * 512:(st + 1) * 512],
                                    b2f[0:8, dc * 128:(dc + 1) * 128],
                                    gnT[0:8, lo:lo + 512], (b2f.b, gnT.b))])
                P.mm_set(pw.b, groups)
                P.do("v", "tensor_add", x_fm[:, dc, th * 1024:(th + 1) * 1024],
                     x_fm[:, dc, th * 1024:(th + 1) * 1024], pw[:],
                     r=[pw.b, x_fm.b], w=[x_fm.b])
        P.sfree(b2f)
        P.sfree(h_sc)
        P.sfree(w1e)
        P.sfree(w2q)
        P.sfree(b1row)
        P.sfree(gnrep)
        P.sfree(gnT)
        P.sfree(x_bf)
        for t_ in tmp:
            P.sfree(t_)
        ln_feature_major(P, A, {"PA": PU[0], "PB": PU[1]}, x_fm,
                         with_store=A["out_ap"] if A.get("store_out") else None)
    return x_fm


def emit_l1attn(P, A):
    ins, c = A["in"], A["c"]

    x_bf = P.sbuf("x_bf", [128, ND, 2048], BF16)
    if A.get("stream") is None:
        P.do("p", "dma_start", x_bf[:], ins["xin"][:].rearrange("(c p) t -> p c t", p=128),
             w=[x_bf.b], sem="Dw", amt=16)
    else:
        cast_stream_to_bf(P, A, A["stream"], x_bf)
        park_stream(P, A)

    QT = P.sbuf("QT", [128, ND, 2048], BF16)
    KT = P.sbuf("KT", [128, ND, 2048], BF16)
    wbuf = P.sbuf("wbuf", [128, ND, 1024], BF16)
    brow = P.sbuf("brow", [1, 1024], BF16)
    rho_row = P.sbuf("rho_row", [1, 2048], F32)

    def proj_fm(dst, wname, bname, scale_evac=None):
        P.do("p", "dma_start", wbuf[:], ins[wname][:].rearrange("(c p) o -> p c o", p=128),
             w=[wbuf.b], sem="Dw", amt=16)
        P.do("p", "dma_start", brow[:], ins[bname][:], w=[brow.b], sem="Dw", amt=16)
        for oc in range(ND):
            for half in range(2):
                pb = PA if (oc * 2 + half) % 2 == 0 else PB
                groups = []
                for st in range(2):
                    lo = half * 1024 + st * 512
                    gr = [(pb[:, st * 512:(st + 1) * 512],
                           wbuf[:, ic, oc * 128:(oc + 1) * 128],
                           x_bf[:, ic, lo:lo + 512], (wbuf.b, x_bf.b)) for ic in range(ND)]
                    gr.append((pb[:, st * 512:(st + 1) * 512],
                               brow[0:1, oc * 128:(oc + 1) * 128],
                               c["ones_bf"][0:1, 0:512], (brow.b, c["ones_bf"].b)))
                    groups.append(gr)
                P.mm_set(pb.b, groups)
                eng = "s" if (oc + half) % 2 == 0 else "v"
                P.do(eng, "tensor_copy", dst[:, oc, half * 1024:(half + 1) * 1024],
                     pb[:], r=[pb.b], w=[dst.b])

    with ExitStack() as pst:
        PA = P.psum(pst, "pa", [128, 1024], F32)
        PB = P.psum(pst, "pb", [128, 1024], F32)
        PTf = P.psum(pst, "ptf", [128, 128], F32)
        proj_fm(QT, "l1_wq", "l1_bq")
        proj_fm(KT, "l1_wk", "l1_bk")

        Ef = P.sbuf("Ef", [128, 2048], F32)
        Edump = P.sbuf("Edump", [128, 2048], BF16)
        rows = [dict(m=P.sbuf(f"r_m{i}", [128, 1], F32), se=P.sbuf(f"r_se{i}", [128, 1], F32),
                     es=P.sbuf(f"r_es{i}", [128, 1], F32), den=P.sbuf(f"r_dn{i}", [128, 1], F32),
                     rd=P.sbuf(f"r_rd{i}", [128, 1], F32), rho=P.sbuf(f"r_rh{i}", [128, 1], F32))
                for i in range(2)]
        for qt in range(NT):
            R = rows[qt % 2]
            m, se, es, den, rd, rho = R["m"], R["se"], R["es"], R["den"], R["rd"], R["rho"]
            Pk = (qt + 1) * 128
            nstr = -(-Pk // 512)
            for st in range(nstr):
                w_ = min(512, Pk - st * 512)
                pb = PA if st % 2 == 0 else PB
                groups = [[(pb[:, 0:w_], QT[:, dcc, qt * 128:(qt + 1) * 128],
                            KT[:, dcc, st * 512:st * 512 + w_], (QT.b, KT.b))
                           for dcc in range(ND)]]
                P.mm_set(pb.b, groups)
                eng = "s" if st % 2 == 0 else "v"
                if eng == "s":
                    P.do("s", "activation", Ef[:, st * 512:st * 512 + w_], pb[:, 0:w_],
                         AF.Identity, scale=1.0 / 32.0, r=[pb.b], w=[Ef.b])
                else:
                    P.do("v", "tensor_scalar_mul", Ef[:, st * 512:st * 512 + w_], pb[:, 0:w_],
                         1.0 / 32.0, r=[pb.b], w=[Ef.b])
            P.do("p", "tensor_mul", Ef[:, qt * 128:(qt + 1) * 128],
                 Ef[:, qt * 128:(qt + 1) * 128], c["am_mul"][:],
                 r=[Ef.b, c["am_mul"].b], w=[Ef.b])
            P.do("v", "reduce_max", m[:], Ef[:, 0:Pk], axis=AX.X, negate=True,
                 r=[Ef.b], w=[m.b])
            if qt < NT - 1:
                P.do("v", "tensor_scalar_min", m[:], m[:], 0.0, r=[m.b], w=[m.b])
            P.do("s", "activation", Edump[:, 0:Pk], Ef[:, 0:Pk], AF.Exp,
                 bias=m[:], accum_out=se[:], r=[Ef.b, m.b], w=[Edump.b, se.b])
            P.do("s", "activation", es[:], m[:], AF.Exp, r=[m.b], w=[es.b])
            cnt = float(T - Pk)
            P.do("v", "scalar_tensor_tensor", den[:], es[:], cnt, se[:], ALU.mult, ALU.add,
                 r=[es.b, se.b], w=[den.b])
            P.do("v", "reciprocal", rd[:], den[:], r=[den.b], w=[rd.b])
            P.do("v", "tensor_mul", rho[:], den[:], rd[:], r=[den.b, rd.b], w=[rho.b])
            P.do("t", "transpose", PTf[0:1, :], rho[:], c["ident_f"][:],
                 r=[rho.b, c["ident_f"].b], w=[PTf.b])
            P.do("v", "tensor_copy", rho_row[0:1, qt * 128:(qt + 1) * 128], PTf[0:1, :],
                 r=[PTf.b], w=[rho_row.b])
        if A.get("dbg"):
            P.do("q", "dma_start", A["dbg"]["d_rho"], rho_row[:], r=[rho_row.b], sem="Ds", amt=16)
            P.do("q", "dma_start", A["dbg"]["d_ef"], Ef[:], r=[Ef.b], sem="Ds", amt=16)
            P.do("v", "tensor_copy", Ef[:], QT[:, 0, :], r=[QT.b], w=[Ef.b])
            P.do("q", "dma_start", A["dbg"]["d_qt"], Ef[:], r=[Ef.b], sem="Ds", amt=16)
            P.do("v", "tensor_copy", Ef[:], KT[:, 0, :], r=[KT.b], w=[Ef.b])
            P.do("q", "dma_start", A["dbg"]["d_kt"], Ef[:], r=[Ef.b], sem="Ds", amt=16)
        P.sfree(Ef)
        P.sfree(Edump)
        for R in rows:
            for sb in R.values():
                P.sfree(sb)
        P.sfree(QT)
        P.sfree(KT)

        rrep = P.sbuf("rrep", [128, 2048], F32)
        for half in range(2):
            pb = PA if half == 0 else PB
            groups = [[(pb[:, st * 512:(st + 1) * 512], c["ones_f"][0:1, 0:128],
                        rho_row[0:1, half * 1024 + st * 512:half * 1024 + (st + 1) * 512],
                        (c["ones_f"].b, rho_row.b))] for st in range(2)]
            P.mm_set(pb.b, groups)
            P.do("v", "tensor_copy", rrep[:, half * 1024:(half + 1) * 1024], pb[:],
                 r=[pb.b], w=[rrep.b])

        y = P.sbuf("y_fm", [128, ND, 2048], F32)
        xres = P.sbuf("xres", [128, 2, 2048], F32)
        tmp = [P.sbuf(f"vtmp{i}", [128, 1024], F32) for i in range(2)]
        P.do("p", "dma_start", wbuf[:], ins["l1_wv"][:].rearrange("(c p) o -> p c o", p=128),
             w=[wbuf.b], sem="Dw", amt=16)
        P.do("p", "dma_start", brow[:], ins["l1_bv"][:], w=[brow.b], sem="Dw", amt=16)
        for oc in range(ND):
            if A.get("parked"):
                src = A["park"][:].rearrange("(c p) t -> p c t", p=128)[:, oc, :]
                P.do("q", "dma_start", xres[:, oc % 2, :], src, r=[A["park_buf"]],
                     w=[xres.b], sem="Dx", amt=16)
            else:
                src = ins["xin"][:].rearrange("(c p) t -> p c t", p=128)[:, oc, :]
                P.do("q", "dma_start", xres[:, oc % 2, :], src, w=[xres.b], sem="Dx", amt=16)
            for half in range(2):
                pb = PA if (oc * 2 + half) % 2 == 0 else PB
                tm = tmp[(oc * 2 + half) % 2]
                groups = []
                for st in range(2):
                    lo = half * 1024 + st * 512
                    gr = [(pb[:, st * 512:(st + 1) * 512],
                           wbuf[:, ic, oc * 128:(oc + 1) * 128],
                           x_bf[:, ic, lo:lo + 512], (wbuf.b, x_bf.b)) for ic in range(ND)]
                    gr.append((pb[:, st * 512:(st + 1) * 512],
                               brow[0:1, oc * 128:(oc + 1) * 128],
                               c["ones_bf"][0:1, 0:512], (brow.b, c["ones_bf"].b)))
                    groups.append(gr)
                P.mm_set(pb.b, groups)
                P.do("v", "tensor_mul", tm[:], pb[:], rrep[:, half * 1024:(half + 1) * 1024],
                     r=[pb.b, rrep.b], w=[tm.b])
                P.do("p", "tensor_add", y[:, oc, half * 1024:(half + 1) * 1024],
                     xres[:, oc % 2, half * 1024:(half + 1) * 1024], tm[:],
                     r=[xres.b, tm.b], w=[y.b])
        if A.get("dbg2"):
            P.do("q", "dma_start", A["dbg2"]["d_yall"].rearrange("(c p) t -> p c t", p=128), y[:],
                 r=[y.b], sem="Ds", amt=16)
        if A.get("dbg"):
            P.do("q", "dma_start", A["dbg"]["d_y"], y[:, 0, :], r=[y.b], sem="Ds", amt=16)
            P.do("q", "dma_start", A["dbg"]["d_rrep"], rrep[:, 0:2048], r=[rrep.b], sem="Ds", amt=16)
            P.do("q", "dma_start", A["dbg"]["d_yall"].rearrange("(c p) t -> p c t", p=128), y[:],
                 r=[y.b], sem="Ds", amt=16)
        P.sfree(x_bf)
        P.sfree(wbuf)
        P.sfree(brow)
        P.sfree(rho_row)
        P.sfree(rrep)
        P.sfree(xres)
        for t_ in tmp:
            P.sfree(t_)
        ln_feature_major(P, A, {"PA": PA, "PB": PB}, y,
                         with_store=A["out_ap"] if A.get("store_out") else None)
    return y


def emit_l1ffn(P, A):
    ins, c = A["in"], A["c"]
    NE = 16

    if A.get("stream") is None:
        x_fm = P.sbuf("x_fm", [128, ND, 2048], F32)
        P.do("q", "dma_start", x_fm[:], ins["xin"][:].rearrange("(c p) t -> p c t", p=128),
             w=[x_fm.b], sem="Dx", amt=16)
    else:
        x_fm = A["stream"]
    x_bf = P.sbuf("x_bf", [128, ND, 2048], BF16)
    cast_stream_to_bf(P, A, x_fm, x_bf)

    rw = P.sbuf("rw", [128, ND, 16], BF16)
    P.do("p", "dma_start", rw[:], ins["l1_rwT"][:].rearrange("(c p) e -> p c e", p=128),
         w=[rw.b], sem="Dw", amt=16)
    rbias = P.sbuf("rbias", [1, 16], F32)
    P.do("q", "dma_start", rbias[:], ins["l1_rbias"][:], w=[rbias.b], sem="Dx", amt=16)
    cT = P.sbuf("cT", [16, 2048], F32)
    sc = P.sbuf("sc", [128, NT, 16], F32)
    rbrep = P.sbuf("rbrep", [128, 16], F32)

    with ExitStack() as pst:
        PD = P.psum(pst, "pd", [128, 512], F32)
        PTf = P.psum(pst, "ptf", [128, 128], F32)
        P.mm_set(PD.b, [[(PD[:, 0:16], c["ones_f"][0:1, 0:128], rbias[0:1, 0:16],
                          (c["ones_f"].b, rbias.b))]])
        P.do("v", "tensor_copy", rbrep[:], PD[:, 0:16], r=[PD.b], w=[rbrep.b])
        negt = P.sbuf("negt", [128, 16], F32)
        P.do("v", "memset", negt[:], -1.0e9, w=[negt.b])
        ws = [dict(sch=P.sbuf(f"w_sch{i}", [128, 16], F32), gs=P.sbuf(f"w_gs{i}", [128, 4], F32),
                   t4=P.sbuf(f"w_t4{i}", [128, 4], F32), m2=P.sbuf(f"w_m2{i}", [128, 1], F32),
                   gm=P.sbuf(f"w_gm{i}", [128, 4], I32), mk=P.sbuf(f"w_mk{i}", [128, 16], F32),
                   t16=P.sbuf(f"w_t16{i}", [128, 16], F32), m1=P.sbuf(f"w_m1{i}", [128, 1], F32),
                   wv=P.sbuf(f"w_wv{i}", [128, 16], F32)) for i in range(2)]
        for tc in range(NT):
            W = ws[tc % 2]
            sch, gs, t4, m2, gm, mk, t16, m1, wv = (W["sch"], W["gs"], W["t4"], W["m2"],
                                                    W["gm"], W["mk"], W["t16"], W["m1"], W["wv"])
            g = [(PD[:, 0:16], x_bf[:, ic, tc * 128:(tc + 1) * 128], rw[:, ic, :],
                  (x_bf.b, rw.b)) for ic in range(ND)]
            P.mm_set(PD.b, [g])
            P.do("s", "activation", sc[:, tc, :], PD[:, 0:16], AF.Sigmoid, r=[PD.b], w=[sc.b])
            P.do("v", "tensor_add", sch[:], sc[:, tc, :], rbrep[:], r=[sc.b, rbrep.b], w=[sch.b])
            schg = sch[:].rearrange("p (g j) -> p g j", j=4)
            first = True
            for a in range(4):
                for bj in range(a + 1, 4):
                    P.do("v", "tensor_add", t4[:], schg[:, :, a], schg[:, :, bj],
                         r=[sch.b], w=[t4.b])
                    if first:
                        P.do("v", "tensor_copy", gs[:], t4[:], r=[t4.b], w=[gs.b])
                        first = False
                    else:
                        P.do("v", "tensor_max", gs[:], gs[:], t4[:], r=[t4.b, gs.b], w=[gs.b])
            first = True
            for a in range(4):
                for bj in range(a + 1, 4):
                    P.do("v", "tensor_tensor", t4[:, 0:1], gs[:, a:a + 1], gs[:, bj:bj + 1],
                         ALU.min, r=[gs.b], w=[t4.b])
                    if first:
                        P.do("v", "tensor_copy", m2[:], t4[:, 0:1], r=[t4.b], w=[m2.b])
                        first = False
                    else:
                        P.do("v", "tensor_max", m2[:], m2[:], t4[:, 0:1], r=[t4.b, m2.b], w=[m2.b])
            P.do("v", "tensor_scalar", gm[:], gs[:], m2[:], None, ALU.is_ge,
                 r=[gs.b, m2.b], w=[gm.b])
            mkg = mk[:].rearrange("p (g j) -> p g j", j=4)
            for j in range(4):
                P.do("v", "select", mkg[:, :, j], gm[:], schg[:, :, j], negt[:, 0:4],
                     r=[gm.b, sch.b, negt.b], w=[mk.b])
            P.do("v", "tensor_copy", t16[:], mk[:], r=[mk.b], w=[t16.b])
            for it in range(3):
                P.do("v", "reduce_max", m1[:], t16[:], axis=AX.X, r=[t16.b], w=[m1.b])
                P.do("v", "tensor_scalar", wv[:], t16[:], m1[:], None, ALU.is_ge,
                     r=[t16.b, m1.b], w=[wv.b])
                P.do("v", "scalar_tensor_tensor", t16[:], wv[:], NEG, t16[:], ALU.mult, ALU.add,
                     r=[wv.b, t16.b], w=[t16.b])
            P.do("v", "reduce_max", m1[:], t16[:], axis=AX.X, r=[t16.b], w=[m1.b])
            P.do("v", "tensor_scalar", wv[:], mk[:], m1[:], None, ALU.is_ge,
                 r=[mk.b, m1.b], w=[wv.b])
            P.do("v", "tensor_mul", wv[:], wv[:], sc[:, tc, :], r=[sc.b, wv.b], w=[wv.b])
            P.do("v", "reduce_sum", m1[:], wv[:], axis=AX.X, r=[wv.b], w=[m1.b])
            P.do("v", "tensor_scalar_add", m1[:], m1[:], 1.0e-20, r=[m1.b], w=[m1.b])
            P.do("v", "reciprocal", m2[:], m1[:], r=[m1.b], w=[m2.b])
            P.do("v", "tensor_scalar", wv[:], wv[:], m2[:], 2.5, ALU.mult, ALU.mult,
                 r=[wv.b, m2.b], w=[wv.b])
            P.do("t", "transpose", PTf[0:16, :], wv[:], c["ident_f"][:],
                 r=[wv.b, c["ident_f"].b], w=[PTf.b])
            P.do("v", "tensor_copy", cT[0:16, tc * 128:(tc + 1) * 128], PTf[0:16, :],
                 r=[PTf.b], w=[cT.b])
        for W in ws:
            for sb in W.values():
                P.sfree(sb)
        P.sfree(negt)
        P.sfree(sc)
        P.sfree(rw)
        P.sfree(rbias)
        P.sfree(rbrep)

    wgT = P.sbuf("wgT", [128, ND, 512], BF16)
    wuT = P.sbuf("wuT", [128, ND, 512], BF16)
    wdT = P.sbuf("wdT", [128, 4, 1024], BF16)
    g_bf = P.sbuf("g_bf", [128, 4, 2048], BF16)
    u_bf = P.sbuf("u_bf", [128, 4, 2048], BF16)
    crep = P.sbuf("crep", [128, 2048], F32)
    tmp = [P.sbuf(f"dtmp{i}", [128, 1024], F32) for i in range(2)]

    with ExitStack() as pst:
        PU = [P.psum(pst, "pu0", [128, 1024], F32), P.psum(pst, "pu1", [128, 1024], F32)]
        PW = [P.psum(pst, "pw0", [128, 1024], F32), P.psum(pst, "pw1", [128, 1024], F32)]
        rot = 0
        for e in range(NE):
            P.do("p", "dma_start", wgT[:],
                 ins["l1_wgT"][e].rearrange("(c p) i -> p c i", p=128), w=[wgT.b], sem="Dw", amt=16)
            P.do("p", "dma_start", wuT[:],
                 ins["l1_wuT"][e].rearrange("(c p) i -> p c i", p=128), w=[wuT.b], sem="Dw", amt=16)
            P.do("p", "dma_start", wdT[:],
                 ins["l1_wdT"][e].rearrange("(c p) h -> p c h", p=128), w=[wdT.b], sem="Dw", amt=16)
            for half in range(2):
                pu = PU[half]
                groups = [[(pu[:, st * 512:(st + 1) * 512],
                            c["eye16"][0:16, e * 128:(e + 1) * 128],
                            cT[0:16, half * 1024 + st * 512:half * 1024 + (st + 1) * 512],
                            (c["eye16"].b, cT.b))] for st in range(2)]
                P.mm_set(pu.b, groups)
                P.do("v", "tensor_copy", crep[:, half * 1024:(half + 1) * 1024], pu[:],
                     r=[pu.b], w=[crep.b])
            for (wt, dst, silu) in ((wgT, g_bf, True), (wuT, u_bf, False)):
                for icc in range(4):
                    for th in range(2):
                        pu = PU[rot % 2]
                        rot += 1
                        groups = []
                        for st in range(2):
                            lo = th * 1024 + st * 512
                            gr = [(pu[:, st * 512:(st + 1) * 512],
                                   wt[:, hcc, icc * 128:(icc + 1) * 128],
                                   x_bf[:, hcc, lo:lo + 512], (wt.b, x_bf.b))
                                  for hcc in range(ND)]
                            groups.append(gr)
                        P.mm_set(pu.b, groups)
                        if silu:
                            P.do("s", "activation", dst[:, icc, th * 1024:(th + 1) * 1024],
                                 pu[:], AF.Silu, r=[pu.b], w=[dst.b])
                        else:
                            P.do("v", "tensor_copy", dst[:, icc, th * 1024:(th + 1) * 1024],
                                 pu[:], r=[pu.b], w=[dst.b])
            for icc in range(4):
                P.do("p", "tensor_mul", g_bf[:, icc, :], g_bf[:, icc, :], u_bf[:, icc, :],
                     r=[u_bf.b, g_bf.b], w=[g_bf.b])
            for hcc in range(ND):
                for th in range(2):
                    pw = PW[rot % 2]
                    tm = tmp[rot % 2]
                    rot += 1
                    groups = []
                    for st in range(2):
                        lo = th * 1024 + st * 512
                        gr = [(pw[:, st * 512:(st + 1) * 512],
                               wdT[:, icc, hcc * 128:(hcc + 1) * 128],
                               g_bf[:, icc, lo:lo + 512], (wdT.b, g_bf.b))
                              for icc in range(4)]
                        groups.append(gr)
                    P.mm_set(pw.b, groups)
                    P.do("v", "tensor_mul", tm[:], pw[:], crep[:, th * 1024:(th + 1) * 1024],
                         r=[pw.b, crep.b], w=[tm.b])
                    P.do("p", "tensor_add", x_fm[:, hcc, th * 1024:(th + 1) * 1024],
                         x_fm[:, hcc, th * 1024:(th + 1) * 1024], tm[:],
                         r=[tm.b, x_fm.b], w=[x_fm.b])
        P.sfree(wgT)
        P.sfree(wuT)
        P.sfree(wdT)
        P.sfree(g_bf)
        P.sfree(u_bf)
        P.sfree(crep)
        P.sfree(cT)
        P.sfree(x_bf)
        for t_ in tmp:
            P.sfree(t_)
        ln_feature_major(P, A, {"PA": PU[0], "PB": PU[1]}, x_fm,
                         with_store=A["out_ap"] if A.get("store_out") else None)
    return x_fm


def emit_l0attn_with_wo(P, A):
    wo_sb = P.sbuf("wo_sb", [128, ND, 1024], BF16)
    A["_l0_wo_sb"] = wo_sb
    P.do("p", "dma_start", wo_sb[:],
         A["in"]["l0_woT"][:].rearrange("(c p) o -> p c o", p=128),
         w=[wo_sb.b], sem="Dw", amt=16)
    y = emit_l0attn(P, A)
    P.sfree(wo_sb)
    return y


_PHASE_INPUTS = {
    "l0attn": [("l0_wqT", [D, D]), ("l0_wkT", [D, D]), ("l0_wvpT", [D, D]),
               ("l0_woT", [D, D]), ("l0_gamma", [1, D]), ("l0_beta", [1, D])],
    "l0ffn": [("l0_wg", [D, 8]), ("l0_bg", [1, 8]), ("l0_w1", [8, D, 4096]),
              ("l0_b1", [8, 4096]), ("l0_w2", [8, 4096, D]), ("l0_b2", [8, D]),
              ("c_eye8", [8, 1024])],
    "l1attn": [("l1_wq", [D, D]), ("l1_bq", [1, D]), ("l1_wk", [D, D]), ("l1_bk", [1, D]),
               ("l1_wv", [D, D]), ("l1_bv", [1, D])],
    "l1ffn": [("l1_rwT", [D, 16]), ("l1_rbias", [1, 16]), ("l1_wgT", [16, D, 512]),
              ("l1_wuT", [16, D, 512]), ("l1_wdT", [16, 512, D]), ("c_eye16", [16, 2048])],
}

_PHASE_EMIT = {
    "l0attn": emit_l0attn_with_wo,
    "l0ffn": emit_l0ffn,
    "l1attn": emit_l1attn,
    "l1ffn": emit_l1ffn,
}

_PROG_CACHE = {}


def build_program(phases):
    key = tuple(phases)
    if key in _PROG_CACHE:
        return _PROG_CACHE[key]
    P = Prog()
    nc = P.nc
    ins = {}
    ins["xin"] = nc.dram_tensor("xin", [D, T], F32, kind="ExternalInput").ap()
    ins["c_ident"] = nc.dram_tensor("c_ident", [128, 128], F32, kind="ExternalInput").ap()
    ins["c_ones"] = nc.dram_tensor("c_ones", [1, 2048], F32, kind="ExternalInput").ap()
    for ph in phases:
        for nm, shp in _PHASE_INPUTS[ph]:
            if nm not in ins:
                ins[nm] = nc.dram_tensor(nm, list(shp), F32, kind="ExternalInput").ap()
    yout = nc.dram_tensor("yout", [D, T], F32, kind="ExternalOutput").ap()
    park = None
    if len(phases) > 1:
        park = nc.dram_tensor("park", [D, T], F32, kind="Internal").ap()

    A = {"in": ins, "out_ap": yout, "stream": None, "park": park, "parked": False}
    import os
    if os.environ.get("KDBG2") and phases == ["l1attn"]:
        A["dbg2"] = {"d_yall": nc.dram_tensor("d_yall", [D, T], F32, kind="ExternalOutput").ap()}
    if os.environ.get("KDBG") and phases == ["l1attn"]:
        A["dbg"] = {
            "d_rho": nc.dram_tensor("d_rho", [1, 2048], F32, kind="ExternalOutput").ap(),
            "d_ef": nc.dram_tensor("d_ef", [128, 2048], F32, kind="ExternalOutput").ap(),
            "d_qt": nc.dram_tensor("d_qt", [128, 2048], F32, kind="ExternalOutput").ap(),
            "d_kt": nc.dram_tensor("d_kt", [128, 2048], F32, kind="ExternalOutput").ap(),
            "d_y": nc.dram_tensor("d_y", [128, 2048], F32, kind="ExternalOutput").ap(),
            "d_rrep": nc.dram_tensor("d_rrep", [128, 2048], F32, kind="ExternalOutput").ap(),
            "d_yall": nc.dram_tensor("d_yall", [D, T], F32, kind="ExternalOutput").ap(),
        }
    load_consts(P, A)
    for i, ph in enumerate(phases):
        A["store_out"] = (i == len(phases) - 1)
        y = _PHASE_EMIT[ph](P, A)
        A["stream"] = y
        A["parked"] = False
        if i < len(phases) - 1:
            P.barrier()
    P.wait_tok("q", ("Ds", P.cnt["Ds"]))
    P.build()
    _PROG_CACHE[key] = P
    return P


def _prep_weights(i):
    perm = np.arange(D).reshape(HD, NH).T.reshape(-1)
    f = np.float32
    return {
        "l0_wqT": np.ascontiguousarray(i["L0_attn_Wq"].T, f),
        "l0_wkT": np.ascontiguousarray(i["L0_attn_Wk"].T, f),
        "l0_wvpT": np.ascontiguousarray(i["L0_attn_Wv"][perm].T, f),
        "l0_woT": np.ascontiguousarray(i["L0_attn_Wo"].T, f),
        "l0_gamma": np.ascontiguousarray(i["L0_attn_gamma"].reshape(1, D), f),
        "l0_beta": np.ascontiguousarray(i["L0_attn_beta"].reshape(1, D), f),
        "l0_wg": np.ascontiguousarray(i["L0_ffn_Wg"], f),
        "l0_bg": np.ascontiguousarray(i["L0_ffn_bg"].reshape(1, 8), f),
        "l0_w1": np.ascontiguousarray(i["L0_ffn_W1"], f),
        "l0_b1": np.ascontiguousarray(i["L0_ffn_b1"], f),
        "l0_w2": np.ascontiguousarray(i["L0_ffn_W2"], f),
        "l0_b2": np.ascontiguousarray(i["L0_ffn_b2"], f),
        "l1_wq": np.ascontiguousarray(i["L1_attn_Wq"], f),
        "l1_bq": np.ascontiguousarray(i["L1_attn_bq"].reshape(1, D), f),
        "l1_wk": np.ascontiguousarray(i["L1_attn_Wk"], f),
        "l1_bk": np.ascontiguousarray(i["L1_attn_bk"].reshape(1, D), f),
        "l1_wv": np.ascontiguousarray(i["L1_attn_Wv"], f),
        "l1_bv": np.ascontiguousarray(i["L1_attn_bv"].reshape(1, D), f),
        "l1_rwT": np.ascontiguousarray(i["L1_ffn_router_weight"].T, f),
        "l1_rbias": np.ascontiguousarray(i["L1_ffn_router_bias"].reshape(1, 16), f),
        "l1_wgT": np.ascontiguousarray(i["L1_ffn_w_gate"].transpose(0, 2, 1), f),
        "l1_wuT": np.ascontiguousarray(i["L1_ffn_w_up"].transpose(0, 2, 1), f),
        "l1_wdT": np.ascontiguousarray(i["L1_ffn_w_down"].transpose(0, 2, 1), f),
    }


def _consts():
    eye8 = np.zeros((8, 1024), np.float32)
    for k in range(8):
        eye8[k, k * 128:(k + 1) * 128] = 1.0
    eye16 = np.zeros((16, 2048), np.float32)
    for k in range(16):
        eye16[k, k * 128:(k + 1) * 128] = 1.0
    return {
        "c_ident": np.eye(128, dtype=np.float32),
        "c_ones": np.ones((1, 2048), dtype=np.float32),
        "c_eye8": eye8,
        "c_eye16": eye16,
    }


_NEEDED = {
    "l0attn": ["xin", "c_ident", "c_ones", "l0_wqT", "l0_wkT", "l0_wvpT", "l0_woT",
               "l0_gamma", "l0_beta"],
    "l0ffn": ["xin", "c_ident", "c_ones", "l0_wg", "l0_bg", "l0_w1", "l0_b1",
              "l0_w2", "l0_b2", "c_eye8"],
    "l1attn": ["xin", "c_ident", "c_ones", "l1_wq", "l1_bq", "l1_wk", "l1_bk",
               "l1_wv", "l1_bv"],
    "l1ffn": ["xin", "c_ident", "c_ones", "l1_rwT", "l1_rbias", "l1_wgT",
              "l1_wuT", "l1_wdT", "c_eye16"],
}

PHASE_ORDER = ["l0attn", "l0ffn", "l1attn", "l1ffn"]
FUSED = True


def run_phase(ph, streams, shared):
    P = build_program([ph])
    names = set(_NEEDED[ph])
    in_maps = []
    for b in range(NCORES):
        m = {k: v for k, v in shared.items() if k in names}
        m["xin"] = streams[b]
        in_maps.append(m)
    res = run_bass_kernel_spmd(P.nc, in_maps, core_ids=list(range(NCORES)))
    return [r["yout"] for r in res.results]


def kernel(**inputs):
    shared = _prep_weights(inputs)
    shared.update(_consts())
    x = inputs["L0_attn_x"]
    streams = [np.ascontiguousarray(x[b].T) for b in range(NCORES)]
    if FUSED:
        P = build_program(PHASE_ORDER)
        names = set(n for ph in PHASE_ORDER for n in _NEEDED[ph])
        in_maps = []
        for b in range(NCORES):
            m = {k: v for k, v in shared.items() if k in names}
            m["xin"] = streams[b]
            in_maps.append(m)
        res = run_bass_kernel_spmd(P.nc, in_maps, core_ids=list(range(NCORES)))
        streams = [r["yout"] for r in res.results]
    else:
        for ph in PHASE_ORDER:
            streams = run_phase(ph, streams, shared)
    return np.stack([s.T for s in streams]).astype(np.float32)


# version 04
# speedup vs baseline: 1.0367x; 1.0367x over previous
import numpy as np
import ml_dtypes
from contextlib import ExitStack

import concourse.bass as bass
import concourse.mybir as mybir
from concourse.bass_utils import run_bass_kernel_spmd

F32 = mybir.dt.float32
BF16 = mybir.dt.bfloat16
I32 = mybir.dt.int32
AF = mybir.ActivationFunctionType
ALU = mybir.AluOpType
AX = mybir.AxisListType

NCORES = 8
T = 2048
D = 1024
NT = T // 128
ND = D // 128
NH = 16
HD = 64
NEG = -1.0e30
LN_EPS = 1.0e-5

SBUF_BASE = 16512
SBUF_TOP = 229344


class Buf:
    __slots__ = ("w", "rs", "nm")

    def __init__(self, nm=""):
        self.w = None
        self.rs = []
        self.nm = nm


class SB:
    __slots__ = ("h", "b", "off", "sz")

    def __init__(self, h, b, off=None, sz=None):
        self.h = h
        self.b = b
        self.off = off
        self.sz = sz

    def __getitem__(self, idx):
        return self.h[idx]


def RD(*bufs):
    out = []
    for b in bufs:
        if b is not None and b.w is not None:
            out.append(b.w)
    return out


def WR(*bufs):
    out = []
    for b in bufs:
        if b is None:
            continue
        if b.w is not None:
            out.append(b.w)
        out.extend(t for t in b.rs if t is not None)
    return out


class _Alloc:
    def __init__(self, base, top):
        self.free = [(base, top - base)]
        self.pending = []

    def alloc(self, size, align=128):
        size = -(-size // align) * align
        for i, (o, s) in enumerate(self.free):
            oo = -(-o // align) * align
            pad = oo - o
            if s >= size + pad:
                del self.free[i]
                if pad:
                    self.free.insert(i, (o, pad))
                rest = s - pad - size
                if rest:
                    self.free.append((oo + size, rest))
                self.free.sort()
                toks = []
                keep = []
                for (po, ps, pt) in self.pending:
                    if po < oo + size and oo < po + ps:
                        toks.extend(pt)
                        if po < oo:
                            keep.append((po, oo - po, pt))
                        if po + ps > oo + size:
                            keep.append((oo + size, po + ps - (oo + size), pt))
                    else:
                        keep.append((po, ps, pt))
                self.pending = keep
                return oo, size, toks
        raise MemoryError(f"SBUF alloc of {size} failed; free={self.free}")

    def free_(self, off, size, tokens):
        self.free.append((off, size))
        self.free.sort()
        merged = []
        for o, s in self.free:
            if merged and merged[-1][0] + merged[-1][1] == o:
                merged[-1] = (merged[-1][0], merged[-1][1] + s)
            else:
                merged.append((o, s))
        self.free = [tuple(x) for x in merged]
        if tokens:
            self.pending.append((off, size, list(tokens)))


_DTS = {F32: 4, BF16: 2, I32: 4}


class Prog:
    ENGS = ("t", "v", "s", "p", "q")

    def __init__(self, name="prog"):
        self.nc = bass.Bass("TRN2", target_bir_lowering=False)
        self.q = {k: [] for k in self.ENGS}
        self.cnt = {}
        self.sems = {}
        self.sat = {k: {} for k in self.ENGS}
        self._stack = ExitStack()
        for k in self.ENGS:
            self._mksem("E" + k)
        for nm in ("Dw", "Dx", "Ds"):
            self._mksem(nm)
        self._mksem("bar")
        self.al = _Alloc(SBUF_BASE, SBUF_TOP)
        self._uid = 0

    def _mksem(self, name):
        self.sems[name] = self._stack.enter_context(self.nc.semaphore(name))
        self.cnt[name] = 0

    def uid(self):
        self._uid += 1
        return self._uid

    def op(self, eng, fname, *args, deps=(), sem=None, amt=1, inc=True, **kw):
        sat = self.sat[eng]
        need = {}
        for d in deps:
            if d is None:
                continue
            s, v = d
            if need.get(s, 0) < v:
                need[s] = v
        waits = []
        for s, v in need.items():
            if sat.get(s, 0) < v:
                waits.append((s, v))
                sat[s] = v
        if eng == "s" and fname == "tensor_copy":
            fname = "copy"
        if sem is None and inc:
            sem = "E" + eng
            amt = 1
        tok = None
        semh = None
        if sem is not None and inc:
            self.cnt[sem] += amt
            tok = (sem, self.cnt[sem])
            semh = self.sems[sem]
        sems = self.sems

        def emit(E, waits=tuple(waits), fname=fname, args=args, kw=kw, semh=semh, amt=amt):
            for s, v in waits:
                E.wait_ge(sems[s], v)
            ins = getattr(E, fname)(*args, **kw)
            if semh is not None:
                ins.then_inc(semh, amt)

        self.q[eng].append(emit)
        return tok

    def do(self, eng, fname, *args, r=(), w=(), extra=(), sem=None, amt=1, **kw):
        deps = RD(*r) + WR(*w) + list(extra)
        tok = self.op(eng, fname, *args, deps=deps, sem=sem, amt=amt, **kw)
        for b in r:
            if b is not None:
                b.rs.append(tok)
        for b in w:
            if b is not None:
                b.w = tok
                b.rs = []
        return tok

    def wait_tok(self, eng, tok):
        if tok is None:
            return
        s, v = tok
        if self.sat[eng].get(s, 0) < v:
            self.sat[eng][s] = v
            sems = self.sems

            def emit(E, s=s, v=v):
                E.wait_ge(sems[s], v)

            self.q[eng].append(emit)

    def mm_set(self, pbuf, groups, extra=()):
        first_deps = WR(pbuf) + list(extra)
        rbufs_all = []
        total = sum(len(g) for g in groups)
        n = 0
        tok = None
        for g in groups:
            L = len(g)
            for i, (pap, lh, rh, rb) in enumerate(g):
                n += 1
                deps = RD(*rb)
                if n == 1:
                    deps = deps + first_deps
                tok = self.op("t", "matmul", pap, lh, rh,
                              start=(i == 0), stop=(i == L - 1),
                              deps=deps, inc=(n == total))
                for b in rb:
                    if b is not None and b not in rbufs_all:
                        rbufs_all.append(b)
        pbuf.w = tok
        pbuf.rs = []
        for b in rbufs_all:
            b.rs.append(tok)
        return tok

    def sbuf(self, name, shape, dtype):
        bpp = int(np.prod(shape[1:])) * _DTS[dtype]
        off, sz, toks = self.al.alloc(bpp)
        h = self.nc.alloc_sbuf_tensor_at(f"{name}_{self.uid()}", list(shape), dtype, offset=off)
        b = Buf(name)
        b.rs = list(toks)
        return SB(h, b, off, sz)

    def sfree(self, sb):
        toks = ([sb.b.w] if sb.b.w else []) + sb.b.rs
        self.al.free_(sb.off, sb.sz, [t for t in toks if t])

    def psum(self, stack, name, shape, dtype=F32):
        h = stack.enter_context(self.nc.psum_tensor(f"{name}_{self.uid()}", list(shape), dtype))
        return SB(h, Buf(name))

    def barrier(self):
        self.cnt["bar"] += 5
        target = self.cnt["bar"]
        sems = self.sems
        for k in self.ENGS:
            def em(E, t=target):
                E.sem_inc(sems["bar"], 1)
                E.wait_ge(sems["bar"], t)
            self.q[k].append(em)
        for k in self.ENGS:
            for s in ("Et", "Ev", "Es", "Ep"):
                if self.sat[k].get(s, 0) < self.cnt[s]:
                    self.sat[k][s] = self.cnt[s]

    def build(self):
        with self.nc.Block() as blk:
            blk.tensor(lambda E: [f(E) for f in self.q["t"]])
            blk.vector(lambda E: [f(E) for f in self.q["v"]])
            blk.scalar(lambda E: [f(E) for f in self.q["s"]])
            blk.gpsimd(lambda E: [f(E) for f in self.q["p"]])
            blk.sync(lambda E: [f(E) for f in self.q["q"]])
        self._stack.close()
        return self.nc


def load_consts(P, A):
    c = {}
    c["ident_bf"] = P.sbuf("ident_bf", [128, 128], BF16)
    c["ident_f"] = P.sbuf("ident_f", [128, 128], F32)
    c["ones_bf"] = P.sbuf("ones_bf", [1, 512], BF16)
    c["ones_f"] = P.sbuf("ones_f", [1, 128], F32)
    c["ones_col"] = P.sbuf("ones_col", [128, 1], F32)
    ins = A["in"]
    P.do("p", "dma_start", c["ident_bf"][:], ins["c_ident"][:], w=[c["ident_bf"].b], sem="Dw", amt=16)
    P.do("q", "dma_start", c["ident_f"][:], ins["c_ident"][:], w=[c["ident_f"].b], sem="Dx", amt=16)
    P.do("p", "dma_start", c["ones_bf"][:], ins["c_ones"][0:1, 0:512], w=[c["ones_bf"].b], sem="Dw", amt=16)
    P.do("q", "dma_start", c["ones_f"][:], ins["c_ones"][0:1, 0:128], w=[c["ones_f"].b], sem="Dx", amt=16)
    P.do("q", "dma_start", c["ones_col"][:], ins["c_ones"][0:1, 0:128].rearrange("o p -> p o"),
         w=[c["ones_col"].b], sem="Dx", amt=16)
    c["am_add"] = P.sbuf("am_add", [128, 128], F32)
    c["am_mul"] = P.sbuf("am_mul", [128, 128], F32)
    P.do("p", "memset", c["am_add"][:], 0.0, w=[c["am_add"].b])
    P.do("p", "affine_select", c["am_add"][:], c["am_add"][:], [[-1, 128]], ALU.is_ge, NEG,
         base=0, channel_multiplier=1, r=[c["am_add"].b], w=[c["am_add"].b])
    P.do("p", "memset", c["am_mul"][:], 1.0, w=[c["am_mul"].b])
    P.do("p", "affine_select", c["am_mul"][:], c["am_mul"][:], [[-1, 128]], ALU.is_ge, 0.0,
         base=0, channel_multiplier=1, r=[c["am_mul"].b], w=[c["am_mul"].b])
    A["c"] = c


def ln_feature_major(P, A, ps, y, with_store=None):
    c = A["c"]
    PA, PB = ps["PA"], ps["PB"]
    sq = P.sbuf("ln_sq", [128, ND, 1024], F32)
    s1 = P.sbuf("ln_s1", [1, 2048], F32)
    s2 = P.sbuf("ln_s2", [1, 2048], F32)
    arow = P.sbuf("ln_ar", [1, 2048], F32)
    brow = P.sbuf("ln_br", [1, 2048], F32)
    arep = P.sbuf("ln_arep", [128, 2048], F32)
    brep = P.sbuf("ln_brep", [128, 2048], F32)

    for half in range(2):
        groups = []
        for st in range(2):
            lo = half * 1024 + st * 512
            g = [(PA[0:1, st * 512:(st + 1) * 512], c["ones_col"][:], y[:, cch, lo:lo + 512],
                  (c["ones_col"].b, y.b)) for cch in range(ND)]
            groups.append(g)
        P.mm_set(PA.b, groups)
        P.do("v", "tensor_copy", s1[0:1, half * 1024:(half + 1) * 1024], PA[0:1, :],
             r=[PA.b], w=[s1.b])
        for cch in range(ND):
            P.do("s", "activation", sq[:, cch, :], y[:, cch, half * 1024:(half + 1) * 1024],
                 AF.Square, r=[y.b], w=[sq.b])
        groups = []
        for st in range(2):
            g = [(PB[0:1, st * 512:(st + 1) * 512], c["ones_col"][:],
                  sq[:, cch, st * 512:(st + 1) * 512],
                  (c["ones_col"].b, sq.b)) for cch in range(ND)]
            groups.append(g)
        P.mm_set(PB.b, groups)
        P.do("v", "tensor_copy", s2[0:1, half * 1024:(half + 1) * 1024], PB[0:1, :],
             r=[PB.b], w=[s2.b])
    mu = P.sbuf("ln_mu", [1, 2048], F32)
    P.do("v", "tensor_scalar_mul", mu[:], s1[:], 1.0 / D, r=[s1.b], w=[mu.b])
    P.do("v", "tensor_scalar_mul", s2[:], s2[:], 1.0 / D, r=[s2.b], w=[s2.b])
    P.do("v", "tensor_mul", arow[:], mu[:], mu[:], r=[mu.b], w=[arow.b])
    P.do("v", "tensor_sub", s2[:], s2[:], arow[:], r=[s2.b, arow.b], w=[s2.b])
    P.do("v", "tensor_scalar_add", s2[:], s2[:], LN_EPS, r=[s2.b], w=[s2.b])
    P.do("s", "activation", s2[:], s2[:], AF.Ln, r=[s2.b], w=[s2.b])
    P.do("s", "activation", arow[:], s2[:], AF.Exp, scale=-0.5, r=[s2.b], w=[arow.b])
    P.do("v", "tensor_mul", brow[:], mu[:], arow[:], r=[mu.b, arow.b], w=[brow.b])
    P.do("v", "tensor_scalar_mul", brow[:], brow[:], -1.0, r=[brow.b], w=[brow.b])
    for (row, rep) in ((arow, arep), (brow, brep)):
        for half in range(2):
            pb = PA if half == 0 else PB
            groups = []
            for st in range(2):
                lo = half * 1024 + st * 512
                groups.append([(pb[:, st * 512:(st + 1) * 512], c["ones_f"][0:1, 0:128],
                                row[0:1, lo:lo + 512], (c["ones_f"].b, row.b))])
            P.mm_set(pb.b, groups)
            P.do("v" if half == 0 else "s", "tensor_copy", rep[:, half * 1024:(half + 1) * 1024],
                 pb[:], r=[pb.b], w=[rep.b])
    for cch in range(ND):
        P.do("v", "tensor_mul", y[:, cch, :], y[:, cch, :], arep[:], r=[arep.b, y.b], w=[y.b])
        P.do("p", "tensor_add", y[:, cch, :], y[:, cch, :], brep[:], r=[brep.b, y.b], w=[y.b])
        if with_store is not None:
            P.do("q", "dma_start", with_store.rearrange("(c p) t -> p c t", p=128)[:, cch, :],
                 y[:, cch, :], r=[y.b], sem="Ds", amt=16)
    for t_ in (sq, s1, s2, arow, brow, arep, brep, mu):
        P.sfree(t_)


def cast_stream_to_bf(P, A, x_fm, x_bf):
    for cch in range(ND):
        eng = ("v", "s", "p")[cch % 3]
        P.do(eng, "tensor_copy", x_bf[:, cch, :], x_fm[:, cch, :], r=[x_fm.b], w=[x_bf.b])


def emit_l0attn(P, A):
    ins = A["in"]
    c = A["c"]
    nc = P.nc

    x_bf = P.sbuf("x_bf", [128, ND, 2048], BF16)
    if A.get("stream") is None:
        P.do("p", "dma_start", x_bf[:],
             ins["xin"][:].rearrange("(c p) t -> p c t", p=128),
             w=[x_bf.b], sem="Dw", amt=16)
    else:
        cast_stream_to_bf(P, A, A["stream"], x_bf)
        park_stream(P, A)

    g_rep = P.sbuf("g_rep", [128, 1024], F32)
    b_rep = P.sbuf("b_rep", [128, 1024], F32)
    grow = P.sbuf("grow", [1, 1024], F32)
    brow = P.sbuf("brow", [1, 1024], F32)
    P.do("q", "dma_start", grow[:], ins["l0_gamma"][:], w=[grow.b], sem="Dx", amt=16)
    P.do("q", "dma_start", brow[:], ins["l0_beta"][:], w=[brow.b], sem="Dx", amt=16)

    with ExitStack() as pst:
        PRA = P.psum(pst, "pra", [128, 1024], F32)
        PRB = P.psum(pst, "prb", [128, 1024], F32)
        for (row, rep, pb) in ((grow, g_rep, PRA), (brow, b_rep, PRB)):
            groups = [[(pb[:, st * 512:(st + 1) * 512], c["ones_f"][0:1, 0:128],
                        row[0:1, st * 512:(st + 1) * 512], (c["ones_f"].b, row.b))]
                      for st in range(2)]
            P.mm_set(pb.b, groups)
            P.do("v", "tensor_copy", rep[:], pb[:], r=[pb.b], w=[rep.b])

        qT = P.sbuf("qT", [128, ND, 2048], BF16)
        kT = P.sbuf("kT", [128, ND, 2048], BF16)
        vtm = P.sbuf("vtm", [128, NT, 1024], BF16)
        wbuf = P.sbuf("wbuf", [128, ND, 1024], BF16)

        def load_w(name):
            return P.do("p", "dma_start", wbuf[:],
                        ins[name][:].rearrange("(c p) o -> p c o", p=128),
                        w=[wbuf.b], sem="Dw", amt=16)

        for wname, dst, scale in (("l0_wqT", qT, None), ("l0_wkT", kT, 0.125)):
            load_w(wname)
            for oc in range(ND):
                for half in range(2):
                    pb = PRA if (oc * 2 + half) % 2 == 0 else PRB
                    groups = []
                    for st in range(2):
                        lo = half * 1024 + st * 512
                        groups.append([(pb[:, st * 512:(st + 1) * 512],
                                        wbuf[:, ic, oc * 128:(oc + 1) * 128],
                                        x_bf[:, ic, lo:lo + 512],
                                        (wbuf.b, x_bf.b)) for ic in range(ND)])
                    P.mm_set(pb.b, groups)
                    eng = "s" if (oc + half) % 2 == 0 else "v"
                    if scale is None:
                        P.do(eng, "tensor_copy", dst[:, oc, half * 1024:(half + 1) * 1024],
                             pb[:], r=[pb.b], w=[dst.b])
                    else:
                        P.do("s", "activation", dst[:, oc, half * 1024:(half + 1) * 1024],
                             pb[:], AF.Identity, scale=scale, r=[pb.b], w=[dst.b])
        load_w("l0_wvpT")
        for tc in range(NT):
            pb = PRA if tc % 2 == 0 else PRB
            groups = []
            for st in range(2):
                groups.append([(pb[:, st * 512:(st + 1) * 512],
                                x_bf[:, ic, tc * 128:(tc + 1) * 128],
                                wbuf[:, ic, st * 512:(st + 1) * 512],
                                (wbuf.b, x_bf.b)) for ic in range(ND)])
            P.mm_set(pb.b, groups)
            eng = "s" if tc % 2 == 0 else "v"
            P.do(eng, "tensor_copy", vtm[:, tc, :], pb[:], r=[pb.b], w=[vtm.b])
    P.sfree(wbuf)
    P.sfree(grow)
    P.sfree(brow)
    P.sfree(x_bf)

    qkv = P.sbuf("qkv", [128, NT, 1024], BF16)
    iter_bufs = []
    for par in range(2):
        iter_bufs.append(dict(
            Ef=P.sbuf(f"Ef{par}", [128, 2048], F32),
            Eb=P.sbuf(f"Eb{par}", [128, 2048], BF16),
            ET=P.sbuf(f"ET{par}", [128, NT, 128], BF16),
            negm=P.sbuf(f"negm{par}", [128, 1], F32),
            den=P.sbuf(f"den{par}", [128, 1], F32),
            rd=P.sbuf(f"rd{par}", [128, 1], F32),
        ))

    with ExitStack() as pst:
        PS = [P.psum(pst, f"ps{i}", [128, 512], F32) for i in range(4)]
        PT = [P.psum(pst, "pt0", [128, 128], BF16), P.psum(pst, "pt1", [128, 128], BF16)]
        PAV = [P.psum(pst, "pav0", [128, 64], F32), P.psum(pst, "pav1", [128, 64], F32)]

        iters = [(h, qt) for h in range(NH) for qt in range(NT)]

        def stageA(i):
            h, qt = iters[i]
            hc, ho = h // 2, (h % 2) * 64
            B = iter_bufs[i % 2]
            Ef, Eb = B["Ef"], B["Eb"]
            negm, den, rd = B["negm"], B["den"], B["rd"]
            Pkeys = (qt + 1) * 128
            for st in range(-(-Pkeys // 512)):
                w_ = min(512, Pkeys - st * 512)
                pb = PS[2 * (i % 2) + st % 2]
                P.mm_set(pb.b, [[(pb[:, 0:w_],
                                  qT[ho:ho + 64, hc, qt * 128:(qt + 1) * 128],
                                  kT[ho:ho + 64, hc, st * 512:st * 512 + w_],
                                  (qT.b, kT.b))]])
                eng = "s" if st % 2 == 0 else "v"
                P.do(eng, "tensor_copy", Ef[:, st * 512:st * 512 + w_], pb[:, 0:w_],
                     r=[pb.b], w=[Ef.b])
            P.do("p", "tensor_add", Ef[:, qt * 128:(qt + 1) * 128],
                 Ef[:, qt * 128:(qt + 1) * 128], c["am_add"][:],
                 r=[Ef.b, c["am_add"].b], w=[Ef.b])
            P.do("v", "reduce_max", negm[:], Ef[:, 0:Pkeys], axis=AX.X, negate=True,
                 r=[Ef.b], w=[negm.b])
            P.do("s", "activation", Eb[:, 0:Pkeys], Ef[:, 0:Pkeys], AF.Exp,
                 bias=negm[:], accum_out=den[:], r=[Ef.b, negm.b], w=[Eb.b, den.b])
            P.do("v", "reciprocal", rd[:], den[:], r=[den.b], w=[rd.b])

        def stageB(i):
            h, qt = iters[i]
            B = iter_bufs[i % 2]
            Eb, ET, rd = B["Eb"], B["ET"], B["rd"]
            for sc in range(qt + 1):
                pt = PT[sc % 2]
                P.do("t", "transpose", pt[:], Eb[:, sc * 128:(sc + 1) * 128],
                     c["ident_bf"][:], r=[Eb.b, c["ident_bf"].b], w=[pt.b])
                eng = "v" if sc % 2 == 0 else "s"
                P.do(eng, "tensor_copy", ET[:, sc, :], pt[:], r=[pt.b], w=[ET.b])
            pav = PAV[i % 2]
            P.mm_set(pav.b, [[(pav[:], ET[:, sc, :], vtm[:, sc, h * 64:h * 64 + 64],
                               (ET.b, vtm.b)) for sc in range(qt + 1)]])
            P.do("s", "activation", qkv[:, qt, h * 64:h * 64 + 64], pav[:],
                 AF.Identity, scale=rd[:], r=[pav.b, rd.b], w=[qkv.b])

        stageA(0)
        for i in range(len(iters)):
            if i + 1 < len(iters):
                stageA(i + 1)
            stageB(i)

    P.sfree(qT)
    P.sfree(kT)
    P.sfree(vtm)
    for B in iter_bufs:
        for sb in B.values():
            P.sfree(sb)

    dump = P.sbuf("dump", [128, 1024], BF16)
    tmp = P.sbuf("mlntmp", [128, 1024], F32)
    rows = [dict(s1=P.sbuf(f"m_s1{i}", [128, 1], F32), s2=P.sbuf(f"m_s2{i}", [128, 1], F32),
                 rstd=P.sbuf(f"m_r{i}", [128, 1], F32), nmr=P.sbuf(f"m_n{i}", [128, 1], F32))
            for i in range(2)]
    for tc in range(NT):
        R = rows[tc % 2]
        s1, s2, rstd, nmr = R["s1"], R["s2"], R["rstd"], R["nmr"]
        P.do("v", "reduce_sum", s1[:], qkv[:, tc, :], axis=AX.X, r=[qkv.b], w=[s1.b])
        P.do("s", "activation", dump[:], qkv[:, tc, :], AF.Square, accum_out=s2[:],
             r=[qkv.b], w=[dump.b, s2.b])
        P.do("v", "tensor_scalar_mul", s1[:], s1[:], 1.0 / D, r=[s1.b], w=[s1.b])
        P.do("v", "tensor_scalar_mul", s2[:], s2[:], 1.0 / D, r=[s2.b], w=[s2.b])
        P.do("v", "tensor_mul", rstd[:], s1[:], s1[:], r=[s1.b], w=[rstd.b])
        P.do("v", "tensor_sub", s2[:], s2[:], rstd[:], r=[s2.b, rstd.b], w=[s2.b])
        P.do("v", "tensor_scalar_add", s2[:], s2[:], LN_EPS, r=[s2.b], w=[s2.b])
        P.do("s", "activation", s2[:], s2[:], AF.Ln, r=[s2.b], w=[s2.b])
        P.do("s", "activation", rstd[:], s2[:], AF.Exp, scale=-0.5, r=[s2.b], w=[rstd.b])
        P.do("v", "tensor_mul", nmr[:], s1[:], rstd[:], r=[s1.b, rstd.b], w=[nmr.b])
        P.do("v", "tensor_scalar_mul", nmr[:], nmr[:], -1.0, r=[nmr.b], w=[nmr.b])
        P.do("s", "activation", tmp[:], qkv[:, tc, :], AF.Identity, scale=rstd[:],
             bias=nmr[:], r=[qkv.b, rstd.b, nmr.b], w=[tmp.b])
        P.do("v", "tensor_mul", tmp[:], tmp[:], g_rep[:], r=[g_rep.b, tmp.b], w=[tmp.b])
        P.do("p", "tensor_add", qkv[:, tc, :], tmp[:], b_rep[:], r=[tmp.b, b_rep.b], w=[qkv.b])

    P.sfree(dump)
    P.sfree(tmp)
    for R in rows:
        for sb in R.values():
            P.sfree(sb)
    P.sfree(g_rep)
    P.sfree(b_rep)

    lnT = P.sbuf("lnT", [128, ND, 2048], BF16)
    with ExitStack() as pst:
        PT = [P.psum(pst, "pt0", [128, 128], BF16), P.psum(pst, "pt1", [128, 128], BF16)]
        for tc in range(NT):
            for oc in range(ND):
                pt = PT[(tc * ND + oc) % 2]
                P.do("t", "transpose", pt[:], qkv[:, tc, oc * 128:(oc + 1) * 128],
                     c["ident_bf"][:], r=[qkv.b, c["ident_bf"].b], w=[pt.b])
                eng = "v" if oc % 2 == 0 else "s"
                P.do(eng, "tensor_copy", lnT[:, oc, tc * 128:(tc + 1) * 128], pt[:],
                     r=[pt.b], w=[lnT.b])
    P.sfree(qkv)

    y = P.sbuf("y_fm", [128, ND, 2048], F32)
    xres = P.sbuf("xres", [128, 2, 2048], F32)
    with ExitStack() as pst:
        PA = P.psum(pst, "pa", [128, 1024], F32)
        PB = P.psum(pst, "pb", [128, 1024], F32)
        wbuf = A["_l0_wo_sb"]
        for oc in range(ND):
            if A.get("parked"):
                src = A["park"][:].rearrange("(c p) t -> p c t", p=128)[:, oc, :]
                P.do("q", "dma_start", xres[:, oc % 2, :], src, r=[A["park_buf"]],
                     w=[xres.b], sem="Dx", amt=16)
            else:
                src = A["in"]["xin"][:].rearrange("(c p) t -> p c t", p=128)[:, oc, :]
                P.do("q", "dma_start", xres[:, oc % 2, :], src, w=[xres.b], sem="Dx", amt=16)
            for half in range(2):
                pb = PA if (oc * 2 + half) % 2 == 0 else PB
                groups = []
                for st in range(2):
                    lo = half * 1024 + st * 512
                    groups.append([(pb[:, st * 512:(st + 1) * 512],
                                    wbuf[:, ic, oc * 128:(oc + 1) * 128],
                                    lnT[:, ic, lo:lo + 512],
                                    (wbuf.b, lnT.b)) for ic in range(ND)])
                P.mm_set(pb.b, groups)
                P.do("v", "tensor_add", y[:, oc, half * 1024:(half + 1) * 1024],
                     pb[:], xres[:, oc % 2, half * 1024:(half + 1) * 1024],
                     r=[pb.b, xres.b], w=[y.b])
        P.sfree(lnT)
        P.sfree(xres)
        ln_feature_major(P, A, {"PA": PA, "PB": PB}, y,
                         with_store=A["out_ap"] if A.get("store_out") else None)
    return y


def park_stream(P, A):
    x = A["stream"]
    pb = A.setdefault("park_buf", Buf("park"))
    P.do("q", "dma_start", A["park"][:].rearrange("(c p) t -> p c t", p=128), x[:],
         r=[x.b], w=[pb], sem="Ds", amt=16)
    P.sfree(x)
    A["stream"] = None
    A["parked"] = True


def emit_l0ffn(P, A):
    ins, c = A["in"], A["c"]

    if A.get("stream") is None:
        x_fm = P.sbuf("x_fm", [128, ND, 2048], F32)
        P.do("q", "dma_start", x_fm[:], ins["xin"][:].rearrange("(c p) t -> p c t", p=128),
             w=[x_fm.b], sem="Dx", amt=16)
    else:
        x_fm = A["stream"]
    x_bf = P.sbuf("x_bf", [128, ND, 2048], BF16)
    cast_stream_to_bf(P, A, x_fm, x_bf)

    c["eye8"] = P.sbuf("eye8", [8, 1024], F32)
    P.do("q", "dma_start", c["eye8"][:], ins["c_eye8"][:], w=[c["eye8"].b], sem="Dx", amt=16)
    wg = P.sbuf("wg", [128, ND, 8], BF16)
    bg = P.sbuf("bg", [1, 8], BF16)
    P.do("p", "dma_start", wg[:], ins["l0_wg"][:].rearrange("(c p) e -> p c e", p=128),
         w=[wg.b], sem="Dw", amt=16)
    P.do("p", "dma_start", bg[:], ins["l0_bg"][:], w=[bg.b], sem="Dw", amt=16)
    gnT = P.sbuf("gnT", [8, 2048], F32)
    gb = P.sbuf("gb", [128, NT, 8], F32)
    rws = [dict(m=P.sbuf(f"g_m{i}", [128, 1], F32), d=P.sbuf(f"g_d{i}", [128, 1], F32),
                r=P.sbuf(f"g_r{i}", [128, 1], F32), t8=P.sbuf(f"g_t8{i}", [128, 8], F32),
                u8=P.sbuf(f"g_u8{i}", [128, 8], F32)) for i in range(2)]
    with ExitStack() as pst:
        PD = P.psum(pst, "pd", [128, 512], F32)
        PTf = P.psum(pst, "ptf", [128, 128], F32)
        for tc in range(NT):
            R = rws[tc % 2]
            m, dsum, rd, t8, u8 = R["m"], R["d"], R["r"], R["t8"], R["u8"]
            g = [(PD[:, 0:8], x_bf[:, ic, tc * 128:(tc + 1) * 128], wg[:, ic, :],
                  (x_bf.b, wg.b)) for ic in range(ND)]
            g.append((PD[:, 0:8], c["ones_bf"][0:1, 0:128], bg[:], (c["ones_bf"].b, bg.b)))
            P.mm_set(PD.b, [g])
            gsl = gb[:, tc, :]
            P.do("s", "tensor_copy", gsl, PD[:, 0:8], r=[PD.b], w=[gb.b])
            P.do("v", "reduce_max", m[:], gsl, axis=AX.X, negate=True, r=[gb.b], w=[m.b])
            P.do("s", "activation", gsl, gsl, AF.Exp, bias=m[:], accum_out=dsum[:],
                 r=[gb.b, m.b], w=[gb.b, dsum.b])
            P.do("v", "reciprocal", rd[:], dsum[:], r=[dsum.b], w=[rd.b])
            P.do("v", "tensor_scalar_mul", gsl, gsl, rd[:], r=[gb.b, rd.b], w=[gb.b])
            P.do("v", "reduce_max", m[:], gsl, axis=AX.X, r=[gb.b], w=[m.b])
            P.do("v", "tensor_scalar", t8[:], gsl, m[:], None, ALU.is_ge,
                 r=[gb.b, m.b], w=[t8.b])
            P.do("v", "scalar_tensor_tensor", u8[:], t8[:], NEG, gsl, ALU.mult, ALU.add,
                 r=[t8.b, gb.b], w=[u8.b])
            P.do("v", "reduce_max", m[:], u8[:], axis=AX.X, r=[u8.b], w=[m.b])
            P.do("v", "tensor_scalar", t8[:], gsl, m[:], None, ALU.is_ge,
                 r=[gb.b, m.b], w=[t8.b])
            P.do("v", "tensor_mul", t8[:], t8[:], gsl, r=[gb.b, t8.b], w=[t8.b])
            P.do("v", "reduce_sum", dsum[:], t8[:], axis=AX.X, r=[t8.b], w=[dsum.b])
            P.do("v", "tensor_scalar_max", dsum[:], dsum[:], 1.0e-12, r=[dsum.b], w=[dsum.b])
            P.do("v", "reciprocal", rd[:], dsum[:], r=[dsum.b], w=[rd.b])
            P.do("v", "tensor_scalar_mul", t8[:], t8[:], rd[:], r=[t8.b, rd.b], w=[t8.b])
            P.do("t", "transpose", PTf[0:8, :], t8[:], c["ident_f"][:],
                 r=[t8.b, c["ident_f"].b], w=[PTf.b])
            P.do("v", "tensor_copy", gnT[0:8, tc * 128:(tc + 1) * 128], PTf[0:8, :],
                 r=[PTf.b], w=[gnT.b])
    for R in rws:
        for sb in R.values():
            P.sfree(sb)
    P.sfree(gb)
    P.sfree(wg)
    P.sfree(bg)

    h_sc = P.sbuf("h_sc", [128, 8, 2048], BF16)
    w1e = [P.sbuf(f"w1e{i}", [128, ND, 256], BF16) for i in range(2)]
    w2q = [P.sbuf(f"w2q{i}", [128, 8, 1024], BF16) for i in range(2)]
    b1row = [P.sbuf(f"b1row{i}", [1, 1024], BF16) for i in range(2)]
    gnrep = P.sbuf("gnrep", [128, 2048], F32)
    tmp = [P.sbuf(f"dtmp{i}", [128, 1024], F32) for i in range(2)]

    with ExitStack() as pst:
        PU = [P.psum(pst, "pu0", [128, 1024], F32), P.psum(pst, "pu1", [128, 1024], F32)]
        PW = [P.psum(pst, "pw0", [128, 1024], F32), P.psum(pst, "pw1", [128, 1024], F32)]
        rot_u = 0
        rot_w = 0
        for e in range(8):
            for half in range(2):
                pu = PU[half]
                groups = [[(pu[:, st * 512:(st + 1) * 512],
                            c["eye8"][0:8, e * 128:(e + 1) * 128],
                            gnT[0:8, half * 1024 + st * 512:half * 1024 + (st + 1) * 512],
                            (c["eye8"].b, gnT.b))] for st in range(2)]
                P.mm_set(pu.b, groups)
                P.do("v", "tensor_copy", gnrep[:, half * 1024:(half + 1) * 1024], pu[:],
                     r=[pu.b], w=[gnrep.b])
            for hq in range(4):
                b1r = b1row[hq % 2]
                P.do("p", "dma_start", b1r[:], ins["l0_b1"][e:e + 1, hq * 1024:(hq + 1) * 1024],
                     w=[b1r.b], sem="Dw", amt=16)
                w2 = w2q[(e * 4 + hq) % 2]
                P.do("p", "dma_start", w2[:],
                     ins["l0_w2"][e, hq * 1024:(hq + 1) * 1024, :].rearrange("(c p) d -> p c d", p=128),
                     w=[w2.b], sem="Dw", amt=16)
                for j in range(4):
                    w1 = w1e[j % 2]
                    P.do("p", "dma_start", w1[:],
                         ins["l0_w1"][e, :, hq * 1024 + j * 256:hq * 1024 + (j + 1) * 256]
                         .rearrange("(c p) h -> p c h", p=128),
                         w=[w1.b], sem="Dw", amt=16)
                    for hc in range(2):
                        hglob = hq * 1024 + j * 256 + hc * 128
                        for th in range(2):
                            pu = PU[rot_u % 2]
                            rot_u += 1
                            groups = []
                            for st in range(2):
                                lo = th * 1024 + st * 512
                                gr = [(pu[:, st * 512:(st + 1) * 512],
                                       w1[:, ic, hc * 128:(hc + 1) * 128],
                                       x_bf[:, ic, lo:lo + 512],
                                       (w1.b, x_bf.b)) for ic in range(ND)]
                                gr.append((pu[:, st * 512:(st + 1) * 512],
                                           b1r[0:1, j * 256 + hc * 128:j * 256 + hc * 128 + 128],
                                           c["ones_bf"][0:1, 0:512],
                                           (b1r.b, c["ones_bf"].b)))
                                groups.append(gr)
                            P.mm_set(pu.b, groups)
                            P.do("s", "activation", h_sc[:, j * 2 + hc, th * 1024:(th + 1) * 1024],
                                 pu[:], AF.Gelu, r=[pu.b], w=[h_sc.b])
                for dc in range(ND):
                    for th in range(2):
                        pw = PW[rot_w % 2]
                        tm = tmp[rot_w % 2]
                        rot_w += 1
                        groups = []
                        for st in range(2):
                            lo = th * 1024 + st * 512
                            gr = [(pw[:, st * 512:(st + 1) * 512],
                                   w2[:, hcq, dc * 128:(dc + 1) * 128],
                                   h_sc[:, hcq, lo:lo + 512],
                                   (w2.b, h_sc.b)) for hcq in range(8)]
                            groups.append(gr)
                        P.mm_set(pw.b, groups)
                        P.do("v", "tensor_mul", tm[:], pw[:],
                             gnrep[:, th * 1024:(th + 1) * 1024],
                             r=[pw.b, gnrep.b], w=[tm.b])
                        P.do("p", "tensor_add", x_fm[:, dc, th * 1024:(th + 1) * 1024],
                             x_fm[:, dc, th * 1024:(th + 1) * 1024], tm[:],
                             r=[tm.b, x_fm.b], w=[x_fm.b])
        P.sfree(gnrep)
        for t_ in w1e + b1row:
            P.sfree(t_)
        b2f = P.sbuf("b2f", [8, 1024], F32)
        P.do("q", "dma_start", b2f[:], ins["l0_b2"][:], w=[b2f.b], sem="Dx", amt=16)
        for dc in range(ND):
            for th in range(2):
                pw = PW[(dc * 2 + th) % 2]
                groups = []
                for st in range(2):
                    lo = th * 1024 + st * 512
                    groups.append([(pw[:, st * 512:(st + 1) * 512],
                                    b2f[0:8, dc * 128:(dc + 1) * 128],
                                    gnT[0:8, lo:lo + 512], (b2f.b, gnT.b))])
                P.mm_set(pw.b, groups)
                P.do("v", "tensor_add", x_fm[:, dc, th * 1024:(th + 1) * 1024],
                     x_fm[:, dc, th * 1024:(th + 1) * 1024], pw[:],
                     r=[pw.b, x_fm.b], w=[x_fm.b])
        P.sfree(b2f)
        P.sfree(h_sc)
        for t_ in w2q:
            P.sfree(t_)
        P.sfree(gnT)
        P.sfree(c.pop("eye8"))
        P.sfree(x_bf)
        for t_ in tmp:
            P.sfree(t_)
        ln_feature_major(P, A, {"PA": PU[0], "PB": PU[1]}, x_fm,
                         with_store=A["out_ap"] if A.get("store_out") else None)
    return x_fm


def emit_l1attn(P, A):
    ins, c = A["in"], A["c"]

    x_bf = P.sbuf("x_bf", [128, ND, 2048], BF16)
    if A.get("stream") is None:
        P.do("p", "dma_start", x_bf[:], ins["xin"][:].rearrange("(c p) t -> p c t", p=128),
             w=[x_bf.b], sem="Dw", amt=16)
    else:
        cast_stream_to_bf(P, A, A["stream"], x_bf)
        park_stream(P, A)

    QT = P.sbuf("QT", [128, ND, 2048], BF16)
    KT = P.sbuf("KT", [128, ND, 2048], BF16)
    wbuf = P.sbuf("wbuf", [128, ND, 1024], BF16)
    brow = P.sbuf("brow", [1, 1024], BF16)
    rho_row = P.sbuf("rho_row", [1, 2048], F32)

    def proj_fm(dst, wname, bname, scale_evac=None):
        P.do("p", "dma_start", wbuf[:], ins[wname][:].rearrange("(c p) o -> p c o", p=128),
             w=[wbuf.b], sem="Dw", amt=16)
        P.do("p", "dma_start", brow[:], ins[bname][:], w=[brow.b], sem="Dw", amt=16)
        for oc in range(ND):
            for half in range(2):
                pb = PA if (oc * 2 + half) % 2 == 0 else PB
                groups = []
                for st in range(2):
                    lo = half * 1024 + st * 512
                    gr = [(pb[:, st * 512:(st + 1) * 512],
                           wbuf[:, ic, oc * 128:(oc + 1) * 128],
                           x_bf[:, ic, lo:lo + 512], (wbuf.b, x_bf.b)) for ic in range(ND)]
                    gr.append((pb[:, st * 512:(st + 1) * 512],
                               brow[0:1, oc * 128:(oc + 1) * 128],
                               c["ones_bf"][0:1, 0:512], (brow.b, c["ones_bf"].b)))
                    groups.append(gr)
                P.mm_set(pb.b, groups)
                eng = "s" if (oc + half) % 2 == 0 else "v"
                P.do(eng, "tensor_copy", dst[:, oc, half * 1024:(half + 1) * 1024],
                     pb[:], r=[pb.b], w=[dst.b])

    with ExitStack() as pst:
        PA = P.psum(pst, "pa", [128, 1024], F32)
        PB = P.psum(pst, "pb", [128, 1024], F32)
        PTf = P.psum(pst, "ptf", [128, 128], F32)
        proj_fm(QT, "l1_wq", "l1_bq")
        proj_fm(KT, "l1_wk", "l1_bk")

        Ef = P.sbuf("Ef", [128, 2048], F32)
        Edump = P.sbuf("Edump", [128, 2048], BF16)
        rows = [dict(m=P.sbuf(f"r_m{i}", [128, 1], F32), se=P.sbuf(f"r_se{i}", [128, 1], F32),
                     es=P.sbuf(f"r_es{i}", [128, 1], F32), den=P.sbuf(f"r_dn{i}", [128, 1], F32),
                     rd=P.sbuf(f"r_rd{i}", [128, 1], F32), rho=P.sbuf(f"r_rh{i}", [128, 1], F32))
                for i in range(2)]
        for qt in range(NT):
            R = rows[qt % 2]
            m, se, es, den, rd, rho = R["m"], R["se"], R["es"], R["den"], R["rd"], R["rho"]
            Pk = (qt + 1) * 128
            nstr = -(-Pk // 512)
            for st in range(nstr):
                w_ = min(512, Pk - st * 512)
                pb = PA if st % 2 == 0 else PB
                groups = [[(pb[:, 0:w_], QT[:, dcc, qt * 128:(qt + 1) * 128],
                            KT[:, dcc, st * 512:st * 512 + w_], (QT.b, KT.b))
                           for dcc in range(ND)]]
                P.mm_set(pb.b, groups)
                eng = "s" if st % 2 == 0 else "v"
                if eng == "s":
                    P.do("s", "activation", Ef[:, st * 512:st * 512 + w_], pb[:, 0:w_],
                         AF.Identity, scale=1.0 / 32.0, r=[pb.b], w=[Ef.b])
                else:
                    P.do("v", "tensor_scalar_mul", Ef[:, st * 512:st * 512 + w_], pb[:, 0:w_],
                         1.0 / 32.0, r=[pb.b], w=[Ef.b])
            P.do("p", "tensor_mul", Ef[:, qt * 128:(qt + 1) * 128],
                 Ef[:, qt * 128:(qt + 1) * 128], c["am_mul"][:],
                 r=[Ef.b, c["am_mul"].b], w=[Ef.b])
            P.do("v", "reduce_max", m[:], Ef[:, 0:Pk], axis=AX.X, negate=True,
                 r=[Ef.b], w=[m.b])
            if qt < NT - 1:
                P.do("v", "tensor_scalar_min", m[:], m[:], 0.0, r=[m.b], w=[m.b])
            P.do("s", "activation", Edump[:, 0:Pk], Ef[:, 0:Pk], AF.Exp,
                 bias=m[:], accum_out=se[:], r=[Ef.b, m.b], w=[Edump.b, se.b])
            P.do("s", "activation", es[:], m[:], AF.Exp, r=[m.b], w=[es.b])
            cnt = float(T - Pk)
            P.do("v", "scalar_tensor_tensor", den[:], es[:], cnt, se[:], ALU.mult, ALU.add,
                 r=[es.b, se.b], w=[den.b])
            P.do("v", "reciprocal", rd[:], den[:], r=[den.b], w=[rd.b])
            P.do("v", "tensor_mul", rho[:], den[:], rd[:], r=[den.b, rd.b], w=[rho.b])
            P.do("t", "transpose", PTf[0:1, :], rho[:], c["ident_f"][:],
                 r=[rho.b, c["ident_f"].b], w=[PTf.b])
            P.do("v", "tensor_copy", rho_row[0:1, qt * 128:(qt + 1) * 128], PTf[0:1, :],
                 r=[PTf.b], w=[rho_row.b])
        if A.get("dbg"):
            P.do("q", "dma_start", A["dbg"]["d_rho"], rho_row[:], r=[rho_row.b], sem="Ds", amt=16)
            P.do("q", "dma_start", A["dbg"]["d_ef"], Ef[:], r=[Ef.b], sem="Ds", amt=16)
            P.do("v", "tensor_copy", Ef[:], QT[:, 0, :], r=[QT.b], w=[Ef.b])
            P.do("q", "dma_start", A["dbg"]["d_qt"], Ef[:], r=[Ef.b], sem="Ds", amt=16)
            P.do("v", "tensor_copy", Ef[:], KT[:, 0, :], r=[KT.b], w=[Ef.b])
            P.do("q", "dma_start", A["dbg"]["d_kt"], Ef[:], r=[Ef.b], sem="Ds", amt=16)
        P.sfree(Ef)
        P.sfree(Edump)
        for R in rows:
            for sb in R.values():
                P.sfree(sb)
        P.sfree(QT)
        P.sfree(KT)

        rrep = P.sbuf("rrep", [128, 2048], F32)
        for half in range(2):
            pb = PA if half == 0 else PB
            groups = [[(pb[:, st * 512:(st + 1) * 512], c["ones_f"][0:1, 0:128],
                        rho_row[0:1, half * 1024 + st * 512:half * 1024 + (st + 1) * 512],
                        (c["ones_f"].b, rho_row.b))] for st in range(2)]
            P.mm_set(pb.b, groups)
            P.do("v", "tensor_copy", rrep[:, half * 1024:(half + 1) * 1024], pb[:],
                 r=[pb.b], w=[rrep.b])

        y = P.sbuf("y_fm", [128, ND, 2048], F32)
        xres = P.sbuf("xres", [128, 2, 2048], F32)
        tmp = [P.sbuf(f"vtmp{i}", [128, 1024], F32) for i in range(2)]
        P.do("p", "dma_start", wbuf[:], ins["l1_wv"][:].rearrange("(c p) o -> p c o", p=128),
             w=[wbuf.b], sem="Dw", amt=16)
        P.do("p", "dma_start", brow[:], ins["l1_bv"][:], w=[brow.b], sem="Dw", amt=16)
        for oc in range(ND):
            if A.get("parked"):
                src = A["park"][:].rearrange("(c p) t -> p c t", p=128)[:, oc, :]
                P.do("q", "dma_start", xres[:, oc % 2, :], src, r=[A["park_buf"]],
                     w=[xres.b], sem="Dx", amt=16)
            else:
                src = ins["xin"][:].rearrange("(c p) t -> p c t", p=128)[:, oc, :]
                P.do("q", "dma_start", xres[:, oc % 2, :], src, w=[xres.b], sem="Dx", amt=16)
            for half in range(2):
                pb = PA if (oc * 2 + half) % 2 == 0 else PB
                tm = tmp[(oc * 2 + half) % 2]
                groups = []
                for st in range(2):
                    lo = half * 1024 + st * 512
                    gr = [(pb[:, st * 512:(st + 1) * 512],
                           wbuf[:, ic, oc * 128:(oc + 1) * 128],
                           x_bf[:, ic, lo:lo + 512], (wbuf.b, x_bf.b)) for ic in range(ND)]
                    gr.append((pb[:, st * 512:(st + 1) * 512],
                               brow[0:1, oc * 128:(oc + 1) * 128],
                               c["ones_bf"][0:1, 0:512], (brow.b, c["ones_bf"].b)))
                    groups.append(gr)
                P.mm_set(pb.b, groups)
                P.do("v", "tensor_mul", tm[:], pb[:], rrep[:, half * 1024:(half + 1) * 1024],
                     r=[pb.b, rrep.b], w=[tm.b])
                P.do("p", "tensor_add", y[:, oc, half * 1024:(half + 1) * 1024],
                     xres[:, oc % 2, half * 1024:(half + 1) * 1024], tm[:],
                     r=[xres.b, tm.b], w=[y.b])
        if A.get("dbg2"):
            P.do("q", "dma_start", A["dbg2"]["d_yall"].rearrange("(c p) t -> p c t", p=128), y[:],
                 r=[y.b], sem="Ds", amt=16)
        if A.get("dbg"):
            P.do("q", "dma_start", A["dbg"]["d_y"], y[:, 0, :], r=[y.b], sem="Ds", amt=16)
            P.do("q", "dma_start", A["dbg"]["d_rrep"], rrep[:, 0:2048], r=[rrep.b], sem="Ds", amt=16)
            P.do("q", "dma_start", A["dbg"]["d_yall"].rearrange("(c p) t -> p c t", p=128), y[:],
                 r=[y.b], sem="Ds", amt=16)
        P.sfree(x_bf)
        P.sfree(wbuf)
        P.sfree(brow)
        P.sfree(rho_row)
        P.sfree(rrep)
        P.sfree(xres)
        for t_ in tmp:
            P.sfree(t_)
        ln_feature_major(P, A, {"PA": PA, "PB": PB}, y,
                         with_store=A["out_ap"] if A.get("store_out") else None)
    return y


def emit_l1ffn(P, A):
    ins, c = A["in"], A["c"]
    NE = 16

    if A.get("stream") is None:
        x_fm = P.sbuf("x_fm", [128, ND, 2048], F32)
        P.do("q", "dma_start", x_fm[:], ins["xin"][:].rearrange("(c p) t -> p c t", p=128),
             w=[x_fm.b], sem="Dx", amt=16)
    else:
        x_fm = A["stream"]
    x_bf = P.sbuf("x_bf", [128, ND, 2048], BF16)
    cast_stream_to_bf(P, A, x_fm, x_bf)

    c["eye16"] = P.sbuf("eye16", [16, 2048], F32)
    P.do("q", "dma_start", c["eye16"][:], ins["c_eye16"][:], w=[c["eye16"].b], sem="Dx", amt=16)
    rw = P.sbuf("rw", [128, ND, 16], BF16)
    P.do("p", "dma_start", rw[:], ins["l1_rwT"][:].rearrange("(c p) e -> p c e", p=128),
         w=[rw.b], sem="Dw", amt=16)
    rbias = P.sbuf("rbias", [1, 16], F32)
    P.do("q", "dma_start", rbias[:], ins["l1_rbias"][:], w=[rbias.b], sem="Dx", amt=16)
    cT = P.sbuf("cT", [16, 2048], F32)
    sc = P.sbuf("sc", [128, NT, 16], F32)
    rbrep = P.sbuf("rbrep", [128, 16], F32)

    with ExitStack() as pst:
        PD = P.psum(pst, "pd", [128, 512], F32)
        PTf = P.psum(pst, "ptf", [128, 128], F32)
        P.mm_set(PD.b, [[(PD[:, 0:16], c["ones_f"][0:1, 0:128], rbias[0:1, 0:16],
                          (c["ones_f"].b, rbias.b))]])
        P.do("v", "tensor_copy", rbrep[:], PD[:, 0:16], r=[PD.b], w=[rbrep.b])
        negt = P.sbuf("negt", [128, 16], F32)
        P.do("v", "memset", negt[:], -1.0e9, w=[negt.b])
        ws = [dict(sch=P.sbuf(f"w_sch{i}", [128, 16], F32), gs=P.sbuf(f"w_gs{i}", [128, 4], F32),
                   t4=P.sbuf(f"w_t4{i}", [128, 4], F32), m2=P.sbuf(f"w_m2{i}", [128, 1], F32),
                   gm=P.sbuf(f"w_gm{i}", [128, 4], I32), mk=P.sbuf(f"w_mk{i}", [128, 16], F32),
                   t16=P.sbuf(f"w_t16{i}", [128, 16], F32), m1=P.sbuf(f"w_m1{i}", [128, 1], F32),
                   wv=P.sbuf(f"w_wv{i}", [128, 16], F32)) for i in range(2)]
        for tc in range(NT):
            W = ws[tc % 2]
            sch, gs, t4, m2, gm, mk, t16, m1, wv = (W["sch"], W["gs"], W["t4"], W["m2"],
                                                    W["gm"], W["mk"], W["t16"], W["m1"], W["wv"])
            g = [(PD[:, 0:16], x_bf[:, ic, tc * 128:(tc + 1) * 128], rw[:, ic, :],
                  (x_bf.b, rw.b)) for ic in range(ND)]
            P.mm_set(PD.b, [g])
            P.do("s", "activation", sc[:, tc, :], PD[:, 0:16], AF.Sigmoid, r=[PD.b], w=[sc.b])
            P.do("v", "tensor_add", sch[:], sc[:, tc, :], rbrep[:], r=[sc.b, rbrep.b], w=[sch.b])
            schg = sch[:].rearrange("p (g j) -> p g j", j=4)
            first = True
            for a in range(4):
                for bj in range(a + 1, 4):
                    P.do("v", "tensor_add", t4[:], schg[:, :, a], schg[:, :, bj],
                         r=[sch.b], w=[t4.b])
                    if first:
                        P.do("v", "tensor_copy", gs[:], t4[:], r=[t4.b], w=[gs.b])
                        first = False
                    else:
                        P.do("v", "tensor_max", gs[:], gs[:], t4[:], r=[t4.b, gs.b], w=[gs.b])
            first = True
            for a in range(4):
                for bj in range(a + 1, 4):
                    P.do("v", "tensor_tensor", t4[:, 0:1], gs[:, a:a + 1], gs[:, bj:bj + 1],
                         ALU.min, r=[gs.b], w=[t4.b])
                    if first:
                        P.do("v", "tensor_copy", m2[:], t4[:, 0:1], r=[t4.b], w=[m2.b])
                        first = False
                    else:
                        P.do("v", "tensor_max", m2[:], m2[:], t4[:, 0:1], r=[t4.b, m2.b], w=[m2.b])
            P.do("v", "tensor_scalar", gm[:], gs[:], m2[:], None, ALU.is_ge,
                 r=[gs.b, m2.b], w=[gm.b])
            mkg = mk[:].rearrange("p (g j) -> p g j", j=4)
            for j in range(4):
                P.do("v", "select", mkg[:, :, j], gm[:], schg[:, :, j], negt[:, 0:4],
                     r=[gm.b, sch.b, negt.b], w=[mk.b])
            P.do("v", "tensor_copy", t16[:], mk[:], r=[mk.b], w=[t16.b])
            for it in range(3):
                P.do("v", "reduce_max", m1[:], t16[:], axis=AX.X, r=[t16.b], w=[m1.b])
                P.do("v", "tensor_scalar", wv[:], t16[:], m1[:], None, ALU.is_ge,
                     r=[t16.b, m1.b], w=[wv.b])
                P.do("v", "scalar_tensor_tensor", t16[:], wv[:], NEG, t16[:], ALU.mult, ALU.add,
                     r=[wv.b, t16.b], w=[t16.b])
            P.do("v", "reduce_max", m1[:], t16[:], axis=AX.X, r=[t16.b], w=[m1.b])
            P.do("v", "tensor_scalar", wv[:], mk[:], m1[:], None, ALU.is_ge,
                 r=[mk.b, m1.b], w=[wv.b])
            P.do("v", "tensor_mul", wv[:], wv[:], sc[:, tc, :], r=[sc.b, wv.b], w=[wv.b])
            P.do("v", "reduce_sum", m1[:], wv[:], axis=AX.X, r=[wv.b], w=[m1.b])
            P.do("v", "tensor_scalar_add", m1[:], m1[:], 1.0e-20, r=[m1.b], w=[m1.b])
            P.do("v", "reciprocal", m2[:], m1[:], r=[m1.b], w=[m2.b])
            P.do("v", "tensor_scalar", wv[:], wv[:], m2[:], 2.5, ALU.mult, ALU.mult,
                 r=[wv.b, m2.b], w=[wv.b])
            P.do("t", "transpose", PTf[0:16, :], wv[:], c["ident_f"][:],
                 r=[wv.b, c["ident_f"].b], w=[PTf.b])
            P.do("v", "tensor_copy", cT[0:16, tc * 128:(tc + 1) * 128], PTf[0:16, :],
                 r=[PTf.b], w=[cT.b])
        for W in ws:
            for sb in W.values():
                P.sfree(sb)
        P.sfree(negt)
        P.sfree(sc)
        P.sfree(rw)
        P.sfree(rbias)
        P.sfree(rbrep)

    wgT = [P.sbuf(f"wgT{i}", [128, ND, 512], BF16) for i in range(2)]
    wuT = [P.sbuf(f"wuT{i}", [128, ND, 512], BF16) for i in range(2)]
    wdT = [P.sbuf(f"wdT{i}", [128, 4, 1024], BF16) for i in range(2)]
    g_bf = P.sbuf("g_bf", [128, 4, 2048], BF16)
    crep = P.sbuf("crep", [128, 2048], F32)
    tmp = [P.sbuf(f"dtmp{i}", [128, 1024], F32) for i in range(2)]

    with ExitStack() as pst:
        PU = [P.psum(pst, "pu0", [128, 1024], F32), P.psum(pst, "pu1", [128, 1024], F32)]
        PW = [P.psum(pst, "pw0", [128, 1024], F32), P.psum(pst, "pw1", [128, 1024], F32)]
        rot = 0
        for e in range(NE):
            wg_ = wgT[e % 2]
            wu_ = wuT[e % 2]
            wd_ = wdT[e % 2]
            P.do("p", "dma_start", wg_[:],
                 ins["l1_wgT"][e].rearrange("(c p) i -> p c i", p=128), w=[wg_.b], sem="Dw", amt=16)
            P.do("p", "dma_start", wu_[:],
                 ins["l1_wuT"][e].rearrange("(c p) i -> p c i", p=128), w=[wu_.b], sem="Dw", amt=16)
            P.do("p", "dma_start", wd_[:],
                 ins["l1_wdT"][e].rearrange("(c p) h -> p c h", p=128), w=[wd_.b], sem="Dw", amt=16)
            for half in range(2):
                pu = PU[half]
                groups = [[(pu[:, st * 512:(st + 1) * 512],
                            c["eye16"][0:16, e * 128:(e + 1) * 128],
                            cT[0:16, half * 1024 + st * 512:half * 1024 + (st + 1) * 512],
                            (c["eye16"].b, cT.b))] for st in range(2)]
                P.mm_set(pu.b, groups)
                P.do("v", "tensor_copy", crep[:, half * 1024:(half + 1) * 1024], pu[:],
                     r=[pu.b], w=[crep.b])
            for icc in range(4):
                for th in range(2):
                    sl = (icc, slice(th * 1024, (th + 1) * 1024))
                    for (wt, silu) in ((wg_, True), (wu_, False)):
                        pu = PU[rot % 2]
                        rot += 1
                        groups = []
                        for st in range(2):
                            lo = th * 1024 + st * 512
                            gr = [(pu[:, st * 512:(st + 1) * 512],
                                   wt[:, hcc, icc * 128:(icc + 1) * 128],
                                   x_bf[:, hcc, lo:lo + 512], (wt.b, x_bf.b))
                                  for hcc in range(ND)]
                            groups.append(gr)
                        P.mm_set(pu.b, groups)
                        if silu:
                            P.do("s", "activation", g_bf[:, sl[0], sl[1]],
                                 pu[:], AF.Silu, r=[pu.b], w=[g_bf.b])
                        else:
                            P.do("v", "tensor_mul", g_bf[:, sl[0], sl[1]],
                                 g_bf[:, sl[0], sl[1]], pu[:],
                                 r=[pu.b, g_bf.b], w=[g_bf.b])
            for hcc in range(ND):
                for th in range(2):
                    pw = PW[rot % 2]
                    tm = tmp[rot % 2]
                    rot += 1
                    groups = []
                    for st in range(2):
                        lo = th * 1024 + st * 512
                        gr = [(pw[:, st * 512:(st + 1) * 512],
                               wd_[:, icc, hcc * 128:(hcc + 1) * 128],
                               g_bf[:, icc, lo:lo + 512], (wd_.b, g_bf.b))
                              for icc in range(4)]
                        groups.append(gr)
                    P.mm_set(pw.b, groups)
                    P.do("v", "tensor_mul", tm[:], pw[:], crep[:, th * 1024:(th + 1) * 1024],
                         r=[pw.b, crep.b], w=[tm.b])
                    P.do("p", "tensor_add", x_fm[:, hcc, th * 1024:(th + 1) * 1024],
                         x_fm[:, hcc, th * 1024:(th + 1) * 1024], tm[:],
                         r=[tm.b, x_fm.b], w=[x_fm.b])
        for t_ in wgT + wuT + wdT:
            P.sfree(t_)
        P.sfree(g_bf)
        P.sfree(crep)
        P.sfree(c.pop("eye16"))
        P.sfree(cT)
        P.sfree(x_bf)
        for t_ in tmp:
            P.sfree(t_)
        ln_feature_major(P, A, {"PA": PU[0], "PB": PU[1]}, x_fm,
                         with_store=A["out_ap"] if A.get("store_out") else None)
    return x_fm


def emit_l0attn_with_wo(P, A):
    wo_sb = P.sbuf("wo_sb", [128, ND, 1024], BF16)
    A["_l0_wo_sb"] = wo_sb
    P.do("p", "dma_start", wo_sb[:],
         A["in"]["l0_woT"][:].rearrange("(c p) o -> p c o", p=128),
         w=[wo_sb.b], sem="Dw", amt=16)
    y = emit_l0attn(P, A)
    P.sfree(wo_sb)
    return y


_PHASE_INPUTS = {
    "l0attn": [("l0_wqT", [D, D]), ("l0_wkT", [D, D]), ("l0_wvpT", [D, D]),
               ("l0_woT", [D, D]), ("l0_gamma", [1, D]), ("l0_beta", [1, D])],
    "l0ffn": [("l0_wg", [D, 8]), ("l0_bg", [1, 8]), ("l0_w1", [8, D, 4096]),
              ("l0_b1", [8, 4096]), ("l0_w2", [8, 4096, D]), ("l0_b2", [8, D]),
              ("c_eye8", [8, 1024])],
    "l1attn": [("l1_wq", [D, D]), ("l1_bq", [1, D]), ("l1_wk", [D, D]), ("l1_bk", [1, D]),
               ("l1_wv", [D, D]), ("l1_bv", [1, D])],
    "l1ffn": [("l1_rwT", [D, 16]), ("l1_rbias", [1, 16]), ("l1_wgT", [16, D, 512]),
              ("l1_wuT", [16, D, 512]), ("l1_wdT", [16, 512, D]), ("c_eye16", [16, 2048])],
}

_PHASE_EMIT = {
    "l0attn": emit_l0attn_with_wo,
    "l0ffn": emit_l0ffn,
    "l1attn": emit_l1attn,
    "l1ffn": emit_l1ffn,
}

_PROG_CACHE = {}


def build_program(phases):
    key = tuple(phases)
    if key in _PROG_CACHE:
        return _PROG_CACHE[key]
    P = Prog()
    nc = P.nc
    ins = {}
    ins["xin"] = nc.dram_tensor("xin", [D, T], F32, kind="ExternalInput").ap()
    ins["c_ident"] = nc.dram_tensor("c_ident", [128, 128], F32, kind="ExternalInput").ap()
    ins["c_ones"] = nc.dram_tensor("c_ones", [1, 2048], F32, kind="ExternalInput").ap()
    for ph in phases:
        for nm, shp in _PHASE_INPUTS[ph]:
            if nm not in ins:
                ins[nm] = nc.dram_tensor(nm, list(shp), F32, kind="ExternalInput").ap()
    yout = nc.dram_tensor("yout", [D, T], F32, kind="ExternalOutput").ap()
    park = None
    if len(phases) > 1:
        park = nc.dram_tensor("park", [D, T], F32, kind="Internal").ap()

    A = {"in": ins, "out_ap": yout, "stream": None, "park": park, "parked": False}
    import os
    if os.environ.get("KDBG2") and phases == ["l1attn"]:
        A["dbg2"] = {"d_yall": nc.dram_tensor("d_yall", [D, T], F32, kind="ExternalOutput").ap()}
    if os.environ.get("KDBG") and phases == ["l1attn"]:
        A["dbg"] = {
            "d_rho": nc.dram_tensor("d_rho", [1, 2048], F32, kind="ExternalOutput").ap(),
            "d_ef": nc.dram_tensor("d_ef", [128, 2048], F32, kind="ExternalOutput").ap(),
            "d_qt": nc.dram_tensor("d_qt", [128, 2048], F32, kind="ExternalOutput").ap(),
            "d_kt": nc.dram_tensor("d_kt", [128, 2048], F32, kind="ExternalOutput").ap(),
            "d_y": nc.dram_tensor("d_y", [128, 2048], F32, kind="ExternalOutput").ap(),
            "d_rrep": nc.dram_tensor("d_rrep", [128, 2048], F32, kind="ExternalOutput").ap(),
            "d_yall": nc.dram_tensor("d_yall", [D, T], F32, kind="ExternalOutput").ap(),
        }
    load_consts(P, A)
    for i, ph in enumerate(phases):
        A["store_out"] = (i == len(phases) - 1)
        y = _PHASE_EMIT[ph](P, A)
        A["stream"] = y
        A["parked"] = False
        if i < len(phases) - 1:
            P.barrier()
    P.wait_tok("q", ("Ds", P.cnt["Ds"]))
    P.build()
    _PROG_CACHE[key] = P
    return P


def _prep_weights(i):
    perm = np.arange(D).reshape(HD, NH).T.reshape(-1)
    f = np.float32
    return {
        "l0_wqT": np.ascontiguousarray(i["L0_attn_Wq"].T, f),
        "l0_wkT": np.ascontiguousarray(i["L0_attn_Wk"].T, f),
        "l0_wvpT": np.ascontiguousarray(i["L0_attn_Wv"][perm].T, f),
        "l0_woT": np.ascontiguousarray(i["L0_attn_Wo"].T, f),
        "l0_gamma": np.ascontiguousarray(i["L0_attn_gamma"].reshape(1, D), f),
        "l0_beta": np.ascontiguousarray(i["L0_attn_beta"].reshape(1, D), f),
        "l0_wg": np.ascontiguousarray(i["L0_ffn_Wg"], f),
        "l0_bg": np.ascontiguousarray(i["L0_ffn_bg"].reshape(1, 8), f),
        "l0_w1": np.ascontiguousarray(i["L0_ffn_W1"], f),
        "l0_b1": np.ascontiguousarray(i["L0_ffn_b1"], f),
        "l0_w2": np.ascontiguousarray(i["L0_ffn_W2"], f),
        "l0_b2": np.ascontiguousarray(i["L0_ffn_b2"], f),
        "l1_wq": np.ascontiguousarray(i["L1_attn_Wq"], f),
        "l1_bq": np.ascontiguousarray(i["L1_attn_bq"].reshape(1, D), f),
        "l1_wk": np.ascontiguousarray(i["L1_attn_Wk"], f),
        "l1_bk": np.ascontiguousarray(i["L1_attn_bk"].reshape(1, D), f),
        "l1_wv": np.ascontiguousarray(i["L1_attn_Wv"], f),
        "l1_bv": np.ascontiguousarray(i["L1_attn_bv"].reshape(1, D), f),
        "l1_rwT": np.ascontiguousarray(i["L1_ffn_router_weight"].T, f),
        "l1_rbias": np.ascontiguousarray(i["L1_ffn_router_bias"].reshape(1, 16), f),
        "l1_wgT": np.ascontiguousarray(i["L1_ffn_w_gate"].transpose(0, 2, 1), f),
        "l1_wuT": np.ascontiguousarray(i["L1_ffn_w_up"].transpose(0, 2, 1), f),
        "l1_wdT": np.ascontiguousarray(i["L1_ffn_w_down"].transpose(0, 2, 1), f),
    }


def _consts():
    eye8 = np.zeros((8, 1024), np.float32)
    for k in range(8):
        eye8[k, k * 128:(k + 1) * 128] = 1.0
    eye16 = np.zeros((16, 2048), np.float32)
    for k in range(16):
        eye16[k, k * 128:(k + 1) * 128] = 1.0
    return {
        "c_ident": np.eye(128, dtype=np.float32),
        "c_ones": np.ones((1, 2048), dtype=np.float32),
        "c_eye8": eye8,
        "c_eye16": eye16,
    }


_NEEDED = {
    "l0attn": ["xin", "c_ident", "c_ones", "l0_wqT", "l0_wkT", "l0_wvpT", "l0_woT",
               "l0_gamma", "l0_beta"],
    "l0ffn": ["xin", "c_ident", "c_ones", "l0_wg", "l0_bg", "l0_w1", "l0_b1",
              "l0_w2", "l0_b2", "c_eye8"],
    "l1attn": ["xin", "c_ident", "c_ones", "l1_wq", "l1_bq", "l1_wk", "l1_bk",
               "l1_wv", "l1_bv"],
    "l1ffn": ["xin", "c_ident", "c_ones", "l1_rwT", "l1_rbias", "l1_wgT",
              "l1_wuT", "l1_wdT", "c_eye16"],
}

PHASE_ORDER = ["l0attn", "l0ffn", "l1attn", "l1ffn"]
FUSED = True


def run_phase(ph, streams, shared):
    P = build_program([ph])
    names = set(_NEEDED[ph])
    in_maps = []
    for b in range(NCORES):
        m = {k: v for k, v in shared.items() if k in names}
        m["xin"] = streams[b]
        in_maps.append(m)
    res = run_bass_kernel_spmd(P.nc, in_maps, core_ids=list(range(NCORES)))
    return [r["yout"] for r in res.results]


def kernel(**inputs):
    shared = _prep_weights(inputs)
    shared.update(_consts())
    x = inputs["L0_attn_x"]
    streams = [np.ascontiguousarray(x[b].T) for b in range(NCORES)]
    if FUSED:
        P = build_program(PHASE_ORDER)
        names = set(n for ph in PHASE_ORDER for n in _NEEDED[ph])
        in_maps = []
        for b in range(NCORES):
            m = {k: v for k, v in shared.items() if k in names}
            m["xin"] = streams[b]
            in_maps.append(m)
        res = run_bass_kernel_spmd(P.nc, in_maps, core_ids=list(range(NCORES)))
        streams = [r["yout"] for r in res.results]
    else:
        for ph in PHASE_ORDER:
            streams = run_phase(ph, streams, shared)
    return np.stack([s.T for s in streams]).astype(np.float32)


# version 05
# speedup vs baseline: 1.0693x; 1.0314x over previous
import numpy as np
import ml_dtypes
from contextlib import ExitStack

import concourse.bass as bass
import concourse.mybir as mybir
from concourse.bass_utils import run_bass_kernel_spmd

F32 = mybir.dt.float32
BF16 = mybir.dt.bfloat16
I32 = mybir.dt.int32
AF = mybir.ActivationFunctionType
ALU = mybir.AluOpType
AX = mybir.AxisListType

NCORES = 8
T = 2048
D = 1024
NT = T // 128
ND = D // 128
NH = 16
HD = 64
NEG = -1.0e30
LN_EPS = 1.0e-5

SBUF_BASE = 16512
SBUF_TOP = 229344


class Buf:
    __slots__ = ("w", "rs", "nm")

    def __init__(self, nm=""):
        self.w = None
        self.rs = []
        self.nm = nm


class SB:
    __slots__ = ("h", "b", "off", "sz")

    def __init__(self, h, b, off=None, sz=None):
        self.h = h
        self.b = b
        self.off = off
        self.sz = sz

    def __getitem__(self, idx):
        return self.h[idx]


def RD(*bufs):
    out = []
    for b in bufs:
        if b is not None and b.w is not None:
            out.append(b.w)
    return out


def WR(*bufs):
    out = []
    for b in bufs:
        if b is None:
            continue
        if b.w is not None:
            out.append(b.w)
        out.extend(t for t in b.rs if t is not None)
    return out


class _Alloc:
    def __init__(self, base, top):
        self.free = [(base, top - base)]
        self.pending = []

    def alloc(self, size, align=128):
        size = -(-size // align) * align
        for i, (o, s) in enumerate(self.free):
            oo = -(-o // align) * align
            pad = oo - o
            if s >= size + pad:
                del self.free[i]
                if pad:
                    self.free.insert(i, (o, pad))
                rest = s - pad - size
                if rest:
                    self.free.append((oo + size, rest))
                self.free.sort()
                toks = []
                keep = []
                for (po, ps, pt) in self.pending:
                    if po < oo + size and oo < po + ps:
                        toks.extend(pt)
                        if po < oo:
                            keep.append((po, oo - po, pt))
                        if po + ps > oo + size:
                            keep.append((oo + size, po + ps - (oo + size), pt))
                    else:
                        keep.append((po, ps, pt))
                self.pending = keep
                return oo, size, toks
        raise MemoryError(f"SBUF alloc of {size} failed; free={self.free}")

    def free_(self, off, size, tokens):
        self.free.append((off, size))
        self.free.sort()
        merged = []
        for o, s in self.free:
            if merged and merged[-1][0] + merged[-1][1] == o:
                merged[-1] = (merged[-1][0], merged[-1][1] + s)
            else:
                merged.append((o, s))
        self.free = [tuple(x) for x in merged]
        if tokens:
            self.pending.append((off, size, list(tokens)))


_DTS = {F32: 4, BF16: 2, I32: 4}


class Prog:
    ENGS = ("t", "v", "s", "p", "q")

    def __init__(self, name="prog"):
        self.nc = bass.Bass("TRN2", target_bir_lowering=False)
        self.q = {k: [] for k in self.ENGS}
        self.cnt = {}
        self.sems = {}
        self.sat = {k: {} for k in self.ENGS}
        self._stack = ExitStack()
        for k in self.ENGS:
            self._mksem("E" + k)
        for nm in ("Dw", "Dx", "Ds"):
            self._mksem(nm)
        self._mksem("bar")
        self.al = _Alloc(SBUF_BASE, SBUF_TOP)
        self._uid = 0

    def _mksem(self, name):
        self.sems[name] = self._stack.enter_context(self.nc.semaphore(name))
        self.cnt[name] = 0

    def uid(self):
        self._uid += 1
        return self._uid

    def op(self, eng, fname, *args, deps=(), sem=None, amt=1, inc=True, **kw):
        sat = self.sat[eng]
        need = {}
        for d in deps:
            if d is None:
                continue
            s, v = d
            if need.get(s, 0) < v:
                need[s] = v
        waits = []
        for s, v in need.items():
            if sat.get(s, 0) < v:
                waits.append((s, v))
                sat[s] = v
        if eng == "s" and fname == "tensor_copy":
            fname = "copy"
        if sem is None and inc:
            sem = "E" + eng
            amt = 1
        tok = None
        semh = None
        if sem is not None and inc:
            self.cnt[sem] += amt
            tok = (sem, self.cnt[sem])
            semh = self.sems[sem]
        sems = self.sems

        def emit(E, waits=tuple(waits), fname=fname, args=args, kw=kw, semh=semh, amt=amt):
            for s, v in waits:
                E.wait_ge(sems[s], v)
            ins = getattr(E, fname)(*args, **kw)
            if semh is not None:
                ins.then_inc(semh, amt)

        self.q[eng].append(emit)
        return tok

    def do(self, eng, fname, *args, r=(), w=(), extra=(), sem=None, amt=1, **kw):
        deps = RD(*r) + WR(*w) + list(extra)
        tok = self.op(eng, fname, *args, deps=deps, sem=sem, amt=amt, **kw)
        for b in r:
            if b is not None:
                b.rs.append(tok)
        for b in w:
            if b is not None:
                b.w = tok
                b.rs = []
        return tok

    def wait_tok(self, eng, tok):
        if tok is None:
            return
        s, v = tok
        if self.sat[eng].get(s, 0) < v:
            self.sat[eng][s] = v
            sems = self.sems

            def emit(E, s=s, v=v):
                E.wait_ge(sems[s], v)

            self.q[eng].append(emit)

    def mm_set(self, pbuf, groups, extra=()):
        first_deps = WR(pbuf) + list(extra)
        rbufs_all = []
        total = sum(len(g) for g in groups)
        n = 0
        tok = None
        for g in groups:
            L = len(g)
            for i, (pap, lh, rh, rb) in enumerate(g):
                n += 1
                deps = RD(*rb)
                if n == 1:
                    deps = deps + first_deps
                tok = self.op("t", "matmul", pap, lh, rh,
                              start=(i == 0), stop=(i == L - 1),
                              deps=deps, inc=(n == total))
                for b in rb:
                    if b is not None and b not in rbufs_all:
                        rbufs_all.append(b)
        pbuf.w = tok
        pbuf.rs = []
        for b in rbufs_all:
            b.rs.append(tok)
        return tok

    def sbuf(self, name, shape, dtype):
        bpp = int(np.prod(shape[1:])) * _DTS[dtype]
        off, sz, toks = self.al.alloc(bpp)
        h = self.nc.alloc_sbuf_tensor_at(f"{name}_{self.uid()}", list(shape), dtype, offset=off)
        b = Buf(name)
        b.rs = list(toks)
        return SB(h, b, off, sz)

    def sfree(self, sb):
        toks = ([sb.b.w] if sb.b.w else []) + sb.b.rs
        self.al.free_(sb.off, sb.sz, [t for t in toks if t])

    def psum(self, stack, name, shape, dtype=F32):
        h = stack.enter_context(self.nc.psum_tensor(f"{name}_{self.uid()}", list(shape), dtype))
        b = Buf(name)
        if not hasattr(self, "_psum_sbs"):
            self._psum_sbs = []
        toks = []
        for old in self._psum_sbs:
            if old.b.w is not None:
                toks.append(old.b.w)
            toks.extend(t for t in old.b.rs if t is not None)
        b.rs = toks
        sb = SB(h, b)
        self._psum_sbs.append(sb)
        return sb

    def barrier(self):
        self.cnt["bar"] += 5
        target = self.cnt["bar"]
        sems = self.sems
        for k in self.ENGS:
            def em(E, t=target):
                E.sem_inc(sems["bar"], 1)
                E.wait_ge(sems["bar"], t)
            self.q[k].append(em)
        for k in self.ENGS:
            for s in ("Et", "Ev", "Es", "Ep"):
                if self.sat[k].get(s, 0) < self.cnt[s]:
                    self.sat[k][s] = self.cnt[s]

    def build(self):
        with self.nc.Block() as blk:
            blk.tensor(lambda E: [f(E) for f in self.q["t"]])
            blk.vector(lambda E: [f(E) for f in self.q["v"]])
            blk.scalar(lambda E: [f(E) for f in self.q["s"]])
            blk.gpsimd(lambda E: [f(E) for f in self.q["p"]])
            blk.sync(lambda E: [f(E) for f in self.q["q"]])
        self._stack.close()
        return self.nc


def load_consts(P, A):
    c = {}
    c["ident_bf"] = P.sbuf("ident_bf", [128, 128], BF16)
    c["ident_f"] = P.sbuf("ident_f", [128, 128], F32)
    c["ones_bf"] = P.sbuf("ones_bf", [1, 512], BF16)
    c["ones_f"] = P.sbuf("ones_f", [1, 128], F32)
    c["ones_col"] = P.sbuf("ones_col", [128, 1], F32)
    ins = A["in"]
    P.do("p", "dma_start", c["ident_bf"][:], ins["c_ident"][:], w=[c["ident_bf"].b], sem="Dw", amt=16)
    P.do("q", "dma_start", c["ident_f"][:], ins["c_ident"][:], w=[c["ident_f"].b], sem="Dx", amt=16)
    P.do("p", "dma_start", c["ones_bf"][:], ins["c_ones"][0:1, 0:512], w=[c["ones_bf"].b], sem="Dw", amt=16)
    P.do("q", "dma_start", c["ones_f"][:], ins["c_ones"][0:1, 0:128], w=[c["ones_f"].b], sem="Dx", amt=16)
    P.do("q", "dma_start", c["ones_col"][:], ins["c_ones"][0:1, 0:128].rearrange("o p -> p o"),
         w=[c["ones_col"].b], sem="Dx", amt=16)
    c["am_add"] = P.sbuf("am_add", [128, 128], F32)
    c["am_mul"] = P.sbuf("am_mul", [128, 128], F32)
    P.do("p", "memset", c["am_add"][:], 0.0, w=[c["am_add"].b])
    P.do("p", "affine_select", c["am_add"][:], c["am_add"][:], [[-1, 128]], ALU.is_ge, NEG,
         base=0, channel_multiplier=1, r=[c["am_add"].b], w=[c["am_add"].b])
    P.do("p", "memset", c["am_mul"][:], 1.0, w=[c["am_mul"].b])
    P.do("p", "affine_select", c["am_mul"][:], c["am_mul"][:], [[-1, 128]], ALU.is_ge, 0.0,
         base=0, channel_multiplier=1, r=[c["am_mul"].b], w=[c["am_mul"].b])
    A["c"] = c


def ln_feature_major(P, A, ps, y, with_store=None):
    c = A["c"]
    PA, PB = ps["PA"], ps["PB"]
    sq = P.sbuf("ln_sq", [128, ND, 1024], F32)
    s1 = P.sbuf("ln_s1", [1, 2048], F32)
    s2 = P.sbuf("ln_s2", [1, 2048], F32)
    arow = P.sbuf("ln_ar", [1, 2048], F32)
    brow = P.sbuf("ln_br", [1, 2048], F32)
    arep = P.sbuf("ln_arep", [128, 2048], F32)
    brep = P.sbuf("ln_brep", [128, 2048], F32)

    for half in range(2):
        groups = []
        for st in range(2):
            lo = half * 1024 + st * 512
            g = [(PA[0:1, st * 512:(st + 1) * 512], c["ones_col"][:], y[:, cch, lo:lo + 512],
                  (c["ones_col"].b, y.b)) for cch in range(ND)]
            groups.append(g)
        P.mm_set(PA.b, groups)
        P.do("v", "tensor_copy", s1[0:1, half * 1024:(half + 1) * 1024], PA[0:1, :],
             r=[PA.b], w=[s1.b])
        for cch in range(ND):
            P.do("s", "activation", sq[:, cch, :], y[:, cch, half * 1024:(half + 1) * 1024],
                 AF.Square, r=[y.b], w=[sq.b])
        groups = []
        for st in range(2):
            g = [(PB[0:1, st * 512:(st + 1) * 512], c["ones_col"][:],
                  sq[:, cch, st * 512:(st + 1) * 512],
                  (c["ones_col"].b, sq.b)) for cch in range(ND)]
            groups.append(g)
        P.mm_set(PB.b, groups)
        P.do("v", "tensor_copy", s2[0:1, half * 1024:(half + 1) * 1024], PB[0:1, :],
             r=[PB.b], w=[s2.b])
    mu = P.sbuf("ln_mu", [1, 2048], F32)
    P.do("v", "tensor_scalar_mul", mu[:], s1[:], 1.0 / D, r=[s1.b], w=[mu.b])
    P.do("v", "tensor_scalar_mul", s2[:], s2[:], 1.0 / D, r=[s2.b], w=[s2.b])
    P.do("v", "tensor_mul", arow[:], mu[:], mu[:], r=[mu.b], w=[arow.b])
    P.do("v", "tensor_sub", s2[:], s2[:], arow[:], r=[s2.b, arow.b], w=[s2.b])
    P.do("v", "tensor_scalar_add", s2[:], s2[:], LN_EPS, r=[s2.b], w=[s2.b])
    P.do("s", "activation", s2[:], s2[:], AF.Ln, r=[s2.b], w=[s2.b])
    P.do("s", "activation", arow[:], s2[:], AF.Exp, scale=-0.5, r=[s2.b], w=[arow.b])
    P.do("v", "tensor_mul", brow[:], mu[:], arow[:], r=[mu.b, arow.b], w=[brow.b])
    P.do("v", "tensor_scalar_mul", brow[:], brow[:], -1.0, r=[brow.b], w=[brow.b])
    for (row, rep) in ((arow, arep), (brow, brep)):
        for half in range(2):
            pb = PA if half == 0 else PB
            groups = []
            for st in range(2):
                lo = half * 1024 + st * 512
                groups.append([(pb[:, st * 512:(st + 1) * 512], c["ones_f"][0:1, 0:128],
                                row[0:1, lo:lo + 512], (c["ones_f"].b, row.b))])
            P.mm_set(pb.b, groups)
            P.do("v" if half == 0 else "s", "tensor_copy", rep[:, half * 1024:(half + 1) * 1024],
                 pb[:], r=[pb.b], w=[rep.b])
    for cch in range(ND):
        P.do("v", "tensor_mul", y[:, cch, :], y[:, cch, :], arep[:], r=[arep.b, y.b], w=[y.b])
        P.do("p", "tensor_add", y[:, cch, :], y[:, cch, :], brep[:], r=[brep.b, y.b], w=[y.b])
        if with_store is not None:
            P.do("q", "dma_start", with_store.rearrange("(c p) t -> p c t", p=128)[:, cch, :],
                 y[:, cch, :], r=[y.b], sem="Ds", amt=16)
    for t_ in (sq, s1, s2, arow, brow, arep, brep, mu):
        P.sfree(t_)


def cast_stream_to_bf(P, A, x_fm, x_bf):
    for cch in range(ND):
        eng = ("v", "s", "p")[cch % 3]
        P.do(eng, "tensor_copy", x_bf[:, cch, :], x_fm[:, cch, :], r=[x_fm.b], w=[x_bf.b])


def emit_l0attn(P, A):
    ins = A["in"]
    c = A["c"]
    nc = P.nc

    x_bf = P.sbuf("x_bf", [128, ND, 2048], BF16)
    if A.get("stream") is None:
        P.do("p", "dma_start", x_bf[:],
             ins["xin"][:].rearrange("(c p) t -> p c t", p=128),
             w=[x_bf.b], sem="Dw", amt=16)
    else:
        cast_stream_to_bf(P, A, A["stream"], x_bf)
        park_stream(P, A)

    g_rep = P.sbuf("g_rep", [128, 1024], F32)
    b_rep = P.sbuf("b_rep", [128, 1024], F32)
    grow = P.sbuf("grow", [1, 1024], F32)
    brow = P.sbuf("brow", [1, 1024], F32)
    P.do("q", "dma_start", grow[:], ins["l0_gamma"][:], w=[grow.b], sem="Dx", amt=16)
    P.do("q", "dma_start", brow[:], ins["l0_beta"][:], w=[brow.b], sem="Dx", amt=16)

    with ExitStack() as pst:
        PRA = P.psum(pst, "pra", [128, 1024], F32)
        PRB = P.psum(pst, "prb", [128, 1024], F32)
        for (row, rep, pb) in ((grow, g_rep, PRA), (brow, b_rep, PRB)):
            groups = [[(pb[:, st * 512:(st + 1) * 512], c["ones_f"][0:1, 0:128],
                        row[0:1, st * 512:(st + 1) * 512], (c["ones_f"].b, row.b))]
                      for st in range(2)]
            P.mm_set(pb.b, groups)
            P.do("v", "tensor_copy", rep[:], pb[:], r=[pb.b], w=[rep.b])

        qT = P.sbuf("qT", [128, ND, 2048], BF16)
        kT = P.sbuf("kT", [128, ND, 2048], BF16)
        vtm = P.sbuf("vtm", [128, NT, 1024], BF16)
        wbuf = P.sbuf("wbuf", [128, ND, 1024], BF16)

        def load_w(name):
            return P.do("p", "dma_start", wbuf[:],
                        ins[name][:].rearrange("(c p) o -> p c o", p=128),
                        w=[wbuf.b], sem="Dw", amt=16)

        for wname, dst, scale in (("l0_wqT", qT, None), ("l0_wkT", kT, 0.125)):
            load_w(wname)
            for oc in range(ND):
                for half in range(2):
                    pb = PRA if (oc * 2 + half) % 2 == 0 else PRB
                    groups = []
                    for st in range(2):
                        lo = half * 1024 + st * 512
                        groups.append([(pb[:, st * 512:(st + 1) * 512],
                                        wbuf[:, ic, oc * 128:(oc + 1) * 128],
                                        x_bf[:, ic, lo:lo + 512],
                                        (wbuf.b, x_bf.b)) for ic in range(ND)])
                    P.mm_set(pb.b, groups)
                    eng = "s" if (oc + half) % 2 == 0 else "v"
                    if scale is None:
                        P.do(eng, "tensor_copy", dst[:, oc, half * 1024:(half + 1) * 1024],
                             pb[:], r=[pb.b], w=[dst.b])
                    else:
                        P.do("s", "activation", dst[:, oc, half * 1024:(half + 1) * 1024],
                             pb[:], AF.Identity, scale=scale, r=[pb.b], w=[dst.b])
        load_w("l0_wvpT")
        for tc in range(NT):
            pb = PRA if tc % 2 == 0 else PRB
            groups = []
            for st in range(2):
                groups.append([(pb[:, st * 512:(st + 1) * 512],
                                x_bf[:, ic, tc * 128:(tc + 1) * 128],
                                wbuf[:, ic, st * 512:(st + 1) * 512],
                                (wbuf.b, x_bf.b)) for ic in range(ND)])
            P.mm_set(pb.b, groups)
            eng = "s" if tc % 2 == 0 else "v"
            P.do(eng, "tensor_copy", vtm[:, tc, :], pb[:], r=[pb.b], w=[vtm.b])
    P.sfree(wbuf)
    P.sfree(grow)
    P.sfree(brow)
    P.sfree(x_bf)

    qkv = P.sbuf("qkv", [128, NT, 1024], BF16)
    iter_bufs = []
    for par in range(3):
        iter_bufs.append(dict(
            Ef=P.sbuf(f"Ef{par}", [128, 2048], F32),
            Eb=P.sbuf(f"Eb{par}", [128, 2048], BF16),
            ET=P.sbuf(f"ET{par}", [128, NT, 128], BF16),
            negm=P.sbuf(f"negm{par}", [128, 1], F32),
            den=P.sbuf(f"den{par}", [128, 1], F32),
            rd=P.sbuf(f"rd{par}", [128, 1], F32),
        ))

    with ExitStack() as pst:
        PS = [P.psum(pst, f"ps{i}", [128, 512], F32) for i in range(4)]
        PT = [P.psum(pst, "pt0", [128, 128], BF16), P.psum(pst, "pt1", [128, 128], BF16)]
        PAV = [P.psum(pst, "pav0", [128, 64], F32), P.psum(pst, "pav1", [128, 64], F32)]

        iters = [(h, qt) for h in range(NH) for qt in range(NT)]

        def stageA(i):
            h, qt = iters[i]
            hc, ho = h // 2, (h % 2) * 64
            B = iter_bufs[i % 3]
            Ef, Eb = B["Ef"], B["Eb"]
            negm, den, rd = B["negm"], B["den"], B["rd"]
            Pkeys = (qt + 1) * 128
            for st in range(-(-Pkeys // 512)):
                w_ = min(512, Pkeys - st * 512)
                pb = PS[2 * (i % 2) + st % 2]
                P.mm_set(pb.b, [[(pb[:, 0:w_],
                                  qT[ho:ho + 64, hc, qt * 128:(qt + 1) * 128],
                                  kT[ho:ho + 64, hc, st * 512:st * 512 + w_],
                                  (qT.b, kT.b))]])
                P.do("s", "tensor_copy", Ef[:, st * 512:st * 512 + w_], pb[:, 0:w_],
                     r=[pb.b], w=[Ef.b])
            P.do("p", "tensor_add", Ef[:, qt * 128:(qt + 1) * 128],
                 Ef[:, qt * 128:(qt + 1) * 128], c["am_add"][:],
                 r=[Ef.b, c["am_add"].b], w=[Ef.b])
            P.do("v", "reduce_max", negm[:], Ef[:, 0:Pkeys], axis=AX.X, negate=True,
                 r=[Ef.b], w=[negm.b])
            P.do("s", "activation", Eb[:, 0:Pkeys], Ef[:, 0:Pkeys], AF.Exp,
                 bias=negm[:], accum_out=den[:], r=[Ef.b, negm.b], w=[Eb.b, den.b])
            P.do("v", "reciprocal", rd[:], den[:], r=[den.b], w=[rd.b])

        def stageT(i):
            h, qt = iters[i]
            B = iter_bufs[i % 3]
            Eb, ET = B["Eb"], B["ET"]
            for sc in range(qt + 1):
                pt = PT[sc % 2]
                P.do("t", "transpose", pt[:], Eb[:, sc * 128:(sc + 1) * 128],
                     c["ident_bf"][:], r=[Eb.b, c["ident_bf"].b], w=[pt.b])
                eng = "v" if sc % 2 == 0 else "s"
                P.do(eng, "tensor_copy", ET[:, sc, :], pt[:], r=[pt.b], w=[ET.b])

        def stageV(i):
            h, qt = iters[i]
            B = iter_bufs[i % 3]
            ET, rd = B["ET"], B["rd"]
            pav = PAV[i % 2]
            P.mm_set(pav.b, [[(pav[:], ET[:, sc, :], vtm[:, sc, h * 64:h * 64 + 64],
                               (ET.b, vtm.b)) for sc in range(qt + 1)]])
            P.do("s", "activation", qkv[:, qt, h * 64:h * 64 + 64], pav[:],
                 AF.Identity, scale=rd[:], r=[pav.b, rd.b], w=[qkv.b])

        n = len(iters)
        import os
        if os.environ.get("KNOSKEW"):
            for i in range(n):
                stageA(i)
                stageT(i)
                stageV(i)
        else:
            stageA(0)
            if n > 1:
                stageA(1)
            stageT(0)
            for i in range(n):
                if i + 2 < n:
                    stageA(i + 2)
                if i + 1 < n:
                    stageT(i + 1)
                stageV(i)

    import os as _os
    if _os.environ.get("KDBG5"):
        dv = P.nc.dram_tensor("d_vtm", [128, NT * 1024], F32, kind="ExternalOutput").ap()
        vf = P.sbuf("vtmf", [128, 768], F32)
        for tcx in range(NT):
            for hf in range(2):
                P.do("v", "tensor_copy", vf[:, 0:512], vtm[:, tcx, hf * 512:(hf + 1) * 512],
                     r=[vtm.b], w=[vf.b])
                P.do("q", "dma_start", dv[:, tcx * 1024 + hf * 512:tcx * 1024 + (hf + 1) * 512],
                     vf[:, 0:512], r=[vf.b], sem="Ds", amt=16)
        P.sfree(vf)
    P.sfree(qT)
    P.sfree(kT)
    P.sfree(vtm)
    for B in iter_bufs:
        for sb in B.values():
            P.sfree(sb)

    dump = P.sbuf("dump", [128, 1024], BF16)
    tmp = P.sbuf("mlntmp", [128, 1024], F32)
    rows = [dict(s1=P.sbuf(f"m_s1{i}", [128, 1], F32), s2=P.sbuf(f"m_s2{i}", [128, 1], F32),
                 rstd=P.sbuf(f"m_r{i}", [128, 1], F32), nmr=P.sbuf(f"m_n{i}", [128, 1], F32))
            for i in range(2)]
    for tc in range(NT):
        R = rows[tc % 2]
        s1, s2, rstd, nmr = R["s1"], R["s2"], R["rstd"], R["nmr"]
        P.do("v", "reduce_sum", s1[:], qkv[:, tc, :], axis=AX.X, r=[qkv.b], w=[s1.b])
        P.do("s", "activation", dump[:], qkv[:, tc, :], AF.Square, accum_out=s2[:],
             r=[qkv.b], w=[dump.b, s2.b])
        P.do("v", "tensor_scalar_mul", s1[:], s1[:], 1.0 / D, r=[s1.b], w=[s1.b])
        P.do("v", "tensor_scalar_mul", s2[:], s2[:], 1.0 / D, r=[s2.b], w=[s2.b])
        P.do("v", "tensor_mul", rstd[:], s1[:], s1[:], r=[s1.b], w=[rstd.b])
        P.do("v", "tensor_sub", s2[:], s2[:], rstd[:], r=[s2.b, rstd.b], w=[s2.b])
        P.do("v", "tensor_scalar_add", s2[:], s2[:], LN_EPS, r=[s2.b], w=[s2.b])
        P.do("s", "activation", s2[:], s2[:], AF.Ln, r=[s2.b], w=[s2.b])
        P.do("s", "activation", rstd[:], s2[:], AF.Exp, scale=-0.5, r=[s2.b], w=[rstd.b])
        P.do("v", "tensor_mul", nmr[:], s1[:], rstd[:], r=[s1.b, rstd.b], w=[nmr.b])
        P.do("v", "tensor_scalar_mul", nmr[:], nmr[:], -1.0, r=[nmr.b], w=[nmr.b])
        P.do("s", "activation", tmp[:], qkv[:, tc, :], AF.Identity, scale=rstd[:],
             bias=nmr[:], r=[qkv.b, rstd.b, nmr.b], w=[tmp.b])
        P.do("v", "tensor_mul", tmp[:], tmp[:], g_rep[:], r=[g_rep.b, tmp.b], w=[tmp.b])
        P.do("p", "tensor_add", qkv[:, tc, :], tmp[:], b_rep[:], r=[tmp.b, b_rep.b], w=[qkv.b])

    P.sfree(dump)
    P.sfree(tmp)
    for R in rows:
        for sb in R.values():
            P.sfree(sb)
    P.sfree(g_rep)
    P.sfree(b_rep)

    lnT = P.sbuf("lnT", [128, ND, 2048], BF16)
    with ExitStack() as pst:
        PT = [P.psum(pst, "pt0", [128, 128], BF16), P.psum(pst, "pt1", [128, 128], BF16)]
        for tc in range(NT):
            for oc in range(ND):
                pt = PT[(tc * ND + oc) % 2]
                P.do("t", "transpose", pt[:], qkv[:, tc, oc * 128:(oc + 1) * 128],
                     c["ident_bf"][:], r=[qkv.b, c["ident_bf"].b], w=[pt.b])
                eng = "v" if oc % 2 == 0 else "s"
                P.do(eng, "tensor_copy", lnT[:, oc, tc * 128:(tc + 1) * 128], pt[:],
                     r=[pt.b], w=[lnT.b])
    P.sfree(qkv)

    y = P.sbuf("y_fm", [128, ND, 2048], F32)
    xres = P.sbuf("xres", [128, 2, 2048], F32)
    with ExitStack() as pst:
        PA = P.psum(pst, "pa", [128, 1024], F32)
        PB = P.psum(pst, "pb", [128, 1024], F32)
        wbuf = A["_l0_wo_sb"]
        for oc in range(ND):
            if A.get("parked"):
                src = A["park"][:].rearrange("(c p) t -> p c t", p=128)[:, oc, :]
                P.do("q", "dma_start", xres[:, oc % 2, :], src, r=[A["park_buf"]],
                     w=[xres.b], sem="Dx", amt=16)
            else:
                src = A["in"]["xin"][:].rearrange("(c p) t -> p c t", p=128)[:, oc, :]
                P.do("q", "dma_start", xres[:, oc % 2, :], src, w=[xres.b], sem="Dx", amt=16)
            for half in range(2):
                pb = PA if (oc * 2 + half) % 2 == 0 else PB
                groups = []
                for st in range(2):
                    lo = half * 1024 + st * 512
                    groups.append([(pb[:, st * 512:(st + 1) * 512],
                                    wbuf[:, ic, oc * 128:(oc + 1) * 128],
                                    lnT[:, ic, lo:lo + 512],
                                    (wbuf.b, lnT.b)) for ic in range(ND)])
                P.mm_set(pb.b, groups)
                P.do("v", "tensor_add", y[:, oc, half * 1024:(half + 1) * 1024],
                     pb[:], xres[:, oc % 2, half * 1024:(half + 1) * 1024],
                     r=[pb.b, xres.b], w=[y.b])
        P.sfree(lnT)
        P.sfree(xres)
        ln_feature_major(P, A, {"PA": PA, "PB": PB}, y,
                         with_store=A["out_ap"] if A.get("store_out") else None)
    return y


def park_stream(P, A):
    x = A["stream"]
    pb = A.setdefault("park_buf", Buf("park"))
    P.do("q", "dma_start", A["park"][:].rearrange("(c p) t -> p c t", p=128), x[:],
         r=[x.b], w=[pb], sem="Ds", amt=16)
    P.sfree(x)
    A["stream"] = None
    A["parked"] = True


def emit_l0ffn(P, A):
    ins, c = A["in"], A["c"]

    if A.get("stream") is None:
        x_fm = P.sbuf("x_fm", [128, ND, 2048], F32)
        P.do("q", "dma_start", x_fm[:], ins["xin"][:].rearrange("(c p) t -> p c t", p=128),
             w=[x_fm.b], sem="Dx", amt=16)
    else:
        x_fm = A["stream"]
    x_bf = P.sbuf("x_bf", [128, ND, 2048], BF16)
    cast_stream_to_bf(P, A, x_fm, x_bf)

    c["eye8"] = P.sbuf("eye8", [8, 1024], F32)
    P.do("q", "dma_start", c["eye8"][:], ins["c_eye8"][:], w=[c["eye8"].b], sem="Dx", amt=16)
    wg = P.sbuf("wg", [128, ND, 8], BF16)
    bg = P.sbuf("bg", [1, 8], BF16)
    P.do("p", "dma_start", wg[:], ins["l0_wg"][:].rearrange("(c p) e -> p c e", p=128),
         w=[wg.b], sem="Dw", amt=16)
    P.do("p", "dma_start", bg[:], ins["l0_bg"][:], w=[bg.b], sem="Dw", amt=16)
    gnT = P.sbuf("gnT", [8, 2048], F32)
    gb = P.sbuf("gb", [128, NT, 8], F32)
    rws = [dict(m=P.sbuf(f"g_m{i}", [128, 1], F32), d=P.sbuf(f"g_d{i}", [128, 1], F32),
                r=P.sbuf(f"g_r{i}", [128, 1], F32), t8=P.sbuf(f"g_t8{i}", [128, 8], F32),
                u8=P.sbuf(f"g_u8{i}", [128, 8], F32)) for i in range(2)]
    with ExitStack() as pst:
        PD = P.psum(pst, "pd", [128, 512], F32)
        PTf = P.psum(pst, "ptf", [128, 128], F32)
        for tc in range(NT):
            R = rws[tc % 2]
            m, dsum, rd, t8, u8 = R["m"], R["d"], R["r"], R["t8"], R["u8"]
            g = [(PD[:, 0:8], x_bf[:, ic, tc * 128:(tc + 1) * 128], wg[:, ic, :],
                  (x_bf.b, wg.b)) for ic in range(ND)]
            g.append((PD[:, 0:8], c["ones_bf"][0:1, 0:128], bg[:], (c["ones_bf"].b, bg.b)))
            P.mm_set(PD.b, [g])
            gsl = gb[:, tc, :]
            P.do("s", "tensor_copy", gsl, PD[:, 0:8], r=[PD.b], w=[gb.b])
            P.do("v", "reduce_max", m[:], gsl, axis=AX.X, negate=True, r=[gb.b], w=[m.b])
            P.do("s", "activation", gsl, gsl, AF.Exp, bias=m[:], accum_out=dsum[:],
                 r=[gb.b, m.b], w=[gb.b, dsum.b])
            P.do("v", "reciprocal", rd[:], dsum[:], r=[dsum.b], w=[rd.b])
            P.do("v", "tensor_scalar_mul", gsl, gsl, rd[:], r=[gb.b, rd.b], w=[gb.b])
            P.do("v", "reduce_max", m[:], gsl, axis=AX.X, r=[gb.b], w=[m.b])
            P.do("v", "tensor_scalar", t8[:], gsl, m[:], None, ALU.is_ge,
                 r=[gb.b, m.b], w=[t8.b])
            P.do("v", "scalar_tensor_tensor", u8[:], t8[:], NEG, gsl, ALU.mult, ALU.add,
                 r=[t8.b, gb.b], w=[u8.b])
            P.do("v", "reduce_max", m[:], u8[:], axis=AX.X, r=[u8.b], w=[m.b])
            P.do("v", "tensor_scalar", t8[:], gsl, m[:], None, ALU.is_ge,
                 r=[gb.b, m.b], w=[t8.b])
            P.do("v", "tensor_mul", t8[:], t8[:], gsl, r=[gb.b, t8.b], w=[t8.b])
            P.do("v", "reduce_sum", dsum[:], t8[:], axis=AX.X, r=[t8.b], w=[dsum.b])
            P.do("v", "tensor_scalar_max", dsum[:], dsum[:], 1.0e-12, r=[dsum.b], w=[dsum.b])
            P.do("v", "reciprocal", rd[:], dsum[:], r=[dsum.b], w=[rd.b])
            P.do("v", "tensor_scalar_mul", t8[:], t8[:], rd[:], r=[t8.b, rd.b], w=[t8.b])
            P.do("t", "transpose", PTf[0:8, :], t8[:], c["ident_f"][:],
                 r=[t8.b, c["ident_f"].b], w=[PTf.b])
            P.do("v", "tensor_copy", gnT[0:8, tc * 128:(tc + 1) * 128], PTf[0:8, :],
                 r=[PTf.b], w=[gnT.b])
    for R in rws:
        for sb in R.values():
            P.sfree(sb)
    P.sfree(gb)
    P.sfree(wg)
    P.sfree(bg)

    h_sc = P.sbuf("h_sc", [128, 8, 2048], BF16)
    w1e = [P.sbuf(f"w1e{i}", [128, ND, 256], BF16) for i in range(2)]
    w2q = [P.sbuf(f"w2q{i}", [128, 8, 1024], BF16) for i in range(2)]
    b1row = [P.sbuf(f"b1row{i}", [1, 1024], BF16) for i in range(2)]
    gnrep = P.sbuf("gnrep", [128, 2048], F32)
    tmp = [P.sbuf(f"dtmp{i}", [128, 1024], F32) for i in range(2)]

    with ExitStack() as pst:
        PU = [P.psum(pst, "pu0", [128, 1024], F32), P.psum(pst, "pu1", [128, 1024], F32)]
        PW = [P.psum(pst, "pw0", [128, 1024], F32), P.psum(pst, "pw1", [128, 1024], F32)]
        rot_u = 0
        rot_w = 0
        for e in range(8):
            for half in range(2):
                pu = PU[half]
                groups = [[(pu[:, st * 512:(st + 1) * 512],
                            c["eye8"][0:8, e * 128:(e + 1) * 128],
                            gnT[0:8, half * 1024 + st * 512:half * 1024 + (st + 1) * 512],
                            (c["eye8"].b, gnT.b))] for st in range(2)]
                P.mm_set(pu.b, groups)
                P.do("v", "tensor_copy", gnrep[:, half * 1024:(half + 1) * 1024], pu[:],
                     r=[pu.b], w=[gnrep.b])
            for hq in range(4):
                b1r = b1row[hq % 2]
                P.do("p", "dma_start", b1r[:], ins["l0_b1"][e:e + 1, hq * 1024:(hq + 1) * 1024],
                     w=[b1r.b], sem="Dw", amt=16)
                w2 = w2q[(e * 4 + hq) % 2]
                P.do("p", "dma_start", w2[:],
                     ins["l0_w2"][e, hq * 1024:(hq + 1) * 1024, :].rearrange("(c p) d -> p c d", p=128),
                     w=[w2.b], sem="Dw", amt=16)
                for j in range(4):
                    w1 = w1e[j % 2]
                    P.do("p", "dma_start", w1[:],
                         ins["l0_w1"][e, :, hq * 1024 + j * 256:hq * 1024 + (j + 1) * 256]
                         .rearrange("(c p) h -> p c h", p=128),
                         w=[w1.b], sem="Dw", amt=16)
                    for hc in range(2):
                        hglob = hq * 1024 + j * 256 + hc * 128
                        for th in range(2):
                            pu = PU[rot_u % 2]
                            rot_u += 1
                            groups = []
                            for st in range(2):
                                lo = th * 1024 + st * 512
                                gr = [(pu[:, st * 512:(st + 1) * 512],
                                       w1[:, ic, hc * 128:(hc + 1) * 128],
                                       x_bf[:, ic, lo:lo + 512],
                                       (w1.b, x_bf.b)) for ic in range(ND)]
                                gr.append((pu[:, st * 512:(st + 1) * 512],
                                           b1r[0:1, j * 256 + hc * 128:j * 256 + hc * 128 + 128],
                                           c["ones_bf"][0:1, 0:512],
                                           (b1r.b, c["ones_bf"].b)))
                                groups.append(gr)
                            P.mm_set(pu.b, groups)
                            P.do("s", "activation", h_sc[:, j * 2 + hc, th * 1024:(th + 1) * 1024],
                                 pu[:], AF.Gelu, r=[pu.b], w=[h_sc.b])
                for dc in range(ND):
                    for th in range(2):
                        pw = PW[rot_w % 2]
                        tm = tmp[rot_w % 2]
                        rot_w += 1
                        groups = []
                        for st in range(2):
                            lo = th * 1024 + st * 512
                            gr = [(pw[:, st * 512:(st + 1) * 512],
                                   w2[:, hcq, dc * 128:(dc + 1) * 128],
                                   h_sc[:, hcq, lo:lo + 512],
                                   (w2.b, h_sc.b)) for hcq in range(8)]
                            groups.append(gr)
                        P.mm_set(pw.b, groups)
                        P.do("v", "tensor_mul", tm[:], pw[:],
                             gnrep[:, th * 1024:(th + 1) * 1024],
                             r=[pw.b, gnrep.b], w=[tm.b])
                        P.do("p", "tensor_add", x_fm[:, dc, th * 1024:(th + 1) * 1024],
                             x_fm[:, dc, th * 1024:(th + 1) * 1024], tm[:],
                             r=[tm.b, x_fm.b], w=[x_fm.b])
        P.sfree(gnrep)
        for t_ in w1e + b1row:
            P.sfree(t_)
        b2f = P.sbuf("b2f", [8, 1024], F32)
        P.do("q", "dma_start", b2f[:], ins["l0_b2"][:], w=[b2f.b], sem="Dx", amt=16)
        for dc in range(ND):
            for th in range(2):
                pw = PW[(dc * 2 + th) % 2]
                groups = []
                for st in range(2):
                    lo = th * 1024 + st * 512
                    groups.append([(pw[:, st * 512:(st + 1) * 512],
                                    b2f[0:8, dc * 128:(dc + 1) * 128],
                                    gnT[0:8, lo:lo + 512], (b2f.b, gnT.b))])
                P.mm_set(pw.b, groups)
                P.do("v", "tensor_add", x_fm[:, dc, th * 1024:(th + 1) * 1024],
                     x_fm[:, dc, th * 1024:(th + 1) * 1024], pw[:],
                     r=[pw.b, x_fm.b], w=[x_fm.b])
        P.sfree(b2f)
        P.sfree(h_sc)
        for t_ in w2q:
            P.sfree(t_)
        P.sfree(gnT)
        P.sfree(c.pop("eye8"))
        P.sfree(x_bf)
        for t_ in tmp:
            P.sfree(t_)
        ln_feature_major(P, A, {"PA": PU[0], "PB": PU[1]}, x_fm,
                         with_store=A["out_ap"] if A.get("store_out") else None)
    return x_fm


def emit_l1attn(P, A):
    ins, c = A["in"], A["c"]

    x_bf = P.sbuf("x_bf", [128, ND, 2048], BF16)
    if A.get("stream") is None:
        P.do("p", "dma_start", x_bf[:], ins["xin"][:].rearrange("(c p) t -> p c t", p=128),
             w=[x_bf.b], sem="Dw", amt=16)
    else:
        cast_stream_to_bf(P, A, A["stream"], x_bf)
        park_stream(P, A)

    QT = P.sbuf("QT", [128, ND, 2048], BF16)
    KT = P.sbuf("KT", [128, ND, 2048], BF16)
    wbuf = P.sbuf("wbuf", [128, ND, 1024], BF16)
    brow = P.sbuf("brow", [1, 1024], BF16)
    rho_row = P.sbuf("rho_row", [1, 2048], F32)

    def proj_fm(dst, wname, bname, scale_evac=None):
        P.do("p", "dma_start", wbuf[:], ins[wname][:].rearrange("(c p) o -> p c o", p=128),
             w=[wbuf.b], sem="Dw", amt=16)
        P.do("p", "dma_start", brow[:], ins[bname][:], w=[brow.b], sem="Dw", amt=16)
        for oc in range(ND):
            for half in range(2):
                pb = PA if (oc * 2 + half) % 2 == 0 else PB
                groups = []
                for st in range(2):
                    lo = half * 1024 + st * 512
                    gr = [(pb[:, st * 512:(st + 1) * 512],
                           wbuf[:, ic, oc * 128:(oc + 1) * 128],
                           x_bf[:, ic, lo:lo + 512], (wbuf.b, x_bf.b)) for ic in range(ND)]
                    gr.append((pb[:, st * 512:(st + 1) * 512],
                               brow[0:1, oc * 128:(oc + 1) * 128],
                               c["ones_bf"][0:1, 0:512], (brow.b, c["ones_bf"].b)))
                    groups.append(gr)
                P.mm_set(pb.b, groups)
                eng = "s" if (oc + half) % 2 == 0 else "v"
                P.do(eng, "tensor_copy", dst[:, oc, half * 1024:(half + 1) * 1024],
                     pb[:], r=[pb.b], w=[dst.b])

    with ExitStack() as pst:
        PA = P.psum(pst, "pa", [128, 1024], F32)
        PB = P.psum(pst, "pb", [128, 1024], F32)
        PTf = P.psum(pst, "ptf", [128, 128], F32)
        proj_fm(QT, "l1_wq", "l1_bq")
        proj_fm(KT, "l1_wk", "l1_bk")

        Ef = P.sbuf("Ef", [128, 2048], F32)
        Edump = P.sbuf("Edump", [128, 2048], BF16)
        rows = [dict(m=P.sbuf(f"r_m{i}", [128, 1], F32), se=P.sbuf(f"r_se{i}", [128, 1], F32),
                     es=P.sbuf(f"r_es{i}", [128, 1], F32), den=P.sbuf(f"r_dn{i}", [128, 1], F32),
                     rd=P.sbuf(f"r_rd{i}", [128, 1], F32), rho=P.sbuf(f"r_rh{i}", [128, 1], F32))
                for i in range(2)]
        for qt in range(NT):
            R = rows[qt % 2]
            m, se, es, den, rd, rho = R["m"], R["se"], R["es"], R["den"], R["rd"], R["rho"]
            Pk = (qt + 1) * 128
            nstr = -(-Pk // 512)
            for st in range(nstr):
                w_ = min(512, Pk - st * 512)
                pb = PA if st % 2 == 0 else PB
                groups = [[(pb[:, 0:w_], QT[:, dcc, qt * 128:(qt + 1) * 128],
                            KT[:, dcc, st * 512:st * 512 + w_], (QT.b, KT.b))
                           for dcc in range(ND)]]
                P.mm_set(pb.b, groups)
                eng = "s" if st % 2 == 0 else "v"
                if eng == "s":
                    P.do("s", "activation", Ef[:, st * 512:st * 512 + w_], pb[:, 0:w_],
                         AF.Identity, scale=1.0 / 32.0, r=[pb.b], w=[Ef.b])
                else:
                    P.do("v", "tensor_scalar_mul", Ef[:, st * 512:st * 512 + w_], pb[:, 0:w_],
                         1.0 / 32.0, r=[pb.b], w=[Ef.b])
            P.do("p", "tensor_mul", Ef[:, qt * 128:(qt + 1) * 128],
                 Ef[:, qt * 128:(qt + 1) * 128], c["am_mul"][:],
                 r=[Ef.b, c["am_mul"].b], w=[Ef.b])
            P.do("v", "reduce_max", m[:], Ef[:, 0:Pk], axis=AX.X, negate=True,
                 r=[Ef.b], w=[m.b])
            if qt < NT - 1:
                P.do("v", "tensor_scalar_min", m[:], m[:], 0.0, r=[m.b], w=[m.b])
            P.do("s", "activation", Edump[:, 0:Pk], Ef[:, 0:Pk], AF.Exp,
                 bias=m[:], accum_out=se[:], r=[Ef.b, m.b], w=[Edump.b, se.b])
            P.do("s", "activation", es[:], m[:], AF.Exp, r=[m.b], w=[es.b])
            cnt = float(T - Pk)
            P.do("v", "scalar_tensor_tensor", den[:], es[:], cnt, se[:], ALU.mult, ALU.add,
                 r=[es.b, se.b], w=[den.b])
            P.do("v", "reciprocal", rd[:], den[:], r=[den.b], w=[rd.b])
            P.do("v", "tensor_mul", rho[:], den[:], rd[:], r=[den.b, rd.b], w=[rho.b])
            P.do("t", "transpose", PTf[0:1, :], rho[:], c["ident_f"][:],
                 r=[rho.b, c["ident_f"].b], w=[PTf.b])
            P.do("v", "tensor_copy", rho_row[0:1, qt * 128:(qt + 1) * 128], PTf[0:1, :],
                 r=[PTf.b], w=[rho_row.b])
        if A.get("dbg"):
            P.do("q", "dma_start", A["dbg"]["d_rho"], rho_row[:], r=[rho_row.b], sem="Ds", amt=16)
            P.do("q", "dma_start", A["dbg"]["d_ef"], Ef[:], r=[Ef.b], sem="Ds", amt=16)
            P.do("v", "tensor_copy", Ef[:], QT[:, 0, :], r=[QT.b], w=[Ef.b])
            P.do("q", "dma_start", A["dbg"]["d_qt"], Ef[:], r=[Ef.b], sem="Ds", amt=16)
            P.do("v", "tensor_copy", Ef[:], KT[:, 0, :], r=[KT.b], w=[Ef.b])
            P.do("q", "dma_start", A["dbg"]["d_kt"], Ef[:], r=[Ef.b], sem="Ds", amt=16)
        P.sfree(Ef)
        P.sfree(Edump)
        for R in rows:
            for sb in R.values():
                P.sfree(sb)
        P.sfree(QT)
        P.sfree(KT)

        rrep = P.sbuf("rrep", [128, 2048], F32)
        for half in range(2):
            pb = PA if half == 0 else PB
            groups = [[(pb[:, st * 512:(st + 1) * 512], c["ones_f"][0:1, 0:128],
                        rho_row[0:1, half * 1024 + st * 512:half * 1024 + (st + 1) * 512],
                        (c["ones_f"].b, rho_row.b))] for st in range(2)]
            P.mm_set(pb.b, groups)
            P.do("v", "tensor_copy", rrep[:, half * 1024:(half + 1) * 1024], pb[:],
                 r=[pb.b], w=[rrep.b])

        y = P.sbuf("y_fm", [128, ND, 2048], F32)
        xres = P.sbuf("xres", [128, 2, 2048], F32)
        tmp = [P.sbuf(f"vtmp{i}", [128, 1024], F32) for i in range(2)]
        P.do("p", "dma_start", wbuf[:], ins["l1_wv"][:].rearrange("(c p) o -> p c o", p=128),
             w=[wbuf.b], sem="Dw", amt=16)
        P.do("p", "dma_start", brow[:], ins["l1_bv"][:], w=[brow.b], sem="Dw", amt=16)
        for oc in range(ND):
            if A.get("parked"):
                src = A["park"][:].rearrange("(c p) t -> p c t", p=128)[:, oc, :]
                P.do("q", "dma_start", xres[:, oc % 2, :], src, r=[A["park_buf"]],
                     w=[xres.b], sem="Dx", amt=16)
            else:
                src = ins["xin"][:].rearrange("(c p) t -> p c t", p=128)[:, oc, :]
                P.do("q", "dma_start", xres[:, oc % 2, :], src, w=[xres.b], sem="Dx", amt=16)
            for half in range(2):
                pb = PA if (oc * 2 + half) % 2 == 0 else PB
                tm = tmp[(oc * 2 + half) % 2]
                groups = []
                for st in range(2):
                    lo = half * 1024 + st * 512
                    gr = [(pb[:, st * 512:(st + 1) * 512],
                           wbuf[:, ic, oc * 128:(oc + 1) * 128],
                           x_bf[:, ic, lo:lo + 512], (wbuf.b, x_bf.b)) for ic in range(ND)]
                    gr.append((pb[:, st * 512:(st + 1) * 512],
                               brow[0:1, oc * 128:(oc + 1) * 128],
                               c["ones_bf"][0:1, 0:512], (brow.b, c["ones_bf"].b)))
                    groups.append(gr)
                P.mm_set(pb.b, groups)
                P.do("v", "tensor_mul", tm[:], pb[:], rrep[:, half * 1024:(half + 1) * 1024],
                     r=[pb.b, rrep.b], w=[tm.b])
                P.do("p", "tensor_add", y[:, oc, half * 1024:(half + 1) * 1024],
                     xres[:, oc % 2, half * 1024:(half + 1) * 1024], tm[:],
                     r=[xres.b, tm.b], w=[y.b])
        if A.get("dbg2"):
            P.do("q", "dma_start", A["dbg2"]["d_yall"].rearrange("(c p) t -> p c t", p=128), y[:],
                 r=[y.b], sem="Ds", amt=16)
        if A.get("dbg"):
            P.do("q", "dma_start", A["dbg"]["d_y"], y[:, 0, :], r=[y.b], sem="Ds", amt=16)
            P.do("q", "dma_start", A["dbg"]["d_rrep"], rrep[:, 0:2048], r=[rrep.b], sem="Ds", amt=16)
            P.do("q", "dma_start", A["dbg"]["d_yall"].rearrange("(c p) t -> p c t", p=128), y[:],
                 r=[y.b], sem="Ds", amt=16)
        P.sfree(x_bf)
        P.sfree(wbuf)
        P.sfree(brow)
        P.sfree(rho_row)
        P.sfree(rrep)
        P.sfree(xres)
        for t_ in tmp:
            P.sfree(t_)
        ln_feature_major(P, A, {"PA": PA, "PB": PB}, y,
                         with_store=A["out_ap"] if A.get("store_out") else None)
    return y


def emit_l1ffn(P, A):
    ins, c = A["in"], A["c"]
    NE = 16

    if A.get("stream") is None:
        x_fm = P.sbuf("x_fm", [128, ND, 2048], F32)
        P.do("q", "dma_start", x_fm[:], ins["xin"][:].rearrange("(c p) t -> p c t", p=128),
             w=[x_fm.b], sem="Dx", amt=16)
    else:
        x_fm = A["stream"]
    x_bf = P.sbuf("x_bf", [128, ND, 2048], BF16)
    cast_stream_to_bf(P, A, x_fm, x_bf)

    c["eye16"] = P.sbuf("eye16", [16, 2048], F32)
    P.do("q", "dma_start", c["eye16"][:], ins["c_eye16"][:], w=[c["eye16"].b], sem="Dx", amt=16)
    rw = P.sbuf("rw", [128, ND, 16], BF16)
    P.do("p", "dma_start", rw[:], ins["l1_rwT"][:].rearrange("(c p) e -> p c e", p=128),
         w=[rw.b], sem="Dw", amt=16)
    rbias = P.sbuf("rbias", [1, 16], F32)
    P.do("q", "dma_start", rbias[:], ins["l1_rbias"][:], w=[rbias.b], sem="Dx", amt=16)
    cT = P.sbuf("cT", [16, 2048], F32)
    sc = P.sbuf("sc", [128, NT, 16], F32)
    rbrep = P.sbuf("rbrep", [128, 16], F32)

    with ExitStack() as pst:
        PD = P.psum(pst, "pd", [128, 512], F32)
        PTf = P.psum(pst, "ptf", [128, 128], F32)
        P.mm_set(PD.b, [[(PD[:, 0:16], c["ones_f"][0:1, 0:128], rbias[0:1, 0:16],
                          (c["ones_f"].b, rbias.b))]])
        P.do("v", "tensor_copy", rbrep[:], PD[:, 0:16], r=[PD.b], w=[rbrep.b])
        negt = P.sbuf("negt", [128, 16], F32)
        P.do("v", "memset", negt[:], -1.0e9, w=[negt.b])
        ws = [dict(sch=P.sbuf(f"w_sch{i}", [128, 16], F32), gs=P.sbuf(f"w_gs{i}", [128, 4], F32),
                   t4=P.sbuf(f"w_t4{i}", [128, 4], F32), m2=P.sbuf(f"w_m2{i}", [128, 1], F32),
                   gm=P.sbuf(f"w_gm{i}", [128, 4], I32), mk=P.sbuf(f"w_mk{i}", [128, 16], F32),
                   t16=P.sbuf(f"w_t16{i}", [128, 16], F32), m1=P.sbuf(f"w_m1{i}", [128, 1], F32),
                   wv=P.sbuf(f"w_wv{i}", [128, 16], F32)) for i in range(2)]
        for tc in range(NT):
            W = ws[tc % 2]
            sch, gs, t4, m2, gm, mk, t16, m1, wv = (W["sch"], W["gs"], W["t4"], W["m2"],
                                                    W["gm"], W["mk"], W["t16"], W["m1"], W["wv"])
            g = [(PD[:, 0:16], x_bf[:, ic, tc * 128:(tc + 1) * 128], rw[:, ic, :],
                  (x_bf.b, rw.b)) for ic in range(ND)]
            P.mm_set(PD.b, [g])
            P.do("s", "activation", sc[:, tc, :], PD[:, 0:16], AF.Sigmoid, r=[PD.b], w=[sc.b])
            P.do("v", "tensor_add", sch[:], sc[:, tc, :], rbrep[:], r=[sc.b, rbrep.b], w=[sch.b])
            schg = sch[:].rearrange("p (g j) -> p g j", j=4)
            first = True
            for a in range(4):
                for bj in range(a + 1, 4):
                    P.do("v", "tensor_add", t4[:], schg[:, :, a], schg[:, :, bj],
                         r=[sch.b], w=[t4.b])
                    if first:
                        P.do("v", "tensor_copy", gs[:], t4[:], r=[t4.b], w=[gs.b])
                        first = False
                    else:
                        P.do("v", "tensor_max", gs[:], gs[:], t4[:], r=[t4.b, gs.b], w=[gs.b])
            first = True
            for a in range(4):
                for bj in range(a + 1, 4):
                    P.do("v", "tensor_tensor", t4[:, 0:1], gs[:, a:a + 1], gs[:, bj:bj + 1],
                         ALU.min, r=[gs.b], w=[t4.b])
                    if first:
                        P.do("v", "tensor_copy", m2[:], t4[:, 0:1], r=[t4.b], w=[m2.b])
                        first = False
                    else:
                        P.do("v", "tensor_max", m2[:], m2[:], t4[:, 0:1], r=[t4.b, m2.b], w=[m2.b])
            P.do("v", "tensor_scalar", gm[:], gs[:], m2[:], None, ALU.is_ge,
                 r=[gs.b, m2.b], w=[gm.b])
            mkg = mk[:].rearrange("p (g j) -> p g j", j=4)
            for j in range(4):
                P.do("v", "select", mkg[:, :, j], gm[:], schg[:, :, j], negt[:, 0:4],
                     r=[gm.b, sch.b, negt.b], w=[mk.b])
            P.do("v", "tensor_copy", t16[:], mk[:], r=[mk.b], w=[t16.b])
            for it in range(3):
                P.do("v", "reduce_max", m1[:], t16[:], axis=AX.X, r=[t16.b], w=[m1.b])
                P.do("v", "tensor_scalar", wv[:], t16[:], m1[:], None, ALU.is_ge,
                     r=[t16.b, m1.b], w=[wv.b])
                P.do("v", "scalar_tensor_tensor", t16[:], wv[:], NEG, t16[:], ALU.mult, ALU.add,
                     r=[wv.b, t16.b], w=[t16.b])
            P.do("v", "reduce_max", m1[:], t16[:], axis=AX.X, r=[t16.b], w=[m1.b])
            P.do("v", "tensor_scalar", wv[:], mk[:], m1[:], None, ALU.is_ge,
                 r=[mk.b, m1.b], w=[wv.b])
            P.do("v", "tensor_mul", wv[:], wv[:], sc[:, tc, :], r=[sc.b, wv.b], w=[wv.b])
            P.do("v", "reduce_sum", m1[:], wv[:], axis=AX.X, r=[wv.b], w=[m1.b])
            P.do("v", "tensor_scalar_add", m1[:], m1[:], 1.0e-20, r=[m1.b], w=[m1.b])
            P.do("v", "reciprocal", m2[:], m1[:], r=[m1.b], w=[m2.b])
            P.do("v", "tensor_scalar", wv[:], wv[:], m2[:], 2.5, ALU.mult, ALU.mult,
                 r=[wv.b, m2.b], w=[wv.b])
            P.do("t", "transpose", PTf[0:16, :], wv[:], c["ident_f"][:],
                 r=[wv.b, c["ident_f"].b], w=[PTf.b])
            P.do("v", "tensor_copy", cT[0:16, tc * 128:(tc + 1) * 128], PTf[0:16, :],
                 r=[PTf.b], w=[cT.b])
        for W in ws:
            for sb in W.values():
                P.sfree(sb)
        P.sfree(negt)
        P.sfree(sc)
        P.sfree(rw)
        P.sfree(rbias)
        P.sfree(rbrep)

    wgT = [P.sbuf(f"wgT{i}", [128, ND, 512], BF16) for i in range(2)]
    wuT = [P.sbuf(f"wuT{i}", [128, ND, 512], BF16) for i in range(2)]
    wdT = [P.sbuf(f"wdT{i}", [128, 4, 1024], BF16) for i in range(2)]
    g_bf = P.sbuf("g_bf", [128, 4, 2048], BF16)
    crep = P.sbuf("crep", [128, 2048], F32)
    tmp = [P.sbuf(f"dtmp{i}", [128, 1024], F32) for i in range(2)]

    with ExitStack() as pst:
        PU = [P.psum(pst, "pu0", [128, 1024], F32), P.psum(pst, "pu1", [128, 1024], F32)]
        PW = [P.psum(pst, "pw0", [128, 1024], F32), P.psum(pst, "pw1", [128, 1024], F32)]
        rot = 0
        for e in range(NE):
            wg_ = wgT[e % 2]
            wu_ = wuT[e % 2]
            wd_ = wdT[e % 2]
            P.do("p", "dma_start", wg_[:],
                 ins["l1_wgT"][e].rearrange("(c p) i -> p c i", p=128), w=[wg_.b], sem="Dw", amt=16)
            P.do("p", "dma_start", wu_[:],
                 ins["l1_wuT"][e].rearrange("(c p) i -> p c i", p=128), w=[wu_.b], sem="Dw", amt=16)
            P.do("p", "dma_start", wd_[:],
                 ins["l1_wdT"][e].rearrange("(c p) h -> p c h", p=128), w=[wd_.b], sem="Dw", amt=16)
            for half in range(2):
                pu = PU[half]
                groups = [[(pu[:, st * 512:(st + 1) * 512],
                            c["eye16"][0:16, e * 128:(e + 1) * 128],
                            cT[0:16, half * 1024 + st * 512:half * 1024 + (st + 1) * 512],
                            (c["eye16"].b, cT.b))] for st in range(2)]
                P.mm_set(pu.b, groups)
                P.do("v", "tensor_copy", crep[:, half * 1024:(half + 1) * 1024], pu[:],
                     r=[pu.b], w=[crep.b])
            for icc in range(4):
                for th in range(2):
                    sl = (icc, slice(th * 1024, (th + 1) * 1024))
                    for (wt, silu) in ((wg_, True), (wu_, False)):
                        pu = PU[rot % 2]
                        rot += 1
                        groups = []
                        for st in range(2):
                            lo = th * 1024 + st * 512
                            gr = [(pu[:, st * 512:(st + 1) * 512],
                                   wt[:, hcc, icc * 128:(icc + 1) * 128],
                                   x_bf[:, hcc, lo:lo + 512], (wt.b, x_bf.b))
                                  for hcc in range(ND)]
                            groups.append(gr)
                        P.mm_set(pu.b, groups)
                        if silu:
                            P.do("s", "activation", g_bf[:, sl[0], sl[1]],
                                 pu[:], AF.Silu, r=[pu.b], w=[g_bf.b])
                        else:
                            P.do("v", "tensor_mul", g_bf[:, sl[0], sl[1]],
                                 g_bf[:, sl[0], sl[1]], pu[:],
                                 r=[pu.b, g_bf.b], w=[g_bf.b])
            for hcc in range(ND):
                for th in range(2):
                    pw = PW[rot % 2]
                    tm = tmp[rot % 2]
                    rot += 1
                    groups = []
                    for st in range(2):
                        lo = th * 1024 + st * 512
                        gr = [(pw[:, st * 512:(st + 1) * 512],
                               wd_[:, icc, hcc * 128:(hcc + 1) * 128],
                               g_bf[:, icc, lo:lo + 512], (wd_.b, g_bf.b))
                              for icc in range(4)]
                        groups.append(gr)
                    P.mm_set(pw.b, groups)
                    P.do("v", "tensor_mul", tm[:], pw[:], crep[:, th * 1024:(th + 1) * 1024],
                         r=[pw.b, crep.b], w=[tm.b])
                    P.do("p", "tensor_add", x_fm[:, hcc, th * 1024:(th + 1) * 1024],
                         x_fm[:, hcc, th * 1024:(th + 1) * 1024], tm[:],
                         r=[tm.b, x_fm.b], w=[x_fm.b])
        for t_ in wgT + wuT + wdT:
            P.sfree(t_)
        P.sfree(g_bf)
        P.sfree(crep)
        P.sfree(c.pop("eye16"))
        P.sfree(cT)
        P.sfree(x_bf)
        for t_ in tmp:
            P.sfree(t_)
        ln_feature_major(P, A, {"PA": PU[0], "PB": PU[1]}, x_fm,
                         with_store=A["out_ap"] if A.get("store_out") else None)
    return x_fm


def emit_l0attn_with_wo(P, A):
    wo_sb = P.sbuf("wo_sb", [128, ND, 1024], BF16)
    A["_l0_wo_sb"] = wo_sb
    P.do("p", "dma_start", wo_sb[:],
         A["in"]["l0_woT"][:].rearrange("(c p) o -> p c o", p=128),
         w=[wo_sb.b], sem="Dw", amt=16)
    y = emit_l0attn(P, A)
    P.sfree(wo_sb)
    return y


_PHASE_INPUTS = {
    "l0attn": [("l0_wqT", [D, D]), ("l0_wkT", [D, D]), ("l0_wvpT", [D, D]),
               ("l0_woT", [D, D]), ("l0_gamma", [1, D]), ("l0_beta", [1, D])],
    "l0ffn": [("l0_wg", [D, 8]), ("l0_bg", [1, 8]), ("l0_w1", [8, D, 4096]),
              ("l0_b1", [8, 4096]), ("l0_w2", [8, 4096, D]), ("l0_b2", [8, D]),
              ("c_eye8", [8, 1024])],
    "l1attn": [("l1_wq", [D, D]), ("l1_bq", [1, D]), ("l1_wk", [D, D]), ("l1_bk", [1, D]),
               ("l1_wv", [D, D]), ("l1_bv", [1, D])],
    "l1ffn": [("l1_rwT", [D, 16]), ("l1_rbias", [1, 16]), ("l1_wgT", [16, D, 512]),
              ("l1_wuT", [16, D, 512]), ("l1_wdT", [16, 512, D]), ("c_eye16", [16, 2048])],
}

_PHASE_EMIT = {
    "l0attn": emit_l0attn_with_wo,
    "l0ffn": emit_l0ffn,
    "l1attn": emit_l1attn,
    "l1ffn": emit_l1ffn,
}

_PROG_CACHE = {}


def build_program(phases):
    key = tuple(phases)
    if key in _PROG_CACHE:
        return _PROG_CACHE[key]
    P = Prog()
    nc = P.nc
    ins = {}
    ins["xin"] = nc.dram_tensor("xin", [D, T], F32, kind="ExternalInput").ap()
    ins["c_ident"] = nc.dram_tensor("c_ident", [128, 128], F32, kind="ExternalInput").ap()
    ins["c_ones"] = nc.dram_tensor("c_ones", [1, 2048], F32, kind="ExternalInput").ap()
    for ph in phases:
        for nm, shp in _PHASE_INPUTS[ph]:
            if nm not in ins:
                ins[nm] = nc.dram_tensor(nm, list(shp), F32, kind="ExternalInput").ap()
    yout = nc.dram_tensor("yout", [D, T], F32, kind="ExternalOutput").ap()
    park = None
    if len(phases) > 1:
        park = nc.dram_tensor("park", [D, T], F32, kind="Internal").ap()

    A = {"in": ins, "out_ap": yout, "stream": None, "park": park, "parked": False}
    import os
    if os.environ.get("KDBG4") and phases == ["l0attn"]:
        A["dbg4"] = {
            "d_ef": nc.dram_tensor("d_ef", [128, 2048], F32, kind="ExternalOutput").ap(),
            "d_eb": nc.dram_tensor("d_eb", [128, 2048], F32, kind="ExternalOutput").ap(),
            "d_et": nc.dram_tensor("d_et", [128, 1024], F32, kind="ExternalOutput").ap(),
            "d_qkv": nc.dram_tensor("d_qkv", [128, 1024], F32, kind="ExternalOutput").ap(),
            "d_den": nc.dram_tensor("d_den", [128, 1], F32, kind="ExternalOutput").ap(),
            "d_rd": nc.dram_tensor("d_rd", [128, 1], F32, kind="ExternalOutput").ap(),
        }
    if os.environ.get("KDBG2") and phases == ["l1attn"]:
        A["dbg2"] = {"d_yall": nc.dram_tensor("d_yall", [D, T], F32, kind="ExternalOutput").ap()}
    if os.environ.get("KDBG") and phases == ["l1attn"]:
        A["dbg"] = {
            "d_rho": nc.dram_tensor("d_rho", [1, 2048], F32, kind="ExternalOutput").ap(),
            "d_ef": nc.dram_tensor("d_ef", [128, 2048], F32, kind="ExternalOutput").ap(),
            "d_qt": nc.dram_tensor("d_qt", [128, 2048], F32, kind="ExternalOutput").ap(),
            "d_kt": nc.dram_tensor("d_kt", [128, 2048], F32, kind="ExternalOutput").ap(),
            "d_y": nc.dram_tensor("d_y", [128, 2048], F32, kind="ExternalOutput").ap(),
            "d_rrep": nc.dram_tensor("d_rrep", [128, 2048], F32, kind="ExternalOutput").ap(),
            "d_yall": nc.dram_tensor("d_yall", [D, T], F32, kind="ExternalOutput").ap(),
        }
    load_consts(P, A)
    for i, ph in enumerate(phases):
        A["store_out"] = (i == len(phases) - 1)
        y = _PHASE_EMIT[ph](P, A)
        A["stream"] = y
        A["parked"] = False
        if i < len(phases) - 1:
            P.barrier()
    P.wait_tok("q", ("Ds", P.cnt["Ds"]))
    P.build()
    _PROG_CACHE[key] = P
    return P


def _prep_weights(i):
    perm = np.arange(D).reshape(HD, NH).T.reshape(-1)
    f = np.float32
    return {
        "l0_wqT": np.ascontiguousarray(i["L0_attn_Wq"].T, f),
        "l0_wkT": np.ascontiguousarray(i["L0_attn_Wk"].T, f),
        "l0_wvpT": np.ascontiguousarray(i["L0_attn_Wv"][perm].T, f),
        "l0_woT": np.ascontiguousarray(i["L0_attn_Wo"].T, f),
        "l0_gamma": np.ascontiguousarray(i["L0_attn_gamma"].reshape(1, D), f),
        "l0_beta": np.ascontiguousarray(i["L0_attn_beta"].reshape(1, D), f),
        "l0_wg": np.ascontiguousarray(i["L0_ffn_Wg"], f),
        "l0_bg": np.ascontiguousarray(i["L0_ffn_bg"].reshape(1, 8), f),
        "l0_w1": np.ascontiguousarray(i["L0_ffn_W1"], f),
        "l0_b1": np.ascontiguousarray(i["L0_ffn_b1"], f),
        "l0_w2": np.ascontiguousarray(i["L0_ffn_W2"], f),
        "l0_b2": np.ascontiguousarray(i["L0_ffn_b2"], f),
        "l1_wq": np.ascontiguousarray(i["L1_attn_Wq"], f),
        "l1_bq": np.ascontiguousarray(i["L1_attn_bq"].reshape(1, D), f),
        "l1_wk": np.ascontiguousarray(i["L1_attn_Wk"], f),
        "l1_bk": np.ascontiguousarray(i["L1_attn_bk"].reshape(1, D), f),
        "l1_wv": np.ascontiguousarray(i["L1_attn_Wv"], f),
        "l1_bv": np.ascontiguousarray(i["L1_attn_bv"].reshape(1, D), f),
        "l1_rwT": np.ascontiguousarray(i["L1_ffn_router_weight"].T, f),
        "l1_rbias": np.ascontiguousarray(i["L1_ffn_router_bias"].reshape(1, 16), f),
        "l1_wgT": np.ascontiguousarray(i["L1_ffn_w_gate"].transpose(0, 2, 1), f),
        "l1_wuT": np.ascontiguousarray(i["L1_ffn_w_up"].transpose(0, 2, 1), f),
        "l1_wdT": np.ascontiguousarray(i["L1_ffn_w_down"].transpose(0, 2, 1), f),
    }


def _consts():
    eye8 = np.zeros((8, 1024), np.float32)
    for k in range(8):
        eye8[k, k * 128:(k + 1) * 128] = 1.0
    eye16 = np.zeros((16, 2048), np.float32)
    for k in range(16):
        eye16[k, k * 128:(k + 1) * 128] = 1.0
    return {
        "c_ident": np.eye(128, dtype=np.float32),
        "c_ones": np.ones((1, 2048), dtype=np.float32),
        "c_eye8": eye8,
        "c_eye16": eye16,
    }


_NEEDED = {
    "l0attn": ["xin", "c_ident", "c_ones", "l0_wqT", "l0_wkT", "l0_wvpT", "l0_woT",
               "l0_gamma", "l0_beta"],
    "l0ffn": ["xin", "c_ident", "c_ones", "l0_wg", "l0_bg", "l0_w1", "l0_b1",
              "l0_w2", "l0_b2", "c_eye8"],
    "l1attn": ["xin", "c_ident", "c_ones", "l1_wq", "l1_bq", "l1_wk", "l1_bk",
               "l1_wv", "l1_bv"],
    "l1ffn": ["xin", "c_ident", "c_ones", "l1_rwT", "l1_rbias", "l1_wgT",
              "l1_wuT", "l1_wdT", "c_eye16"],
}

PHASE_ORDER = ["l0attn", "l0ffn", "l1attn", "l1ffn"]
FUSED = True


def run_phase(ph, streams, shared):
    P = build_program([ph])
    names = set(_NEEDED[ph])
    in_maps = []
    for b in range(NCORES):
        m = {k: v for k, v in shared.items() if k in names}
        m["xin"] = streams[b]
        in_maps.append(m)
    res = run_bass_kernel_spmd(P.nc, in_maps, core_ids=list(range(NCORES)))
    return [r["yout"] for r in res.results]


def kernel(**inputs):
    shared = _prep_weights(inputs)
    shared.update(_consts())
    x = inputs["L0_attn_x"]
    streams = [np.ascontiguousarray(x[b].T) for b in range(NCORES)]
    if FUSED:
        P = build_program(PHASE_ORDER)
        names = set(n for ph in PHASE_ORDER for n in _NEEDED[ph])
        in_maps = []
        for b in range(NCORES):
            m = {k: v for k, v in shared.items() if k in names}
            m["xin"] = streams[b]
            in_maps.append(m)
        res = run_bass_kernel_spmd(P.nc, in_maps, core_ids=list(range(NCORES)))
        streams = [r["yout"] for r in res.results]
    else:
        for ph in PHASE_ORDER:
            streams = run_phase(ph, streams, shared)
    return np.stack([s.T for s in streams]).astype(np.float32)
